# version 06
# speedup vs baseline: 1.2371x; 1.0060x over previous
import contextlib
import numpy as np
import ml_dtypes

import concourse.bass as bass
import concourse.mybir as mybir
from concourse.bass_utils import run_bass_kernel_spmd

F32 = mybir.dt.float32
BF16 = mybir.dt.bfloat16
I32 = mybir.dt.int32
U8 = mybir.dt.uint8
AF = mybir.ActivationFunctionType
ALU = mybir.AluOpType
BF = ml_dtypes.bfloat16

B, T, D = 4, 2048, 4096
TOK = 1024
NCORE = 8

PERM_D = np.concatenate([np.arange(0, D, 2), np.arange(1, D, 2)])
INV_PERM_D = np.argsort(PERM_D)
N_GU = 22016
PERM_GU = np.concatenate([np.arange(0, N_GU, 2), np.arange(1, N_GU, 2)])
N_IN = 11008
PERM_H = np.concatenate([np.arange(0, N_IN, 2), np.arange(1, N_IN, 2)])

ENGINES = ("sync", "vector", "scalar", "tensor", "gpsimd")


class KB:
    NSEM = 90
    CAP = 50000

    def __init__(self, nc, es):
        self.nc = nc
        self.es = es
        self.steps = []
        self.h = [es.enter_context(nc.semaphore(f"sm{i}")) for i in range(self.NSEM)]
        self.cnt = [0] * self.NSEM
        self.free = list(range(self.NSEM))
        self.slot_sems = []
        self.eng_sem = {}
        self.hwm = {}
        self.last = {e: None for e in ENGINES}
        self.cur_scope = None

    def _cur(self, eng, amt):
        key = (eng, "dma" if amt >= 16 else "cmp")
        si = self.eng_sem.get(key)
        if si is None or self.cnt[si] + amt > self.CAP:
            si = self.free.pop(0)
            self.eng_sem[key] = si
        return si

    def wait(self, eng, evs):
        if evs is None:
            return
        if isinstance(evs, tuple) and len(evs) == 2 and isinstance(evs[0], int):
            evs = [evs]
        for ev in evs:
            if ev is None:
                continue
            si, val = ev
            if si == "SLOT":
                si = val
                val = self.cnt[si]
            if val == -1:
                val = self.cnt[si]
            if self.hwm.get((eng, si), 0) < val:
                hnd = self.h[si]
                self.steps.append((eng, lambda e, hnd=hnd, val=val: e.wait_ge(hnd, val), self.cur_scope))
                self.hwm[(eng, si)] = val

    def dslot(self):
        for idx, si in enumerate(self.free):
            if self.cnt[si] < 30000:
                self.free.pop(idx)
                self.slot_sems.append(si)
                return ("SLOT", si)
        raise RuntimeError("out of slot semaphores")

    def reclaim_slots(self):
        self.free = self.free + self.slot_sems
        self.slot_sems = []

    def op(self, eng, fn, waits=(), inc=1, slot=None):
        self.wait(eng, waits)
        if slot is not None:
            si = slot[1]
        else:
            si = self._cur(eng, inc)
        self.cnt[si] += inc
        val = self.cnt[si]
        hnd = self.h[si]
        self.steps.append((eng, lambda e, fn=fn, hnd=hnd, inc=inc: fn(e).then_inc(hnd, inc), self.cur_scope))
        if slot is None and inc >= 16:
            ev = (si, -1)
        else:
            ev = (si, val)
        self.last[eng] = (si, val)
        return ev

    def raw(self, eng, fn, waits=()):
        self.wait(eng, waits)
        self.steps.append((eng, fn, self.cur_scope))

    def set_scope(self, name):
        self.cur_scope = name

    def init_echo(self, nc, es):
        self._echo_buf = _sb(nc, es, "kb_echo", [1, 8], F32)

    def echo(self, eng):
        b = self._echo_buf
        if eng == "vector":
            return self.op("vector", lambda e: e.tensor_copy(out=b[:, 4:8], in_=b[:, 0:4]))
        if eng == "scalar":
            return self.op("scalar", lambda e: e.activation(out=b[:, 4:8], in_=b[:, 0:4], func=AF.Copy))
        if eng == "gpsimd":
            return self.op("gpsimd", lambda e: e.tensor_copy(out=b[:, 4:8], in_=b[:, 0:4]))
        return self.last[eng]

    def fence_all(self):
        for e in ("vector", "scalar", "gpsimd"):
            if self.last[e] is not None:
                self.echo(e)
        evs = [(si, self.cnt[si]) for si in range(self.NSEM) if self.cnt[si] > 0]
        for e in ENGINES:
            self.wait(e, evs)

    def build(self, block):
        i = 0
        n = len(self.steps)
        while i < n:
            eng = self.steps[i][0]
            j = i
            fns = []
            while j < n and self.steps[j][0] == eng:
                fns.append(self.steps[j][1:])
                j += 1
            def section(e, fns=fns):
                for fn, scope in fns:
                    if scope is not None:
                        with self.nc.named_scope(scope):
                            fn(e)
                    else:
                        fn(e)
            getattr(block, eng)(section)
            i = j


def _sb(nc, es, name, shape, dtype):
    return es.enter_context(nc.sbuf_tensor(name, shape, dtype))


def _ps(nc, es, name, shape, dtype=F32):
    return es.enter_context(nc.psum_tensor(name, shape, dtype))


class WStream:

    def __init__(self, k, nc, es, name, rows, cols, dq="sync", cast_eng="scalar"):
        self.k = k
        self.wf = [_sb(nc, es, f"{name}_f{i}", [rows, cols], F32) for i in range(2)]
        self.wb = [_sb(nc, es, f"{name}_b{i}", [rows, cols], BF16) for i in range(2)]
        self.rows, self.cols = rows, cols
        self.dq, self.cast_eng = dq, cast_eng
        self.sel = 0
        self.cast_ev = [None, None]
        self.use_ev = [None, None]
        self.slots = [k.dslot(), k.dslot()]

    def get(self, dram_ap, rows=None, cols=None):
        r = rows or self.rows
        c = cols or self.cols
        s = self.sel
        k = self.k
        ev_d = k.op(self.dq, lambda e, s=s, r=r, c=c, a=dram_ap: e.dma_start(
            out=self.wf[s][:r, :c], in_=a), inc=16,
            waits=(self.cast_ev[s],), slot=self.slots[s])
        ev_c = k.op(self.cast_eng, lambda e, s=s, r=r, c=c: e.activation(
            out=self.wb[s][:r, :c], in_=self.wf[s][:r, :c], func=AF.Copy),
            waits=(ev_d, self.use_ev[s]))
        self.cast_ev[s] = ev_c
        self.cur = s
        self.sel = 1 - s
        return self.wb[s], ev_c

    def done(self, ev):
        self.use_ev[self.cur] = ev


class WChunk:

    def __init__(self, k, nc, es, name, dram, tc, chunk, nbuf=3):
        self.k = k
        self.dram = dram
        self.tc, self.chunk, self.nbuf = tc, chunk, nbuf
        w = chunk * tc
        self.wf = [_sb(nc, es, f"{name}_f{i}", [128, w], F32) for i in range(nbuf)]
        self.wb = [_sb(nc, es, f"{name}_b{i}", [128, w], BF16) for i in range(nbuf)]
        self.slots = [k.dslot() for _ in range(nbuf)]
        self.cast_ev = [None] * nbuf
        self.free_ev = [None] * nbuf
        self.idx = 0

    def _load(self, ci):
        b = ci % self.nbuf
        k = self.k
        ev_d = k.op("sync", lambda e, ci=ci, b=b: e.dma_start(
            out=self.wf[b][:], in_=self.dram[ci * 128:(ci + 1) * 128, :]), inc=16,
            waits=(self.cast_ev[b],), slot=self.slots[b])
        self.cast_ev[b] = k.op("scalar", lambda e, b=b: e.activation(
            out=self.wb[b][:], in_=self.wf[b][:], func=AF.Copy),
            waits=(ev_d, self.free_ev[b]))
        self.loaded = ci

    def get(self):
        i = self.idx
        self.idx += 1
        ci, ti = divmod(i, self.chunk)
        b = ci % self.nbuf
        ntot = self.dram.ap[0][1] // 128
        if i == 0:
            for c2 in range(min(self.nbuf, ntot)):
                self._load(c2)
        elif ti == 0 and self.loaded < min(ci + self.nbuf - 1, ntot - 1):
            self._load(ci + self.nbuf - 1)
        self.cur = b
        self.last_tile = (ti == self.chunk - 1)
        return self.wb[b][:, ti * self.tc:(ti + 1) * self.tc], self.cast_ev[b]

    def done(self, ev):
        if self.last_tile:
            self.free_ev[self.cur] = ev


def pack_tiles(tiles, tc, chunk):
    n = len(tiles)
    nch = -(-n // chunk)
    out = np.zeros((nch * 128, chunk * tc), dtype=np.float32)
    for i, t in enumerate(tiles):
        ci, ti = divmod(i, chunk)
        out[ci * 128:(ci + 1) * 128, ti * tc: ti * tc + t.shape[1]] = t
    return np.ascontiguousarray(out)


def emit_phi_evac(k, psum_ap, out_ap, tmp1, tmpe, tmp2, bias_col, mm_ev, extra=()):
    if bias_col is not None:
        ev1 = k.op("vector", lambda e: e.tensor_scalar(
            out=tmp1, in0=psum_ap, scalar1=bias_col, scalar2=0.0,
            op0=ALU.add, op1=ALU.min), waits=(mm_ev,) + tuple(extra))
        ev3 = k.op("vector", lambda e: e.tensor_scalar(
            out=tmp2, in0=psum_ap, scalar1=bias_col, scalar2=0.0,
            op0=ALU.add, op1=ALU.max), waits=(mm_ev,) + tuple(extra))
    else:
        ev1 = k.op("vector", lambda e: e.tensor_scalar(
            out=tmp1, in0=psum_ap, scalar1=0.0, scalar2=None, op0=ALU.min),
            waits=(mm_ev,) + tuple(extra))
        ev3 = k.op("vector", lambda e: e.tensor_scalar(
            out=tmp2, in0=psum_ap, scalar1=0.0, scalar2=None, op0=ALU.max),
            waits=(mm_ev,) + tuple(extra))
    ev2 = k.op("scalar", lambda e: e.activation(out=tmpe, in_=tmp1, func=AF.Exp),
               waits=(ev1,))
    ev4 = k.op("vector", lambda e: e.tensor_tensor(
        out=out_ap, in0=tmp2, in1=tmpe, op=ALU.add), waits=(ev2, ev3))
    return ev4


def emit_A1(k, nc, es, io):
    x0_fm, wkT, wvT, bkr, bvr, ones_bf, ones_col, o_kv = (
        io["x0_fm"], io["wkT"], io["wvT"], io["bk"], io["bv"],
        io["ones_bf"], io["ones_col"], io["o_kv"])

    x0bf = _sb(nc, es, "a1_x0bf", [128, 32 * TOK], BF16)
    xf = _sb(nc, es, "a1_xf", [128, TOK], F32)
    sl_x0 = k.dslot()
    ws = WStream(k, nc, es, "a1_w", 128, 512)
    brow_f = _sb(nc, es, "a1_bf", [1, 1024], F32)
    brow_b = _sb(nc, es, "a1_bb", [1, 1024], BF16)
    s_ones = _sb(nc, es, "a1_ones", [1, 128], BF16)
    s_onec = _sb(nc, es, "a1_onec", [128, 1], BF16)
    kT = _sb(nc, es, "a1_kT", [128, 8 * 1024], BF16)
    vT = _sb(nc, es, "a1_vT", [128, 8 * 1024], BF16)
    tmp1 = _sb(nc, es, "a1_t1", [128, 512], F32)
    tmp2 = _sb(nc, es, "a1_t2", [128, 512], F32)
    tmpe = _sb(nc, es, "a1_te", [128, 512], F32)
    okv = _sb(nc, es, "a1_okv", [128, 1032], F32)
    psum = [_ps(nc, es, f"a1_p{i}", [128, 512]) for i in range(8)]

    ev_ones = k.op("sync", lambda e: e.dma_start(out=s_ones[:], in_=ones_bf), inc=16)
    ev_onec = k.op("sync", lambda e: e.dma_start(out=s_onec[:], in_=ones_col), inc=16)

    x_evs = []
    prev_cast = None
    for kc in range(32):
        ev_d = k.op("sync", lambda e, kc=kc: e.dma_start(
            out=xf[:], in_=x0_fm[kc * 128:(kc + 1) * 128, :]), inc=16,
            waits=(prev_cast,), slot=sl_x0)
        ev_c = k.op("scalar", lambda e, kc=kc: e.activation(
            out=x0bf[:, kc * TOK:(kc + 1) * TOK], in_=xf[:], func=AF.Copy),
            waits=(ev_d,))
        prev_cast = ev_c
        x_evs.append(ev_c)

    kv_out = {"k": kT, "v": vT}
    w_dram = {"k": wkT, "v": wvT}
    b_dram = {"k": bkr, "v": bvr}
    phi_evs = []
    bank_free = [None] * 8
    for src in ("k", "v"):
        ev_b = k.op("sync", lambda e, src=src: e.dma_start(out=brow_f[:], in_=b_dram[src]), inc=16,
                    waits=(k.last["vector"],))
        ev_bb = k.op("vector", lambda e: e.tensor_copy(out=brow_b[:], in_=brow_f[:]), waits=(ev_b,))
        for ng in range(2):
            for kc in range(32):
                wtile, ev_w = ws.get(w_dram[src][kc * 128:(kc + 1) * 128, ng * 512:(ng + 1) * 512])
                mm_ev = None
                for tc in range(8):
                    fn = lambda e, wtile=wtile, kc=kc, tc=tc: e.matmul(
                        out=psum[tc][:],
                        lhsT=x0bf[:, kc * TOK + tc * 128: kc * TOK + tc * 128 + 128],
                        rhs=wtile[:, 0:512], start=(kc == 0), stop=False)
                    w = [ev_w, x_evs[kc]]
                    if kc == 0:
                        w += [bank_free[tc]]
                    if tc == 7:
                        mm_ev = k.op("tensor", fn, waits=w)
                    else:
                        k.raw("tensor", fn, waits=w)
                ws.done(mm_ev)
            for tc in range(8):
                ev_mm = k.op("tensor", lambda e, tc=tc, ng=ng: e.matmul(
                    out=psum[tc][:], lhsT=s_ones[:],
                    rhs=brow_b[:, ng * 512:(ng + 1) * 512],
                    start=False, stop=True), waits=(ev_bb, ev_ones))
                dst = kv_out[src][:, tc * 1024 + ng * 512: tc * 1024 + ng * 512 + 512]
                if src == "k":
                    ev4 = emit_phi_evac(
                        k, psum[tc][:], dst,
                        tmp1[:], tmpe[:], tmp2[:], None, ev_mm)
                else:
                    ev4 = k.op("vector", lambda e, tc=tc, dst=dst: e.tensor_copy(
                        out=dst, in_=psum[tc][:]), waits=(ev_mm,))
                phi_evs.append(ev4)
                bank_free[tc] = k.echo("vector")

    kv_done = []
    for h in range(8):
        mm_ev = None
        for tc in range(8):
            fn = lambda e, h=h, tc=tc: e.matmul(
                out=psum[h // 4][:, (h % 4) * 128:(h % 4) * 128 + 128],
                lhsT=kT[:, tc * 1024 + h * 128: tc * 1024 + h * 128 + 128],
                rhs=vT[:, tc * 1024 + h * 128: tc * 1024 + h * 128 + 128],
                start=(tc == 0), stop=(tc == 7))
            w = list(phi_evs) + ([bank_free[h // 4]] if tc == 0 else [])
            if tc == 7:
                mm_ev = k.op("tensor", fn, waits=w)
            else:
                k.raw("tensor", fn, waits=w)
        kv_done.append(mm_ev)
    ks_done = []
    for h in range(8):
        mm_ev = None
        for tc in range(8):
            fn = lambda e, h=h, tc=tc: e.matmul(
                out=psum[2][:, h:h + 1],
                lhsT=kT[:, tc * 1024 + h * 128: tc * 1024 + h * 128 + 128],
                rhs=s_onec[:],
                start=(tc == 0), stop=(tc == 7))
            w = [bank_free[2], ev_onec] if (h == 0 and tc == 0) else ()
            if tc == 7:
                mm_ev = k.op("tensor", fn, waits=w)
            else:
                k.raw("tensor", fn, waits=w)
        ks_done.append(mm_ev)

    ev_e1 = k.op("vector", lambda e: e.tensor_copy(out=okv[:, 0:512], in_=psum[0][:]), waits=kv_done)
    ev_e2 = k.op("vector", lambda e: e.tensor_copy(out=okv[:, 512:1024], in_=psum[1][:]), waits=kv_done)
    ev_e3 = k.op("vector", lambda e: e.tensor_copy(out=okv[:, 1024:1032], in_=psum[2][:, 0:8]), waits=ks_done)
    ev_o = k.op("gpsimd", lambda e: e.dma_start(out=o_kv, in_=okv[:]),
                waits=(ev_e1, ev_e2, ev_e3), inc=16)
    return ev_o


def emit_A2(k, nc, es, io):
    x0_fm, wqT, woT, bq, bo, kvA, kvB, ones_bf, ones_col, o_x1, x1pre_d = (
        io["x0_fm"], io["wqT"], io.get("woT"), io["bq"], io["bo"], io["kvA"],
        io.get("kvB"), io["ones_bf"], io["ones_col"], io["o_x1"], io["x1pre_d"])
    lnrow_d = io["lnrow_d"]

    x0bf = _sb(nc, es, "a2_x0bf", [128, 32 * TOK], BF16)
    xf = _sb(nc, es, "a2_xf", [128, TOK], F32)
    sl_x0 = k.dslot(); sl_bq = k.dslot(); sl_bo = k.dslot(); sl_xr = k.dslot()
    kva = _sb(nc, es, "a2_kva", [128, 1032], F32)
    kvb = _sb(nc, es, "a2_kvb", [128, 1032], F32)
    kvf = _sb(nc, es, "a2_kvf", [128, 1032], F32)
    kvbf = _sb(nc, es, "a2_kvbf", [128, 1024], BF16)
    rk = _sb(nc, es, "a2_rk", [128, 8], F32)
    wsq = WChunk(k, nc, es, "a2_wq", wqT, 128, 8, nbuf=2)
    wso = wsq
    qb = _sb(nc, es, "a2_qb", [128, TOK], BF16)
    attnb = _sb(nc, es, "a2_attnb", [128, 32 * TOK], BF16)
    bqs = _sb(nc, es, "a2_bqs", [128, 1], F32)
    bos = _sb(nc, es, "a2_bos", [128, 1], F32)
    tmp1 = _sb(nc, es, "a2_t1", [128, 512], F32)
    tmp2 = _sb(nc, es, "a2_t2", [128, 512], F32)
    tmpe = _sb(nc, es, "a2_te", [128, 512], F32)
    x1t = _sb(nc, es, "a2_x1t", [128, 512], F32)
    x1bb = _sb(nc, es, "a2_x1b", [128, 512], BF16)
    xsq = _sb(nc, es, "a2_xsq", [128, 512], BF16)
    s_ones = _sb(nc, es, "a2_ones", [1, 128], BF16)
    s_onec = _sb(nc, es, "a2_onec", [128, 1], BF16)
    srow = _sb(nc, es, "a2_srow", [1, 6 * 1024], F32)
    arep = _sb(nc, es, "a2_arep", [128, TOK], F32)
    brep = _sb(nc, es, "a2_brep", [128, TOK], F32)
    psum = [_ps(nc, es, f"a2_p{i}", [128, 512]) for i in range(4)]
    pstat = [_ps(nc, es, f"a2_st{j}", [1, 512]) for j in range(4)]

    ev_ones = k.op("sync", lambda e: e.dma_start(out=s_ones[:], in_=ones_bf), inc=16)
    ev_onec = k.op("sync", lambda e: e.dma_start(out=s_onec[:], in_=ones_col), inc=16)

    ev_a = k.op("sync", lambda e: e.dma_start(out=kva[:], in_=kvA), inc=16)
    if kvB is not None:
        ev_b = k.op("sync", lambda e: e.dma_start(out=kvb[:], in_=kvB), inc=16)
        ev_s = k.op("vector", lambda e: e.tensor_tensor(out=kvf[:], in0=kva[:], in1=kvb[:], op=ALU.add),
                    waits=(ev_a, ev_b))
    else:
        ev_s = k.op("vector", lambda e: e.tensor_copy(out=kvf[:], in_=kva[:]), waits=(ev_a,))
    ev_kb = k.op("scalar", lambda e: e.activation(out=kvbf[:], in_=kvf[:, 0:1024], func=AF.Copy),
                 waits=(ev_s,))
    ev_ke = k.op("vector", lambda e: e.tensor_scalar(
        out=kva[:, 0:8], in0=kvf[:, 1024:1032], scalar1=1e-10, scalar2=None, op0=ALU.add),
        waits=(ev_s,))
    ev_rk = k.op("vector", lambda e: e.reciprocal(out=rk[:], in_=kva[:, 0:8]), waits=(ev_ke,))

    xf2 = _sb(nc, es, "a2_xf2", [128, TOK], F32)
    sl_x0b = k.dslot()
    x_evs = []
    prev_cast = [None, None]
    for kc in range(32):
        li = kc % 2
        buf = xf if li == 0 else xf2
        ev_d = k.op("sync" if li == 0 else "gpsimd", lambda e, kc=kc, buf=buf: e.dma_start(
            out=buf[:], in_=x0_fm[kc * 128:(kc + 1) * 128, :]), inc=16,
            waits=(prev_cast[li],), slot=(sl_x0 if li == 0 else sl_x0b))
        ev_c = k.op("scalar", lambda e, kc=kc, buf=buf: e.activation(
            out=x0bf[:, kc * TOK:(kc + 1) * TOK], in_=buf[:], func=AF.Copy),
            waits=(ev_d,))
        prev_cast[li] = ev_c
        x_evs.append(ev_c)

    k.set_scope('A2_q')
    attn_evs = []
    prev_bq = None
    prev_phi = [None, None]
    bank_free = [None] * 4
    for c in range(32):
        ev_bqc = k.op("sync", lambda e, c=c: e.dma_start(
            out=bqs[:], in_=bq[c * 128:(c + 1) * 128, :]), inc=16, waits=(prev_bq,), slot=sl_bq)
        q_done = None
        for kc in range(32):
            wtile, ev_w = wsq.get()
            mm_ev = None
            for tg in range(2):
                fn = lambda e, wtile=wtile, kc=kc, tg=tg: e.matmul(
                    out=psum[tg][:], lhsT=wtile,
                    rhs=x0bf[:, kc * TOK + tg * 512: kc * TOK + tg * 512 + 512],
                    start=(kc == 0), stop=(kc == 31))
                w = [ev_w, x_evs[kc]]
                if kc == 0:
                    w += [bank_free[tg]]
                if tg == 1:
                    mm_ev = k.op("tensor", fn, waits=w)
                else:
                    k.raw("tensor", fn, waits=w)
            wsq.done(mm_ev)
            if kc == 31:
                q_done = mm_ev
        for tg in range(2):
            ev4 = emit_phi_evac(
                k, psum[tg][:], qb[:, tg * 512:(tg + 1) * 512],
                tmp1[:], tmpe[:], tmp2[:], bqs[:, 0:1], q_done,
                extra=(ev_bqc, prev_phi[tg]))
            bank_free[tg] = k.echo("vector")
            prev_phi[tg] = ev4
            ev_mm = k.op("tensor", lambda e, c=c, tg=tg: e.matmul(
                out=psum[2 + tg][:], lhsT=kvbf[:, (c // 4) * 128:(c // 4) * 128 + 128],
                rhs=qb[:, tg * 512:(tg + 1) * 512], start=True, stop=True),
                waits=(ev4, ev_kb, bank_free[2 + tg]))
            ev_dv = k.op("vector", lambda e, c=c, tg=tg: e.tensor_scalar(
                out=attnb[:, c * TOK + tg * 512: c * TOK + tg * 512 + 512],
                in0=psum[2 + tg][:], scalar1=rk[:, (c // 4):(c // 4) + 1], scalar2=None,
                op0=ALU.mult), waits=(ev_mm, ev_rk))
            bank_free[2 + tg] = k.echo("vector")
            attn_evs.append(ev_dv)
        prev_bq = prev_phi[1]

    k.set_scope('A2_wo')
    stat_done = []
    prev_bo = None
    prev_s1 = prev_s2 = None
    ev_x1_prev = [None, None]
    ev_st_prev = [None, None]
    ev_cb_prev = [None, None]
    for mc in range(32):
        ev_boc = k.op("sync", lambda e, mc=mc: e.dma_start(
            out=bos[:], in_=bo[mc * 128:(mc + 1) * 128, :]), inc=16, waits=(prev_bo,), slot=sl_bo)
        o_done = None
        for kc in range(32):
            wtile, ev_w = wso.get()
            mm_ev = None
            for tg in range(2):
                fn = lambda e, wtile=wtile, kc=kc, tg=tg: e.matmul(
                    out=psum[tg][:], lhsT=wtile,
                    rhs=attnb[:, kc * TOK + tg * 512: kc * TOK + tg * 512 + 512],
                    start=(kc == 0), stop=(kc == 31))
                w = list(attn_evs) if (kc == 0 and mc == 0) else []
                w += [ev_w]
                if kc == 0:
                    w += [bank_free[tg]]
                if tg == 1:
                    mm_ev = k.op("tensor", fn, waits=w)
                else:
                    k.raw("tensor", fn, waits=w)
            wso.done(mm_ev)
            if kc == 31:
                o_done = mm_ev
        for tg in range(2):
            ev_xd = k.op("scalar", lambda e, mc=mc, tg=tg: e.dma_start(
                out=xf[:, tg * 512:(tg + 1) * 512],
                in_=x0_fm[mc * 128:(mc + 1) * 128, tg * 512:(tg + 1) * 512]),
                inc=16, waits=(ev_x1_prev[tg], x_evs[-1]), slot=sl_xr)
            ev_x1 = k.op("vector", lambda e, tg=tg: e.scalar_tensor_tensor(
                out=x1t[:], in0=psum[tg][:], scalar=bos[:, 0:1], in1=xf[:, tg * 512:(tg + 1) * 512],
                op0=ALU.add, op1=ALU.add),
                waits=(o_done, ev_boc, ev_xd, ev_st_prev[0], ev_st_prev[1],
                       ev_cb_prev[0], ev_cb_prev[1]))
            bank_free[tg] = k.echo("vector")
            ev_x1_prev[tg] = ev_x1
            ev_st = k.op("gpsimd", lambda e, mc=mc, tg=tg: e.dma_start(
                out=x1pre_d[mc * 128:(mc + 1) * 128, tg * 512:(tg + 1) * 512], in_=x1t[:]),
                waits=(ev_x1,), inc=16)
            ev_cb = k.op("scalar", lambda e: e.activation(out=x1bb[:], in_=x1t[:], func=AF.Copy),
                         waits=(ev_x1, prev_s1))
            ev_sq = k.op("scalar", lambda e: e.activation(out=xsq[:], in_=x1t[:], func=AF.Square),
                         waits=(ev_x1, prev_s2))
            ev_s1 = k.op("tensor", lambda e, mc=mc, tg=tg: e.matmul(
                out=pstat[tg][:], lhsT=s_onec[:], rhs=x1bb[:],
                start=(mc == 0), stop=(mc == 31)),
                waits=(ev_cb, ev_onec))
            ev_s2 = k.op("tensor", lambda e, mc=mc, tg=tg: e.matmul(
                out=pstat[2 + tg][:], lhsT=s_onec[:], rhs=xsq[:],
                start=(mc == 0), stop=(mc == 31)),
                waits=(ev_sq,))
            prev_s1, prev_s2 = ev_s1, ev_s2
            ev_cb_prev[tg] = ev_sq
            ev_st_prev[tg] = ev_st
            if mc == 31:
                stat_done += [ev_s1, ev_s2]
        prev_bo = ev_x1_prev[1]

    ev_r1a = k.op("vector", lambda e: e.tensor_copy(out=srow[:, 0:512], in_=pstat[0][:]), waits=stat_done)
    ev_r1b = k.op("vector", lambda e: e.tensor_copy(out=srow[:, 512:1024], in_=pstat[1][:]), waits=stat_done)
    ev_r2a = k.op("vector", lambda e: e.tensor_copy(out=srow[:, 1024:1536], in_=pstat[2][:]), waits=stat_done)
    ev_r2b = k.op("vector", lambda e: e.tensor_copy(out=srow[:, 1536:2048], in_=pstat[3][:]), waits=stat_done)
    ev_m = k.op("vector", lambda e: e.tensor_scalar(
        out=srow[:, 2048:3072], in0=srow[:, 0:1024], scalar1=1.0 / D, scalar2=None, op0=ALU.mult),
        waits=(ev_r1a, ev_r1b))
    ev_msq = k.op("vector", lambda e: e.tensor_tensor(
        out=srow[:, 3072:4096], in0=srow[:, 2048:3072], in1=srow[:, 2048:3072], op=ALU.mult),
        waits=(ev_m,))
    ev_v = k.op("vector", lambda e: e.scalar_tensor_tensor(
        out=srow[:, 4096:5120], in0=srow[:, 1024:2048], scalar=1.0 / D, in1=srow[:, 3072:4096],
        op0=ALU.mult, op1=ALU.subtract), waits=(ev_msq, ev_r2a, ev_r2b))
    ev_ve = k.op("vector", lambda e: e.tensor_scalar(
        out=srow[:, 5120:6144], in0=srow[:, 4096:5120], scalar1=1e-5, scalar2=None, op0=ALU.add),
        waits=(ev_v,))
    ev_sq2 = k.op("scalar", lambda e: e.activation(
        out=srow[:, 3072:4096], in_=srow[:, 5120:6144], func=AF.Sqrt), waits=(ev_ve,))
    ev_rc = k.op("vector", lambda e: e.reciprocal(
        out=srow[:, 4096:5120], in_=srow[:, 3072:4096]), waits=(ev_sq2,))
    ev_bb2 = k.op("vector", lambda e: e.scalar_tensor_tensor(
        out=srow[:, 5120:6144], in0=srow[:, 2048:3072], scalar=-1.0, in1=srow[:, 4096:5120],
        op0=ALU.mult, op1=ALU.mult), waits=(ev_rc,))
    ev_wa = k.op("gpsimd", lambda e: e.dma_start(
        out=lnrow_d[0:1, 0:1024], in_=srow[:, 4096:5120]), waits=(ev_rc,), inc=16)
    ev_wb = k.op("gpsimd", lambda e: e.dma_start(
        out=lnrow_d[1:2, 0:1024], in_=srow[:, 5120:6144]), waits=(ev_bb2,), inc=16)
    ev_ar = k.op("gpsimd", lambda e: e.dma_start(
        out=arep[:], in_=bass.AP(lnrow_d.tensor, 0, [[0, 128], [1, 1024]])),
        waits=(ev_wa,), inc=16)
    ev_br = k.op("gpsimd", lambda e: e.dma_start(
        out=brep[:], in_=bass.AP(lnrow_d.tensor, 1024, [[0, 128], [1, 1024]])),
        waits=(ev_wb,), inc=16)

    last_out = None
    prev_ops = [None, None]
    prev_a = [None, None]
    for mc in range(32):
        ev_xd = k.op("sync", lambda e, mc=mc: e.dma_start(
            out=xf[:], in_=x1pre_d[mc * 128:(mc + 1) * 128, :]), inc=16,
            waits=[prev_ops[0], prev_ops[1], ev_st_prev[0], ev_st_prev[1],
                   ev_x1_prev[0], ev_x1_prev[1], ev_cb_prev[0], ev_cb_prev[1],
                   prev_cast[0], prev_cast[1]],
            slot=sl_x0)
        ev_m1 = k.op("vector", lambda e: e.tensor_tensor(
            out=tmp1[:], in0=xf[:, 0:512], in1=arep[:, 0:512], op=ALU.mult),
            waits=(ev_xd, ev_ar, prev_a[0]))
        ev_m2 = k.op("vector", lambda e: e.tensor_tensor(
            out=tmp2[:], in0=xf[:, 512:1024], in1=arep[:, 512:1024], op=ALU.mult),
            waits=(ev_xd, ev_ar, prev_a[1]))
        ev_a1 = k.op("vector", lambda e: e.tensor_tensor(
            out=xf[:, 0:512], in0=tmp1[:], in1=brep[:, 0:512], op=ALU.add),
            waits=(ev_m1, ev_br))
        ev_a2 = k.op("vector", lambda e: e.tensor_tensor(
            out=xf[:, 512:1024], in0=tmp2[:], in1=brep[:, 512:1024], op=ALU.add),
            waits=(ev_m2, ev_br))
        last_out = k.op("gpsimd", lambda e, mc=mc: e.dma_start(
            out=o_x1[mc * 128:(mc + 1) * 128, :], in_=xf[:]), waits=(ev_a1, ev_a2), inc=16)
        prev_ops = [last_out, ev_a2]
        prev_a = [ev_a1, ev_a2]
    return last_out


def build_L1():
    nc = bass.Bass("TRN2", target_bir_lowering=False)
    es = contextlib.ExitStack()
    io = dict(
        x0_fm=nc.dram_tensor("x0_fm", [D, TOK], F32, kind="ExternalInput").ap(),
        wkT=nc.dram_tensor("wkT", [D, 1024], F32, kind="ExternalInput").ap(),
        wvT=nc.dram_tensor("wvT", [D, 1024], F32, kind="ExternalInput").ap(),
        bk=nc.dram_tensor("bk", [1, 1024], F32, kind="ExternalInput").ap(),
        bv=nc.dram_tensor("bv", [1, 1024], F32, kind="ExternalInput").ap(),
        ones_bf=nc.dram_tensor("ones_bf", [1, 128], BF16, kind="ExternalInput").ap(),
        ones_col=nc.dram_tensor("ones_col", [128, 1], BF16, kind="ExternalInput").ap(),
        o_kv=nc.dram_tensor("o_kv", [128, 1032], F32, kind="ExternalOutput").ap(),
    )
    k = KB(nc, es)
    k.init_echo(nc, es)
    blk = es.enter_context(nc.Block())
    ev = emit_A1(k, nc, es, io)
    k.wait("gpsimd", (ev,))
    k.build(blk)
    es.close()
    return nc


def build_L2(debug=False):
    nc = bass.Bass("TRN2", target_bir_lowering=False)
    es = contextlib.ExitStack()
    io = dict(
        x0_fm=nc.dram_tensor("x0_fm", [D, TOK], F32, kind="ExternalInput").ap(),
        wqT=nc.dram_tensor("wqT", [2 * 128 * 128, 1024], F32, kind="ExternalInput").ap(),
        bq=nc.dram_tensor("bq", [D, 1], F32, kind="ExternalInput").ap(),
        bo=nc.dram_tensor("bo", [D, 1], F32, kind="ExternalInput").ap(),
        kvA=nc.dram_tensor("kvA", [128, 1032], F32, kind="ExternalInput").ap(),
        kvB=nc.dram_tensor("kvB", [128, 1032], F32, kind="ExternalInput").ap(),
        ones_bf=nc.dram_tensor("ones_bf", [1, 128], BF16, kind="ExternalInput").ap(),
        ones_col=nc.dram_tensor("ones_col", [128, 1], BF16, kind="ExternalInput").ap(),
        o_x1=nc.dram_tensor("o_x1", [D, TOK], F32, kind="ExternalOutput").ap(),
        x1pre_d=nc.dram_tensor("x1pre_d", [D, TOK], F32,
                               kind="ExternalOutput" if debug else "Internal").ap(),
        lnrow_d=nc.dram_tensor("lnrow_d", [2, 1024], F32, kind="Internal").ap(),
    )
    k = KB(nc, es)
    k.init_echo(nc, es)
    blk = es.enter_context(nc.Block())
    ev = emit_A2(k, nc, es, io)
    k.wait("gpsimd", (ev,))
    k.build(blk)
    es.close()
    return nc


def emit_ln(k, nc, es, name, src_d, out_d, lnrow_d, onec_bf_d, ndim=D, wait_evs=()):
    xf2b = [_sb(nc, es, f"{name}_xf{i}", [128, TOK], F32) for i in range(2)]
    xf = xf2b[0]
    cbb = _sb(nc, es, f"{name}_cb", [128, 512], BF16)
    sqb = _sb(nc, es, f"{name}_sq", [128, 512], BF16)
    tmp1 = [_sb(nc, es, f"{name}_t1{i}", [128, 512], F32) for i in range(2)]
    tmp2 = [_sb(nc, es, f"{name}_t2{i}", [128, 512], F32) for i in range(2)]
    srow = _sb(nc, es, f"{name}_sr", [1, 6 * 1024], F32)
    arep = _sb(nc, es, f"{name}_ar", [128, TOK], F32)
    brep = _sb(nc, es, f"{name}_br", [128, TOK], F32)
    s_onec = _sb(nc, es, f"{name}_oc", [128, 1], BF16)
    pstat = [_ps(nc, es, f"{name}_p{j}", [1, 512]) for j in range(4)]
    nch = ndim // 128
    sl_xf = k.dslot()
    sl_xf2 = k.dslot()

    ev_onec = k.op("sync", lambda e: e.dma_start(out=s_onec[:], in_=onec_bf_d), inc=16)
    prev_cb = [None, None]
    prev_sq = [None, None]
    prev_s = [None, None, None, None]
    stat_done = []
    for mc in range(nch):
        li = mc % 2
        xb = xf2b[li]
        ev_xd = k.op("sync" if li == 0 else "gpsimd", lambda e, mc=mc, xb=xb: e.dma_start(
            out=xb[:], in_=src_d[mc * 128:(mc + 1) * 128, :]), inc=16,
            waits=[prev_cb[li], prev_sq[li]] + list(wait_evs if mc == 0 else ()),
            slot=(sl_xf if li == 0 else sl_xf2))
        for tg in range(2):
            ev_cb = k.op("scalar", lambda e, tg=tg, xb=xb: e.activation(
                out=cbb[:], in_=xb[:, tg * 512:(tg + 1) * 512], func=AF.Copy),
                waits=(ev_xd, prev_s[0], prev_s[1]))
            ev_sq = k.op("scalar", lambda e, tg=tg, xb=xb: e.activation(
                out=sqb[:], in_=xb[:, tg * 512:(tg + 1) * 512], func=AF.Square),
                waits=(ev_xd, prev_s[2], prev_s[3]))
            ev_s1 = k.op("tensor", lambda e, mc=mc, tg=tg: e.matmul(
                out=pstat[tg][:], lhsT=s_onec[:], rhs=cbb[:],
                start=(mc == 0), stop=(mc == nch - 1)), waits=(ev_cb, ev_onec))
            ev_s2 = k.op("tensor", lambda e, mc=mc, tg=tg: e.matmul(
                out=pstat[2 + tg][:], lhsT=s_onec[:], rhs=sqb[:],
                start=(mc == 0), stop=(mc == nch - 1)), waits=(ev_sq,))
            prev_s[tg] = ev_s1
            prev_s[2 + tg] = ev_s2
            prev_cb[li], prev_sq[li] = ev_cb, ev_sq
            if mc == nch - 1:
                stat_done += [ev_s1, ev_s2]
    ev_r1a = k.op("vector", lambda e: e.tensor_copy(out=srow[:, 0:512], in_=pstat[0][:]), waits=stat_done)
    ev_r1b = k.op("vector", lambda e: e.tensor_copy(out=srow[:, 512:1024], in_=pstat[1][:]), waits=stat_done)
    ev_r2a = k.op("vector", lambda e: e.tensor_copy(out=srow[:, 1024:1536], in_=pstat[2][:]), waits=stat_done)
    ev_r2b = k.op("vector", lambda e: e.tensor_copy(out=srow[:, 1536:2048], in_=pstat[3][:]), waits=stat_done)
    ev_m = k.op("vector", lambda e: e.tensor_scalar(
        out=srow[:, 2048:3072], in0=srow[:, 0:1024], scalar1=1.0 / ndim, scalar2=None, op0=ALU.mult),
        waits=(ev_r1a, ev_r1b))
    ev_msq = k.op("vector", lambda e: e.tensor_tensor(
        out=srow[:, 3072:4096], in0=srow[:, 2048:3072], in1=srow[:, 2048:3072], op=ALU.mult),
        waits=(ev_m,))
    ev_v = k.op("vector", lambda e: e.scalar_tensor_tensor(
        out=srow[:, 4096:5120], in0=srow[:, 1024:2048], scalar=1.0 / ndim, in1=srow[:, 3072:4096],
        op0=ALU.mult, op1=ALU.subtract), waits=(ev_msq, ev_r2a, ev_r2b))
    ev_ve = k.op("vector", lambda e: e.tensor_scalar(
        out=srow[:, 5120:6144], in0=srow[:, 4096:5120], scalar1=1e-5, scalar2=None, op0=ALU.add),
        waits=(ev_v,))
    ev_sq2 = k.op("scalar", lambda e: e.activation(
        out=srow[:, 3072:4096], in_=srow[:, 5120:6144], func=AF.Sqrt), waits=(ev_ve,))
    ev_rc = k.op("vector", lambda e: e.reciprocal(
        out=srow[:, 4096:5120], in_=srow[:, 3072:4096]), waits=(ev_sq2,))
    ev_bb2 = k.op("vector", lambda e: e.scalar_tensor_tensor(
        out=srow[:, 5120:6144], in0=srow[:, 2048:3072], scalar=-1.0, in1=srow[:, 4096:5120],
        op0=ALU.mult, op1=ALU.mult), waits=(ev_rc,))
    ev_wa = k.op("gpsimd", lambda e: e.dma_start(
        out=lnrow_d[0:1, 0:1024], in_=srow[:, 4096:5120]), waits=(ev_rc,), inc=16)
    ev_wb = k.op("gpsimd", lambda e: e.dma_start(
        out=lnrow_d[1:2, 0:1024], in_=srow[:, 5120:6144]), waits=(ev_bb2,), inc=16)
    ev_ar = k.op("gpsimd", lambda e: e.dma_start(
        out=arep[:], in_=bass.AP(lnrow_d.tensor, 0, [[0, 128], [1, 1024]])),
        waits=(ev_wa,), inc=16)
    ev_br = k.op("gpsimd", lambda e: e.dma_start(
        out=brep[:], in_=bass.AP(lnrow_d.tensor, 1024, [[0, 128], [1, 1024]])),
        waits=(ev_wb,), inc=16)
    last_out = None
    prev_ops = [None, None]
    prev_a = [[None, None], [None, None]]
    for mc in range(nch):
        li = mc % 2
        xb = xf2b[li]
        ev_xd = k.op("sync" if li == 0 else "gpsimd", lambda e, mc=mc, xb=xb: e.dma_start(
            out=xb[:], in_=src_d[mc * 128:(mc + 1) * 128, :]), inc=16,
            waits=[prev_ops[li], prev_cb[li], prev_sq[li]],
            slot=(sl_xf if li == 0 else sl_xf2))
        ev_m1 = k.op("vector", lambda e, xb=xb, li=li: e.tensor_tensor(
            out=tmp1[li][:], in0=xb[:, 0:512], in1=arep[:, 0:512], op=ALU.mult),
            waits=(ev_xd, ev_ar, prev_a[li][0]))
        ev_m2 = k.op("vector", lambda e, xb=xb, li=li: e.tensor_tensor(
            out=tmp2[li][:], in0=xb[:, 512:1024], in1=arep[:, 512:1024], op=ALU.mult),
            waits=(ev_xd, ev_ar, prev_a[li][1]))
        ev_a1 = k.op("vector", lambda e, xb=xb, li=li: e.tensor_tensor(
            out=xb[:, 0:512], in0=tmp1[li][:], in1=brep[:, 0:512], op=ALU.add),
            waits=(ev_m1, ev_br))
        ev_a2 = k.op("vector", lambda e, xb=xb, li=li: e.tensor_tensor(
            out=xb[:, 512:1024], in0=tmp2[li][:], in1=brep[:, 512:1024], op=ALU.add),
            waits=(ev_m2, ev_br))
        last_out = k.op("scalar", lambda e, mc=mc, xb=xb: e.dma_start(
            out=out_d[mc * 128:(mc + 1) * 128, :], in_=xb[:]), waits=(ev_a1, ev_a2), inc=16)
        prev_ops[li] = last_out
        prev_cb[li] = ev_a1
        prev_sq[li] = ev_a2
        prev_a[li] = [ev_a1, ev_a2]
    return last_out


GATE_CHUNKS = list(range(0, 43)) + list(range(86, 129))


def emit_B(k, nc, es, io, phases=('b0','gate','down','ln')):
    x1_d = io["x1"]
    qgu = io["qw_gu8"]; scgu = io["sc_gu"]; qzgu = io["qz_gu8"]
    qdn = io["qw_dn8"]; scdn = io["sc_dn"]; qzdn = io["qz_dn8"]
    ind32T_d = io["ind32T"]
    h_d = io["h_d"]; sdn_d = io["sdn_d"]; zdnx_d = io["zdnx_d"]; zdn_d = io["zdn_d"]
    x2pre_d = io["x2pre_d"]
    o_x2 = io["o_x2"]; lnrow_d = io["lnrow_d"]; onec_d = io["ones_col"]

    x1bf = _sb(nc, es, "b_x1bf", [128, 32 * TOK], BF16)
    zbgu = _sb(nc, es, "b_zbgu", [32, N_GU], BF16)
    ind32T = _sb(nc, es, "b_i32t", [32, 16 * 128], BF16)

    es0 = contextlib.ExitStack()
    xf0 = _sb(nc, es0, "b0_xf", [128, TOK], F32)
    sl_x1 = k.dslot()
    x_evs = []
    prev_cast = None
    for kc in range(32):
        ev_d = k.op("sync", lambda e, kc=kc: e.dma_start(
            out=xf0[:], in_=x1_d[kc * 128:(kc + 1) * 128, :]), inc=16, waits=(prev_cast,), slot=sl_x1)
        ev_c = k.op("scalar", lambda e, kc=kc: e.activation(
            out=x1bf[:, kc * TOK:(kc + 1) * TOK], in_=xf0[:], func=AF.Copy),
            waits=(ev_d,))
        prev_cast = ev_c
        x_evs.append(ev_c)

    ev_i32t = k.op("sync", lambda e: e.dma_start(out=ind32T[:], in_=ind32T_d), inc=16)

    zqgu = _sb(nc, es0, "b0_zqgu", [32, N_IN], U8)
    zq8 = _sb(nc, es0, "b0_zq8", [86, 2048], U8)
    zu8 = _sb(nc, es0, "b0_zu8", [86, N_IN], U8)
    zdnb = _sb(nc, es0, "b0_zdnb", [86, D], BF16)
    ev_zg = k.op("sync", lambda e: e.dma_start(out=zqgu[:], in_=qzgu), inc=16)
    ev_zd = k.op("sync", lambda e: e.dma_start(out=zq8[:], in_=qzdn), inc=16)
    for half in range(2):
        op0 = ALU.bitwise_and if half == 0 else ALU.logical_shift_right
        sc1 = 15 if half == 0 else 4
        k.fence_all()
        ev_u = k.op("vector", lambda e, op0=op0, sc1=sc1: e.tensor_scalar(
            out=zu8[0:32, 0:N_IN], in0=zqgu[:], scalar1=sc1, scalar2=None, op0=op0),
            waits=(ev_zg,))
        k.fence_all()
        k.op("vector", lambda e, half=half: e.tensor_copy(
            out=zbgu[:, half * N_IN:(half + 1) * N_IN], in_=zu8[0:32, 0:N_IN]),
            waits=(ev_u,))
        k.fence_all()
        ev_u2 = k.op("vector", lambda e, op0=op0, sc1=sc1: e.tensor_scalar(
            out=zu8[0:86, 0:2048], in0=zq8[:], scalar1=sc1, scalar2=None, op0=op0),
            waits=(ev_zd,))
        k.fence_all()
        k.op("vector", lambda e, half=half: e.tensor_copy(
            out=zdnb[:, half * 2048:(half + 1) * 2048], in_=zu8[0:86, 0:2048]),
            waits=(ev_u2,))
        k.fence_all()
    ev_zw = k.op("gpsimd", lambda e: e.dma_start(out=zdn_d, in_=zdnb[:]),
                 waits=[k.last["vector"]], inc=16)

    ps_pre = [_ps(nc, es0, f"b0_ps{i}", [128, 512]) for i in range(4)]
    scr2 = _sb(nc, es0, "b0_scr2", [2, D], F32)
    scb2 = _sb(nc, es0, "b0_scb2", [2, D], BF16)
    zr2 = _sb(nc, es0, "b0_zr2", [2, D], BF16)
    sdt = [_sb(nc, es0, f"b0_sdt{i}", [128, 512], BF16) for i in range(2)]
    zdt = [_sb(nc, es0, f"b0_zdt{i}", [128, 512], BF16) for i in range(2)]
    ind2sb = _sb(nc, es0, "b0_i2", [2, 128], BF16)
    ev_i2 = k.op("sync", lambda e: e.dma_start(out=ind2sb[:], in_=io["ind2"]), inc=16)
    sl_sc2 = k.dslot(); sl_zr2 = k.dslot()
    sl_sdt = [k.dslot(), k.dslot()]
    sl_zdt = [k.dslot(), k.dslot()]
    prev_sc = None
    prev_zr = None
    prev_sdt = [None, None]
    prev_zdt = [None, None]
    prev_smm = [None, None]
    prev_zmm = [None, None]
    for hc in range(86):
        g0 = 2 * (hc % 43)
        ev_sd = k.op("scalar", lambda e, g0=g0: e.dma_start(
            out=scr2[:], in_=scdn[g0:g0 + 2, :]), inc=16, waits=(prev_sc,), slot=sl_sc2)
        ev_sc = k.op("scalar", lambda e: e.activation(out=scb2[:], in_=scr2[:], func=AF.Copy),
                     waits=(ev_sd, prev_smm[0], prev_smm[1]))
        prev_sc = ev_sc
        ev_zr = k.op("gpsimd", lambda e, g0=g0: e.dma_start(
            out=zr2[:], in_=zdn_d[g0:g0 + 2, :]), inc=16,
            waits=(ev_zw, prev_zmm[0], prev_zmm[1]), slot=sl_zr2)
        prev_zr = ev_zr
        for sl in range(8):
            par = sl % 2
            ev_mm = k.op("tensor", lambda e, sl=sl, par=par: e.matmul(
                out=ps_pre[par][:], lhsT=ind2sb[:], rhs=scb2[:, sl * 512:(sl + 1) * 512],
                start=True, stop=True), waits=(ev_sc, ev_i2, prev_sdt[par]))
            prev_smm[par] = ev_mm
            ev_e = k.op("scalar", lambda e, par=par: e.activation(
                out=sdt[par][:], in_=ps_pre[par][:], func=AF.Copy), waits=(ev_mm, prev_sdt[par]))
            ev_o = k.op("gpsimd", lambda e, hc=hc, sl=sl, par=par: e.dma_start(
                out=sdn_d[hc * 128:(hc + 1) * 128, sl * 512:(sl + 1) * 512], in_=sdt[par][:]),
                waits=(ev_e,), inc=16, slot=sl_sdt[par])
            prev_sdt[par] = ev_o
            ev_mz = k.op("tensor", lambda e, sl=sl, par=par: e.matmul(
                out=ps_pre[2 + par][:], lhsT=ind2sb[:], rhs=zr2[:, sl * 512:(sl + 1) * 512],
                start=True, stop=True), waits=(ev_zr, prev_zdt[par]))
            prev_zmm[par] = ev_mz
            ev_ez = k.op("vector", lambda e, par=par: e.tensor_copy(
                out=zdt[par][:], in_=ps_pre[2 + par][:]), waits=(ev_mz, prev_zdt[par]))
            ev_oz = k.op("gpsimd", lambda e, hc=hc, sl=sl, par=par: e.dma_start(
                out=zdnx_d[hc * 128:(hc + 1) * 128, sl * 512:(sl + 1) * 512], in_=zdt[par][:]),
                waits=(ev_ez,), inc=16, slot=sl_zdt[par])
            prev_zdt[par] = ev_oz
    k.fence_all()
    es0.close()
    k.set_scope('B_gate')
    if 'gate' not in phases:
        return k.last["gpsimd"] or k.last["vector"]

    esg = contextlib.ExitStack()
    qsl = [_sb(nc, esg, f"bg_q{i}", [128, 4096], U8) for i in range(2)]
    pl8 = [_sb(nc, esg, f"bg_pl{i}", [128, 4096], U8) for i in range(2)]
    tb = [_sb(nc, esg, f"bg_t{i}", [128, 4096], BF16) for i in range(2)]
    wsl = [_sb(nc, esg, f"bg_w{i}", [128, 4096], BF16) for i in range(2)]
    Ssb = [_sb(nc, esg, f"bg_S{i}", [128, 2048], BF16) for i in range(2)]
    scb = [_sb(nc, esg, f"bg_sc{i}", [32, 128], BF16) for i in range(2)]
    scbf = [_sb(nc, esg, f"bg_scf{i}", [32, 128], F32) for i in range(2)]
    sg = _sb(nc, esg, "bg_sg", [128, 1024], BF16)
    hsb = [_sb(nc, esg, f"bg_h{i}", [128, 1024], BF16) for i in range(2)]
    pacc = [_ps(nc, esg, f"bg_a{i}", [128, 512]) for i in range(4)]
    pZ = _ps(nc, esg, "bg_pZ", [128, 1024])
    pS2 = _ps(nc, esg, "bg_pS2", [128, 1024])
    sl_q = [k.dslot(), k.dslot()]
    sl_sc = [k.dslot(), k.dslot()]
    sl_h = [k.dslot(), k.dslot()]

    use_q = [None, None]
    use_pl = [None, None]
    use_t = [None, None]
    use_w = [None, None]
    use_S = [None, None]
    use_sc = [None, None]
    use_pZ = None
    use_pS2 = None
    use_sgw = None
    ev_sg = None
    use_h_pe = [None, None]
    use_h_dma = [None, None]
    acc_free = [None, None]
    prev_scl = None
    h_count = 0
    chunk_seq = []
    for jg in GATE_CHUNKS:
        chunk_seq.append((jg, True))
        chunk_seq.append((jg + 43, False))

    st = {"acc_free": [None, None, None, None], "use_sgw": None, "ev_sg": None,
          "use_h_pe": [None, None], "use_h_dma": [None, None], "h_count": 0,
          "pending": None}

    def make_main(par, is_gate, ec_w):
        def emit_main():
            b0_, b1_ = par * 2, par * 2 + 1
            mm_last = None
            for kc in range(32):
                for tg in range(2):
                    fn = lambda e, par=par, kc=kc, tg=tg: e.matmul(
                        out=pacc[par * 2 + tg][:],
                        lhsT=wsl[par][:, kc * 128:(kc + 1) * 128],
                        rhs=x1bf[:, kc * TOK + tg * 512: kc * TOK + tg * 512 + 512],
                        start=(kc == 0), stop=(kc == 31))
                    w = ([ec_w] + x_evs) if kc == 0 else []
                    if kc == 0:
                        w += [st["acc_free"][par * 2 + tg]]
                    if kc == 31 and tg == 1:
                        mm_last = k.op("tensor", fn, waits=w)
                    else:
                        k.raw("tensor", fn, waits=w)
            use_w[par] = mm_last
            if is_gate:
                ev_sg = None
                for tg in range(2):
                    evs = k.op("scalar", lambda e, tg=tg, b0_=b0_: e.activation(
                        out=sg[:, tg * 512:(tg + 1) * 512], in_=pacc[b0_ + tg][:], func=AF.Silu),
                        waits=(mm_last, st["use_sgw"]))
                    if tg == 1:
                        ev_sg = evs
                st["ev_sg"] = ev_sg
                ec = k.echo("scalar")
                st["acc_free"][b0_] = st["acc_free"][b1_] = ec
            else:
                hc = st["h_count"]
                hpar = hc % 2
                evh = None
                for tg in range(2):
                    evh = k.op("vector", lambda e, tg=tg, hpar=hpar, b0_=b0_: e.tensor_tensor(
                        out=hsb[hpar][:, tg * 512:(tg + 1) * 512],
                        in0=pacc[b0_ + tg][:], in1=sg[:, tg * 512:(tg + 1) * 512],
                        op=ALU.mult),
                        waits=(mm_last, st["ev_sg"], st["use_h_pe"][hpar], st["use_h_dma"][hpar]))
                ec = k.echo("vector")
                st["acc_free"][b0_] = st["acc_free"][b1_] = ec
                st["use_sgw"] = ec
                ev_ho = k.op("gpsimd", lambda e, hc=hc, hpar=hpar: e.dma_start(
                    out=h_d[hc * 128:(hc + 1) * 128, :], in_=hsb[hpar][:]), waits=(evh,), inc=16,
                    slot=sl_h[hpar])
                st["use_h_pe"][hpar] = evh
                st["use_h_dma"][hpar] = ev_ho
                st["h_count"] += 1
        return emit_main

    for ci, (c, is_gate) in enumerate(chunk_seq):
        par = ci % 2
        lo = c < 86
        r = c % 86
        ap_in = bass.AP(qgu.tensor, r * 128 * 4096, [[4096, 128], [1, 4096]])
        evq = k.op("sync", lambda e, par=par, ap_in=ap_in: e.dma_start(
            out=qsl[par][:], in_=ap_in), inc=16, waits=(use_q[par],), slot=sl_q[par])
        op0 = ALU.bitwise_and if lo else ALU.logical_shift_right
        sc1 = 15 if lo else 4
        evp = k.op("vector", lambda e, par=par, op0=op0, sc1=sc1: e.tensor_scalar(
            out=pl8[par][:], in0=qsl[par][:], scalar1=sc1, scalar2=None, op0=op0),
            waits=(evq, use_pl[par]))
        use_q[par] = evp
        evsd = k.op("scalar", lambda e, c=c, par=par: e.dma_start(
            out=scbf[par][:], in_=scgu[:, c * 128:(c + 1) * 128]), inc=16, waits=(prev_scl,),
            slot=sl_sc[par])
        evsc = k.op("scalar", lambda e, par=par: e.activation(
            out=scb[par][:], in_=scbf[par][:], func=AF.Copy), waits=(evsd, use_sc[par]))
        prev_scl = evsc
        sub_last = None
        for hs in range(2):
            pslc = slice(hs * 512, hs * 512 + 512)
            z_done = None
            s_done = None
            for j in range(8):
                k16 = 8 * hs + j
                fnz = lambda e, k16=k16, j=j, c=c, par=par: e.matmul(
                    out=pZ[:, j * 128:(j + 1) * 128],
                    lhsT=ind32T[:, k16 * 128:(k16 + 1) * 128],
                    rhs=zbgu[:, c * 128:(c + 1) * 128], start=True, stop=True)
                wz = [ev_i32t, use_pZ]
                if j == 7:
                    z_done = k.op("tensor", fnz, waits=wz)
                else:
                    k.raw("tensor", fnz, waits=wz)
            for j in range(8):
                k16 = 8 * hs + j
                fns = lambda e, k16=k16, j=j, par=par: e.matmul(
                    out=pS2[:, j * 128:(j + 1) * 128],
                    lhsT=ind32T[:, k16 * 128:(k16 + 1) * 128],
                    rhs=scb[par][:, 0:128], start=True, stop=True)
                ws_ = [evsc, use_pS2]
                if j == 7:
                    s_done = k.op("tensor", fns, waits=ws_)
                else:
                    k.raw("tensor", fns, waits=ws_)
            use_sc[par] = s_done
            evS = k.op("scalar", lambda e, hs=hs, par=par: e.activation(
                out=Ssb[par][:, hs * 1024:(hs + 1) * 1024], in_=pS2[:], func=AF.Copy),
                waits=(s_done, use_S[par]))
            use_pS2 = k.echo("scalar")
            sub_evs = []
            for rr in (hs, hs + 2):
                evsub = k.op("vector", lambda e, rr=rr, par=par: e.tensor_tensor(
                    out=tb[par][:, rr * 1024:(rr + 1) * 1024],
                    in0=pl8[par][:, rr * 1024:(rr + 1) * 1024], in1=pZ[:], op=ALU.subtract),
                    waits=(z_done, evp, use_t[par]))
                sub_evs.append(evsub)
            use_pZ = k.echo("vector")
            sub_last = sub_evs[-1]
            for i_r, rr in enumerate((hs, hs + 2)):
                evm = k.op("vector", lambda e, rr=rr, hs=hs, par=par: e.tensor_tensor(
                    out=wsl[par][:, rr * 1024:(rr + 1) * 1024],
                    in0=tb[par][:, rr * 1024:(rr + 1) * 1024],
                    in1=Ssb[par][:, hs * 1024:(hs + 1) * 1024], op=ALU.mult),
                    waits=(sub_evs[i_r], evS, use_w[par]))
        ec_w = k.echo("vector")
        use_t[par] = ec_w
        use_S[par] = ec_w
        use_pl[par] = sub_last
        if st["pending"] is not None:
            st["pending"]()
        st["pending"] = make_main(par, is_gate, ec_w)
    st["pending"]()
    k.fence_all()
    esg.close()
    k.set_scope('B_down')
    if 'down' not in phases:
        return k.last["gpsimd"] or k.last["vector"]

    esd = contextlib.ExitStack()
    NB = 4
    qd = [_sb(nc, esd, f"bd_q{i}", [128, 512], U8) for i in range(NB)]
    pld = [_sb(nc, esd, f"bd_pl{i}", [128, 512], U8) for i in range(NB)]
    tdn = [_sb(nc, esd, f"bd_t{i}", [128, 512], BF16) for i in range(NB)]
    sdn_t = [_sb(nc, esd, f"bd_sd{i}", [128, 512], BF16) for i in range(NB)]
    zdn_t = [_sb(nc, esd, f"bd_zd{i}", [128, 512], BF16) for i in range(NB)]
    wdn = [_sb(nc, esd, f"bd_w{i}", [128, 512], BF16) for i in range(NB)]
    hdn = [_sb(nc, esd, f"bd_h{i}", [128, 1024], BF16) for i in range(NB)]
    xres = _sb(nc, esd, "bd_xr", [128, 512], F32)
    x2t = _sb(nc, esd, "bd_x2", [128, 512], F32)
    psum8 = [_ps(nc, esd, f"bd_p{i}", [128, 512]) for i in range(8)]

    sl_qd = [k.dslot() for _ in range(NB)]
    sl_zd = [k.dslot() for _ in range(NB)]
    sl_sd = [k.dslot() for _ in range(NB)]
    sl_hd = [k.dslot() for _ in range(NB)]
    sl_xres = k.dslot()
    sl_x2o = k.dslot()
    use_qd = [None] * NB
    use_pld = [None] * NB
    use_tdn = [None] * NB
    use_sdn = [None] * NB
    use_zdn = [None] * NB
    use_wdn = [None] * NB
    use_hdn = [None] * NB
    acc_free2 = [None] * 8
    prev_res = None
    prev_x2 = None
    for mb in range(8):
        rb = mb % 4
        op0 = ALU.bitwise_and if mb < 4 else ALU.logical_shift_right
        sc1 = 15 if mb < 4 else 4
        mm_last = None
        for hc in range(86):
            par = hc % NB
            if io.get("_rhstest") == 4:
                evw = None
            else:
                evq = k.op("sync", lambda e, hc=hc, rb=rb, par=par: e.dma_start(
                    out=qd[par][:], in_=qdn[(rb * 86 + hc) * 128:(rb * 86 + hc + 1) * 128, :]),
                    inc=16, waits=(use_qd[par],), slot=sl_qd[par])
                evp = k.op("vector", lambda e, par=par, op0=op0, sc1=sc1: e.tensor_scalar(
                    out=pld[par][:], in0=qd[par][:], scalar1=sc1, scalar2=None, op0=op0),
                    waits=(evq, use_pld[par]))
                use_qd[par] = evp
                evzd = k.op("scalar", lambda e, hc=hc, mb=mb, par=par: e.dma_start(
                    out=zdn_t[par][:], in_=zdnx_d[hc * 128:(hc + 1) * 128, mb * 512:(mb + 1) * 512]),
                    inc=16, waits=(use_zdn[par],), slot=sl_zd[par])
                evsub = k.op("vector", lambda e, par=par: e.tensor_tensor(
                    out=tdn[par][:], in0=pld[par][:], in1=zdn_t[par][:], op=ALU.subtract),
                    waits=(evp, evzd, use_tdn[par]))
                use_pld[par] = evsub
                use_zdn[par] = evsub
                evsd = k.op("scalar", lambda e, hc=hc, mb=mb, par=par: e.dma_start(
                    out=sdn_t[par][:], in_=sdn_d[hc * 128:(hc + 1) * 128, mb * 512:(mb + 1) * 512]),
                    inc=16, waits=(use_sdn[par],), slot=sl_sd[par])
                evw = k.op("vector", lambda e, par=par: e.tensor_tensor(
                    out=wdn[par][:], in0=tdn[par][:], in1=sdn_t[par][:], op=ALU.mult),
                    waits=(evsub, evsd, use_wdn[par]))
                evw = k.echo("vector")
                use_tdn[par] = evw
                use_sdn[par] = evw
            evhd = k.op("sync", lambda e, hc=hc, par=par: e.dma_start(
                out=hdn[par][:], in_=h_d[hc * 128:(hc + 1) * 128, :]), inc=16,
                waits=(use_hdn[par],), slot=sl_hd[par])
            if io.get("_paranoid") in (True, "down"):
                k.fence_all()
            mm_last = None
            for mc4 in range(4):
                for tg in range(2):
                    rhs_sel = (x1bf[:, tg * 512:tg * 512 + 512] if io.get("_rhstest") in (1, 2)
                               else hdn[par][:, tg * 512:(tg + 1) * 512])
                    lhs_sel = (x1bf[:, 0:128] if io.get("_rhstest") in (2, 3, 4)
                               else wdn[par][:, mc4 * 128:(mc4 + 1) * 128])
                    fn = lambda e, mc4=mc4, tg=tg, par=par, hc=hc, rhs_sel=rhs_sel, lhs_sel=lhs_sel: e.matmul(
                        out=psum8[mc4 * 2 + tg][:], lhsT=lhs_sel,
                        rhs=rhs_sel,
                        start=(hc == 0), stop=(hc == 85))
                    w = [evw, evhd]
                    if hc == 0:
                        w += [acc_free2[mc4 * 2 + tg]]
                    if mc4 == 3 and tg == 1:
                        mm_last = k.op("tensor", fn, waits=w)
                    else:
                        k.raw("tensor", fn, waits=w)
            use_wdn[par] = mm_last
            use_hdn[par] = mm_last
        for mc4 in range(4):
            mc = 4 * mb + mc4
            for tg in range(2):
                ev_rd = k.op("scalar", lambda e, mc=mc, tg=tg: e.dma_start(
                    out=xres[:], in_=x1_d[mc * 128:(mc + 1) * 128, tg * 512:(tg + 1) * 512]),
                    inc=16, waits=(prev_res,), slot=sl_xres)
                ev_x2 = k.op("vector", lambda e, mc4=mc4, tg=tg: e.tensor_tensor(
                    out=x2t[:], in0=psum8[mc4 * 2 + tg][:], in1=xres[:], op=ALU.add),
                    waits=(mm_last, ev_rd, prev_x2))
                prev_res = ev_x2
                ev_o = k.op("gpsimd", lambda e, mc=mc, tg=tg: e.dma_start(
                    out=x2pre_d[mc * 128:(mc + 1) * 128, tg * 512:(tg + 1) * 512], in_=x2t[:]),
                    waits=(ev_x2,), inc=16, slot=sl_x2o)
                prev_x2 = ev_o
                acc_free2[mc4 * 2 + tg] = None
        ec2 = k.echo("vector")
        for j8 in range(8):
            acc_free2[j8] = ec2
    k.fence_all()
    esd.close()
    k.set_scope('B_ln')
    if 'ln' not in phases:
        return k.last["gpsimd"] or k.last["vector"]

    esl = contextlib.ExitStack()
    ev = emit_ln(k, nc, esl, "bln", x2pre_d, o_x2, lnrow_d, onec_d)
    k.fence_all()
    esl.close()
    return ev


def build_L3(debug=False, phases=('b0','gate','down','ln'), paranoid=False, rhstest=False):
    nc = bass.Bass("TRN2", target_bir_lowering=False)
    es = contextlib.ExitStack()
    io = dict(
        x1=nc.dram_tensor("x1", [D, TOK], F32, kind="ExternalInput").ap(),
        qw_gu8=nc.dram_tensor("qw_gu8", [86 * 128, 4096], U8, kind="ExternalInput").ap(),
        sc_gu=nc.dram_tensor("sc_gu", [32, N_GU], F32, kind="ExternalInput").ap(),
        qz_gu8=nc.dram_tensor("qz_gu8", [32, N_IN], U8, kind="ExternalInput").ap(),
        qw_dn8=nc.dram_tensor("qw_dn8", [4 * 86 * 128, 512], U8, kind="ExternalInput").ap(),
        sc_dn=nc.dram_tensor("sc_dn", [86, D], F32, kind="ExternalInput").ap(),
        qz_dn8=nc.dram_tensor("qz_dn8", [86, 2048], U8, kind="ExternalInput").ap(),
        ind2=nc.dram_tensor("ind2", [2, 128], BF16, kind="ExternalInput").ap(),
        ind32T=nc.dram_tensor("ind32T", [32, 16 * 128], BF16, kind="ExternalInput").ap(),
        ones_col=nc.dram_tensor("ones_col", [128, 1], BF16, kind="ExternalInput").ap(),
        o_x2=nc.dram_tensor("o_x2", [D, TOK], F32, kind="ExternalOutput").ap(),
        h_d=nc.dram_tensor("h_d", [N_IN, TOK], BF16,
                           kind="ExternalOutput" if debug else "Internal").ap(),
        sdn_d=nc.dram_tensor("sdn_d", [N_IN, D], BF16,
                             kind="ExternalOutput" if debug else "Internal").ap(),
        zdnx_d=nc.dram_tensor("zdnx_d", [N_IN, D], BF16, kind="Internal").ap(),
        zdn_d=nc.dram_tensor("zdn_d", [86, D], BF16, kind="Internal").ap(),
        x2pre_d=nc.dram_tensor("x2pre_d", [D, TOK], F32,
                               kind="ExternalOutput" if debug else "Internal").ap(),
        lnrow_d=nc.dram_tensor("lnrow_d", [2, 1024], F32, kind="Internal").ap(),
    )
    io["_paranoid"] = paranoid
    io["_rhstest"] = rhstest
    k = KB(nc, es)
    k.init_echo(nc, es)
    blk = es.enter_context(nc.Block())
    ev = emit_B(k, nc, es, io, phases=phases)
    k.fence_all()
    k.build(blk)
    es.close()
    return nc


def host_prep_A(inputs):
    x0 = inputs["L0_attn_hidden_states"]
    wq = inputs["L0_attn_Wq"]; bq = inputs["L0_attn_bq"]
    wk = inputs["L0_attn_Wk"]; bk = inputs["L0_attn_bk"]
    wv = inputs["L0_attn_Wv"]; bv = inputs["L0_attn_bv"]
    wo = inputs["L0_attn_Wo"]; bo = inputs["L0_attn_bo"]
    wqT = np.ascontiguousarray(wq[:, PERM_D].T)
    wkT = np.ascontiguousarray(wk[:, PERM_D].T)
    wvT = np.ascontiguousarray(wv[:, PERM_D].T)
    woT = np.ascontiguousarray(wo[PERM_D, :].T)
    wqT = np.concatenate([
        pack_tiles([wqT[kc * 128:(kc + 1) * 128, c * 128:(c + 1) * 128]
                    for c in range(32) for kc in range(32)], 128, 8),
        pack_tiles([woT[kc * 128:(kc + 1) * 128, mc * 128:(mc + 1) * 128]
                    for mc in range(32) for kc in range(32)], 128, 8)], axis=0)
    woT = None
    ones_bf = np.ones((1, 128), dtype=BF)
    ones_col = np.ones((128, 1), dtype=BF)
    shared = dict(wqT=wqT, wkT=wkT, wvT=wvT,
                  bq=np.ascontiguousarray(bq.reshape(-1, 1)),
                  bk=np.ascontiguousarray(bk.reshape(1, -1)),
                  bv=np.ascontiguousarray(bv.reshape(1, -1)),
                  bo=np.ascontiguousarray(bo[PERM_D].reshape(-1, 1)),
                  ones_bf=ones_bf, ones_col=ones_col)
    x0_fms = []
    for c in range(NCORE):
        beta, eta = c // 2, c % 2
        x0_fms.append(np.ascontiguousarray(x0[beta, eta * TOK:(eta + 1) * TOK, :][:, PERM_D].T))
    return shared, x0_fms


def host_prep_B(inputs):
    qw_gu8 = np.ascontiguousarray(inputs["L0_ffn_qweight_gu"][PERM_D]).view(np.uint8)
    qw_gu8 = np.ascontiguousarray(
        qw_gu8.reshape(32, 128, 86, 128).transpose(2, 1, 0, 3)).reshape(86 * 128, 32 * 128)
    sc_gu = np.ascontiguousarray(inputs["L0_ffn_scales_gu"][:, PERM_GU])
    qz_gu8 = np.ascontiguousarray(inputs["L0_ffn_qzeros_gu"]).view(np.uint8)
    qw_dn8 = np.ascontiguousarray(inputs["L0_ffn_qweight_dn"][PERM_H]).view(np.uint8)
    qw_dn8 = np.ascontiguousarray(
        qw_dn8.reshape(86, 128, 4, 512).transpose(2, 0, 1, 3)).reshape(4 * 86 * 128, 512)
    sc_dn = np.ascontiguousarray(inputs["L0_ffn_scales_dn"][:, PERM_D])
    qz_dn8 = np.ascontiguousarray(inputs["L0_ffn_qzeros_dn"]).view(np.uint8)
    ind2 = np.zeros((2, 128), dtype=BF); ind2[0, :64] = 1; ind2[1, 64:] = 1
    p = np.arange(128)
    ind32T = np.zeros((32, 16, 128), dtype=BF)
    for k16 in range(16):
        ind32T[2 * k16 + (p >= 64), k16, p] = 1
    ind32T = np.ascontiguousarray(ind32T.reshape(32, 16 * 128))
    ones_col = np.ones((128, 1), dtype=BF)
    return dict(qw_gu8=qw_gu8, sc_gu=sc_gu, qz_gu8=qz_gu8, qw_dn8=qw_dn8,
                sc_dn=sc_dn, qz_dn8=qz_dn8, ind2=ind2, ind32T=ind32T,
                ones_col=ones_col)


def run_stage_A(inputs):
    shared, x0_fms = host_prep_A(inputs)
    nc1 = build_L1()
    maps1 = [dict(x0_fm=x0_fms[c], wkT=shared["wkT"], wvT=shared["wvT"],
                  bk=shared["bk"], bv=shared["bv"], ones_bf=shared["ones_bf"],
                  ones_col=shared["ones_col"]) for c in range(NCORE)]
    res1 = run_bass_kernel_spmd(nc1, maps1, core_ids=list(range(NCORE)))
    kvs = [np.asarray(res1.results[c]["o_kv"], dtype=np.float32) for c in range(NCORE)]
    nc2 = build_L2()
    maps2 = [dict(x0_fm=x0_fms[c], wqT=shared["wqT"],
                  bq=shared["bq"], bo=shared["bo"],
                  kvA=kvs[(c // 2) * 2], kvB=kvs[(c // 2) * 2 + 1],
                  ones_bf=shared["ones_bf"], ones_col=shared["ones_col"])
             for c in range(NCORE)]
    res2 = run_bass_kernel_spmd(nc2, maps2, core_ids=list(range(NCORE)))
    x1s = [np.asarray(res2.results[c]["o_x1"], dtype=np.float32) for c in range(NCORE)]
    return x1s


def kernel(**inputs):
    return kernel_fused(**inputs)


def kernel_unfused(**inputs):
    x1s = run_stage_A(inputs)
    shb = host_prep_B(inputs)
    nc3 = build_L3()
    maps3 = [dict(x1=x1s[c], **shb) for c in range(NCORE)]
    res3 = run_bass_kernel_spmd(nc3, maps3, core_ids=list(range(NCORE)))
    x2s = [np.asarray(res3.results[c]["o_x2"], dtype=np.float32) for c in range(NCORE)]
    per_eta, shc = host_prep_C(inputs)
    nc4 = build_L4()
    maps4 = []
    for c in range(NCORE):
        beta, eta = c // 2, c % 2
        maps4.append(dict(x2a=x2s[2 * beta], x2b=x2s[2 * beta + 1],
                          **per_eta[eta], **shc))
    res4 = run_bass_kernel_spmd(nc4, maps4, core_ids=list(range(NCORE)))
    parts = [np.asarray(res4.results[c]["o_part"], dtype=np.float32) for c in range(NCORE)]
    ssqs = [np.asarray(res4.results[c]["o_ssq"], dtype=np.float32) for c in range(NCORE)]
    shd = host_prep_D(inputs)
    nc5 = build_L5()
    maps5 = []
    for c in range(NCORE):
        beta, eta = c // 2, c % 2
        sl = slice(eta * TOK, (eta + 1) * TOK)
        maps5.append(dict(
            partA=np.ascontiguousarray(parts[2 * beta][:, sl]),
            partB=np.ascontiguousarray(parts[2 * beta + 1][:, sl]),
            ssqA=np.ascontiguousarray(ssqs[2 * beta][:, sl]),
            ssqB=np.ascontiguousarray(ssqs[2 * beta + 1][:, sl]),
            x2f=x2s[c], **shd))
    res5 = run_bass_kernel_spmd(nc5, maps5, core_ids=list(range(NCORE)))
    out = np.empty((B, T, D), dtype=np.float32)
    for c in range(NCORE):
        beta, eta = c // 2, c % 2
        o = np.asarray(res5.results[c]["o_out"], dtype=np.float32)
        out[beta, eta * TOK:(eta + 1) * TOK, :] = o[INV_PERM_D].T
    return out


def emit_C(k, nc, es, io):
    x2a, x2b = io["x2a"], io["x2b"]
    wipT = io["wipT"]
    convw = io["convw"]; convb = io["convb"]
    dtb = io["dtb"]; alog = io["alog"]
    dexp = io["dexp"]; nwv = io["nw"]
    woutT = io["woutT"]
    oh64 = io["oh64"]; ohpair = io["ohpair"]
    ones64 = io["ones64"]
    mask0 = io["mask0"]; mask1 = io["mask1"]
    idn = io["idn"]
    zx_d = io["zx_d"]; ynw_d = io["ynw_d"]
    o_part = io["o_part"]; o_ssq = io["o_ssq"]
    part2 = io.get("_part2", False)

    TT = 2048

    k.set_scope('C1')
    es1 = contextlib.ExitStack()
    x2ab = _sb(nc, es1, "c1_x2", [128, 32 * TT], BF16)
    xf = [_sb(nc, es1, f"c1_xf{i}", [128, 1024], F32) for i in range(4)]
    sl_xa = [k.dslot() for _ in range(4)]
    prev_cast = [None] * 4
    x_evs = []
    for kc in range(32):
        for half, src in ((0, x2a), (1, x2b)):
            li = (2 * kc + half) % 4
            dq = "sync" if li % 2 == 0 else "gpsimd"
            ev_d = k.op(dq, lambda e, kc=kc, src=src, li=li: e.dma_start(
                out=xf[li][:], in_=src[kc * 128:(kc + 1) * 128, :]), inc=16,
                waits=(prev_cast[li],), slot=sl_xa[li])
            ev_c = k.op("scalar", lambda e, kc=kc, half=half, li=li: e.activation(
                out=x2ab[:, kc * TT + half * 1024: kc * TT + half * 1024 + 1024],
                in_=xf[li][:], func=AF.Copy), waits=(ev_d,))
            prev_cast[li] = ev_c
            x_evs.append(ev_c)
    ws1 = WChunk(k, nc, es1, "c1_w", wipT, 256, 4)
    zxt = [_sb(nc, es1, f"c1_zt{i}", [128, 512], BF16) for i in range(2)]
    sl_zx = [k.dslot(), k.dslot()]
    ps1 = [_ps(nc, es1, f"c1_p{i}", [128, 512]) for i in range(8)]
    bank_free = [None] * 8
    use_zxt = [None, None]
    zpar = 0
    for ocg in range(37):
        noc = 2 if ocg < 36 else 1
        mm_done = [None] * 8
        for kc in range(32):
            wtile, ev_w = ws1.get()
            for oc4 in range(noc):
                for tg in range(4):
                    b = oc4 * 4 + tg
                    fn = lambda e, wtile=wtile, kc=kc, oc4=oc4, tg=tg, b=b: e.matmul(
                        out=ps1[b][:], lhsT=wtile[:, oc4 * 128: oc4 * 128 + 128],
                        rhs=x2ab[:, kc * TT + tg * 512: kc * TT + tg * 512 + 512],
                        start=(kc == 0), stop=(kc == 31))
                    w = [ev_w, x_evs[2 * kc], x_evs[2 * kc + 1]]
                    if kc == 0:
                        w += [bank_free[b]]
                    if oc4 == noc - 1 and tg == 3:
                        mm_done[b] = k.op("tensor", fn, waits=w)
                    else:
                        k.raw("tensor", fn, waits=w)
            ws1.done(k.last["tensor"])
        last_mm = k.last["tensor"]
        for oc4 in range(noc):
            for tg in range(4):
                b = oc4 * 4 + tg
                oc = ocg * 2 + oc4
                ev_e = k.op("vector", lambda e, b=b, zp=zpar: e.tensor_copy(
                    out=zxt[zp][:], in_=ps1[b][:]), waits=(last_mm, use_zxt[zpar]))
                ev_ec = k.echo("vector")
                bank_free[b] = ev_ec
                ev_o = k.op("gpsimd", lambda e, oc=oc, tg=tg, zp=zpar: e.dma_start(
                    out=zx_d[oc * 128:(oc + 1) * 128, tg * 512:(tg + 1) * 512], in_=zxt[zp][:]),
                    waits=(ev_e,), inc=16, slot=sl_zx[zpar])
                use_zxt[zpar] = ev_o
                zpar = 1 - zpar
    k.fence_all()
    es1.close()

    k.set_scope('C2')
    es2 = contextlib.ExitStack()
    xbcr = _sb(nc, es2, "c2_xr", [128, 41 * 515], BF16)
    xbcc = _sb(nc, es2, "c2_xc", [128, 40 * 512], BF16)
    cwc = _sb(nc, es2, "c2_cw", [128, 160], F32)
    cbc = _sb(nc, es2, "c2_cb", [128, 40], F32)
    ct1 = _sb(nc, es2, "c2_t1", [128, 512], F32)
    ct2 = _sb(nc, es2, "c2_t2", [128, 512], F32)
    dtc = _sb(nc, es2, "c2_dt", [64, 8 * 512], F32)
    dAf = _sb(nc, es2, "c2_dA", [64, 256], F32)
    csf = [_sb(nc, es2, f"c2_cs{i}", [64, 256], F32) for i in range(2)]
    odf = _sb(nc, es2, "c2_od", [64, 256], BF16)
    decf = _sb(nc, es2, "c2_dec", [64, 256], BF16)
    Tcol = _sb(nc, es2, "c2_T", [64, 1], F32)
    eTc = _sb(nc, es2, "c2_eT", [64, 1], F32)
    hi_b = _sb(nc, es2, "c2_hi", [64, 256], BF16)
    lo_f = _sb(nc, es2, "c2_lof", [64, 256], F32)
    lo_b = _sb(nc, es2, "c2_lo", [64, 256], BF16)
    dt_b = _sb(nc, es2, "c2_dtb", [64, 256], BF16)
    csT = _sb(nc, es2, "c2_csT", [128, 128], F32)
    dtT = _sb(nc, es2, "c2_dtT", [128, 128], BF16)
    decT = _sb(nc, es2, "c2_decT", [128, 128], BF16)
    s_id = _sb(nc, es2, "c2_id", [128, 128], BF16)
    s_oh64 = _sb(nc, es2, "c2_oh64", [64, 64 * 128], BF16)
    s_ohp = _sb(nc, es2, "c2_ohp", [64, 32 * 128], BF16)
    s_o64 = _sb(nc, es2, "c2_o64", [64, 128], BF16)
    s_m0 = _sb(nc, es2, "c2_m0", [128, 256], BF16)
    s_m1 = _sb(nc, es2, "c2_m1", [128, 128], BF16)
    s_onec = _sb(nc, es2, "c2_oc", [128, 1], BF16)
    aCol = _sb(nc, es2, "c2_acol", [64, 1], F32)
    dtbc = _sb(nc, es2, "c2_dtbc", [64, 1], F32)
    dxc = _sb(nc, es2, "c2_dx", [128, 32], F32)
    nwc = _sb(nc, es2, "c2_nw", [128, 32], F32)
    scT = _sb(nc, es2, "c2_scT", [128, 4 * 512], BF16)
    xT = _sb(nc, es2, "c2_xT", [128, 256], BF16)
    xdtT = _sb(nc, es2, "c2_xdtT", [128, 256], BF16)
    xddT = _sb(nc, es2, "c2_xddT", [128, 256], BF16)
    BT = _sb(nc, es2, "c2_BT", [128, 4 * 256], BF16)
    LTe = _sb(nc, es2, "c2_LTe", [128, 256], F32)
    LTm = _sb(nc, es2, "c2_LTm", [128, 256], BF16)
    slt = _sb(nc, es2, "c2_slt", [128, 384], BF16)
    od_sb = _sb(nc, es2, "c2_odsb", [128, 256], F32)
    ytmp = _sb(nc, es2, "c2_yt", [128, 256], F32)
    ydx = _sb(nc, es2, "c2_ydx", [128, 256], F32)
    ysum = _sb(nc, es2, "c2_ys", [128, 256], F32)
    zp_b = _sb(nc, es2, "c2_zp", [128, 256], BF16)
    sz = _sb(nc, es2, "c2_sz", [128, 256], F32)
    yg = _sb(nc, es2, "c2_yg", [128, 256], F32)
    sqb = _sb(nc, es2, "c2_sq", [128, 256], BF16)
    ynt = _sb(nc, es2, "c2_ynt", [128, 256], BF16)
    afr = _sb(nc, es2, "c2_afr", [64, 128], BF16)
    Sf = _sb(nc, es2, "c2_Sf", [128, 32 * 128], F32)
    Sb = _sb(nc, es2, "c2_Sb", [128, 32 * 128], BF16)
    St = _sb(nc, es2, "c2_St", [128, 128], F32)
    ssqrow = _sb(nc, es2, "c2_ssq", [1, 2048], F32)
    pY = _ps(nc, es2, "c2_pY", [128, 512])
    pod = _ps(nc, es2, "c2_pod", [128, 512])
    pca = _ps(nc, es2, "c2_pca", [128, 512])
    ptr = [es2.enter_context(nc.psum_tensor(f"c2_ptr{i}", [128, 512], BF16)) for i in range(2)]
    psq = _ps(nc, es2, "c2_psq", [1, 512])
    psc = _ps(nc, es2, "c2_psc", [128, 512])

    sl_xbcr = k.dslot()
    sl_z = k.dslot()
    sl_yo = k.dslot()
    prev_sse = None
    prev_esz = None

    ev0 = k.op("sync", lambda e: e.dma_start(out=cwc[:], in_=convw), inc=16)
    ev1 = k.op("sync", lambda e: e.dma_start(out=cbc[:], in_=convb), inc=16)
    ev2 = k.op("sync", lambda e: e.dma_start(out=dtbc[:], in_=dtb), inc=16)
    ev3 = k.op("sync", lambda e: e.dma_start(out=aCol[:], in_=alog), inc=16)
    ev4 = k.op("sync", lambda e: e.dma_start(out=dxc[:], in_=dexp), inc=16)
    ev5 = k.op("sync", lambda e: e.dma_start(out=nwc[:], in_=nwv), inc=16)
    ev6 = k.op("sync", lambda e: e.dma_start(out=s_id[:], in_=idn), inc=16)
    ev7 = k.op("sync", lambda e: e.dma_start(out=s_oh64[:], in_=oh64), inc=16)
    ev8 = k.op("sync", lambda e: e.dma_start(out=s_ohp[:], in_=ohpair), inc=16)
    ev9 = k.op("sync", lambda e: e.dma_start(out=s_o64[:], in_=ones64), inc=16)
    eva = k.op("sync", lambda e: e.dma_start(out=s_m0[:], in_=mask0), inc=16)
    evb = k.op("sync", lambda e: e.dma_start(out=s_m1[:], in_=mask1), inc=16)
    evc = k.op("sync", lambda e: e.dma_start(out=s_onec[:], in_=io["ones_col"]), inc=16)
    evA1 = k.op("scalar", lambda e: e.activation(out=eTc[:], in_=aCol[:], func=AF.Exp), waits=(ev3,))
    evA2 = k.op("vector", lambda e: e.tensor_scalar(
        out=aCol[:], in0=eTc[:], scalar1=-1.0, scalar2=None, op0=ALU.mult), waits=(evA1,))
    evz0 = k.op("vector", lambda e: e.memset(Sf[:], 0.0), waits=())
    evz1 = k.op("vector", lambda e: e.memset(Sb[:], 0.0), waits=())
    evz2 = k.op("vector", lambda e: e.memset(xbcr[:], 0.0), waits=())
    evz3 = k.op("vector", lambda e: e.memset(ssqrow[:], 0.0), waits=())
    k.fence_all()

    for q2 in range(4):
        col0 = 512 * q2 - 3
        if q2 == 0:
            ap_in = bass.AP(zx_d.tensor, 32 * 128 * TT, [[TT, 128], [128 * TT, 41], [1, 512]])
            ev_xr = k.op("sync", lambda e, ap_in=ap_in: e.dma_start(
                out=bass.AP(xbcr, 3, [[41 * 515, 128], [515, 41], [1, 512]]), in_=ap_in),
                inc=16, waits=(evz2, k.last["vector"]), slot=sl_xbcr)
        else:
            ap_in = bass.AP(zx_d.tensor, 32 * 128 * TT + col0, [[TT, 128], [128 * TT, 41], [1, 515]])
            ev_xr = k.op("sync", lambda e, ap_in=ap_in: e.dma_start(
                out=bass.AP(xbcr, 0, [[41 * 515, 128], [515, 41], [1, 515]]), in_=ap_in),
                inc=16, waits=(k.last["vector"],), slot=sl_xbcr)
        conv_evs = []
        prev_c1 = prev_c2 = None
        for ci in range(40):
            evt0 = k.op("vector", lambda e, ci=ci: e.tensor_scalar(
                out=ct1[:], in0=bass.AP(xbcr, ci * 515 + 0, [[41 * 515, 128], [1, 512]]),
                scalar1=cwc[:, ci * 4: ci * 4 + 1], scalar2=None, op0=ALU.mult),
                waits=(ev_xr, ev0, prev_c1))
            evt1 = k.op("vector", lambda e, ci=ci: e.scalar_tensor_tensor(
                out=ct2[:], in0=bass.AP(xbcr, ci * 515 + 1, [[41 * 515, 128], [1, 512]]),
                scalar=cwc[:, ci * 4 + 1: ci * 4 + 2], in1=ct1[:], op0=ALU.mult, op1=ALU.add),
                waits=(evt0, prev_c2))
            evt2 = k.op("vector", lambda e, ci=ci: e.scalar_tensor_tensor(
                out=ct1[:], in0=bass.AP(xbcr, ci * 515 + 2, [[41 * 515, 128], [1, 512]]),
                scalar=cwc[:, ci * 4 + 2: ci * 4 + 3], in1=ct2[:], op0=ALU.mult, op1=ALU.add),
                waits=(evt1,))
            evt3 = k.op("vector", lambda e, ci=ci: e.scalar_tensor_tensor(
                out=ct2[:], in0=bass.AP(xbcr, ci * 515 + 3, [[41 * 515, 128], [1, 512]]),
                scalar=cwc[:, ci * 4 + 3: ci * 4 + 4], in1=ct1[:], op0=ALU.mult, op1=ALU.add),
                waits=(evt2,))
            prev_c1 = evt2
            evs = k.op("scalar", lambda e, ci=ci: e.activation(
                out=xbcc[:, ci * 512:(ci + 1) * 512], in_=ct2[:], func=AF.Silu,
                bias=cbc[:, ci:ci + 1]), waits=(evt3, ev1))
            prev_c2 = evs
            conv_evs.append(evs)
        evd0 = k.op("vector", lambda e, q2=q2: e.tensor_scalar(
            out=dtc[:, q2 * 1024: q2 * 1024 + 512],
            in0=bass.AP(xbcr, 40 * 515 + 3, [[41 * 515, 64], [1, 512]]),
            scalar1=dtbc[:, 0:1], scalar2=None, op0=ALU.add), waits=(ev_xr, ev2))
        evd1 = k.op("vector", lambda e, q2=q2: e.tensor_scalar(
            out=dtc[:, q2 * 1024 + 512: q2 * 1024 + 1024],
            in0=dtc[:, q2 * 1024: q2 * 1024 + 512],
            scalar1=0.0, scalar2=None, op0=ALU.max), waits=(evd0,))
        evd1b = k.op("vector", lambda e, q2=q2: e.scalar_tensor_tensor(
            out=ct1[0:64, 0:512], in0=dtc[0:64, q2 * 1024 + 512: q2 * 1024 + 1024],
            scalar=-2.0, in1=dtc[0:64, q2 * 1024: q2 * 1024 + 512],
            op0=ALU.mult, op1=ALU.add), waits=(evd1,))
        evd2 = k.op("scalar", lambda e: e.activation(
            out=ct2[0:64, 0:512], in_=ct1[0:64, 0:512], func=AF.Exp), waits=(evd1b,))
        evd3 = k.op("scalar", lambda e: e.activation(
            out=ct1[0:64, 0:512], in_=ct2[0:64, 0:512], func=AF.Ln, bias=1.0), waits=(evd2,))
        evd4 = k.op("vector", lambda e, q2=q2: e.tensor_tensor(
            out=dtc[:, q2 * 1024: q2 * 1024 + 512],
            in0=dtc[:, q2 * 1024 + 512: q2 * 1024 + 1024],
            in1=ct1[0:64, 0:512], op=ALU.add), waits=(evd3,))
        evd4 = k.echo("vector")

        for qq in range(2):
            q = 2 * q2 + qq
            lsl = qq * 256
            e_dA = k.op("vector", lambda e, q2=q2, lsl=lsl: e.tensor_scalar(
                out=dAf[:], in0=dtc[0:64, q2 * 1024 + lsl: q2 * 1024 + lsl + 256],
                scalar1=aCol[:, 0:1], scalar2=None, op0=ALU.mult), waits=(evd4, evA2))
            cur, oth = 0, 1
            e_cp = k.op("vector", lambda e: e.tensor_copy(out=csf[0][:], in_=dAf[:]), waits=(e_dA,))
            prev = e_cp
            sft = 1
            while sft < 256:
                e_s1 = k.op("vector", lambda e, cur=cur, oth=oth, sft=sft: e.tensor_tensor(
                    out=csf[oth][:, sft:256], in0=csf[cur][:, sft:256],
                    in1=csf[cur][:, 0:256 - sft], op=ALU.add), waits=(prev,))
                e_s2 = k.op("vector", lambda e, cur=cur, oth=oth, sft=sft: e.tensor_copy(
                    out=csf[oth][:, 0:sft], in_=csf[cur][:, 0:sft]), waits=(e_s1,))
                prev = e_s2
                cur, oth = oth, cur
                sft *= 2
            cs = csf[cur]
            e_od = k.op("scalar", lambda e, cur=cur: e.activation(
                out=odf[:], in_=csf[cur][:], func=AF.Exp), waits=(prev,))
            e_T = k.op("vector", lambda e, cur=cur: e.tensor_copy(
                out=Tcol[:], in_=csf[cur][:, 255:256]), waits=(prev,))
            e_d1 = k.op("vector", lambda e, cur=cur: e.tensor_scalar(
                out=lo_f[:], in0=csf[cur][:], scalar1=Tcol[:, 0:1], scalar2=None,
                op0=ALU.subtract), waits=(e_T,))
            e_dec = k.op("scalar", lambda e: e.activation(
                out=decf[:], in_=lo_f[:], func=AF.Exp, scale=-1.0), waits=(e_d1,))
            e_eT = k.op("scalar", lambda e: e.activation(
                out=eTc[:], in_=Tcol[:], func=AF.Exp), waits=(e_T,))
            e_h1 = k.op("vector", lambda e, cur=cur: e.tensor_copy(out=hi_b[:], in_=csf[cur][:]),
                        waits=(prev, k.last["scalar"]))
            e_h2 = k.op("vector", lambda e, cur=cur: e.tensor_tensor(
                out=lo_f[:], in0=csf[cur][:], in1=hi_b[:], op=ALU.subtract), waits=(e_h1, e_d1, e_dec))
            e_h3 = k.op("vector", lambda e: e.tensor_copy(out=lo_b[:], in_=lo_f[:]), waits=(e_h2,))
            e_db = k.op("vector", lambda e, q2=q2, lsl=lsl: e.tensor_copy(
                out=dt_b[:], in_=dtc[0:64, q2 * 1024 + lsl: q2 * 1024 + lsl + 256]), waits=(evd4,))
            ech = k.echo("vector")
            tr_evs = []
            for tb in range(2):
                e_t1 = k.op("tensor", lambda e, tb=tb: e.transpose(
                    out=ptr[0][0:128, 0:64], in_=_pad64(dt_b, tb), identity=s_id[0:64, 0:64]),
                    waits=(ech, ev6, k.last["vector"]))
                e_t1e = k.op("vector", lambda e, tb=tb: e.tensor_copy(
                    out=dtT[:, tb * 64: tb * 64 + 64], in_=ptr[0][0:128, 0:64]), waits=(e_t1,))
                e_t2 = k.op("tensor", lambda e, tb=tb: e.transpose(
                    out=ptr[0][0:128, 64:128], in_=_pad64(decf, tb), identity=s_id[0:64, 0:64]),
                    waits=(e_t1e,))
                e_t2e = k.op("vector", lambda e, tb=tb: e.tensor_copy(
                    out=decT[:, tb * 64: tb * 64 + 64], in_=ptr[0][0:128, 64:128]), waits=(e_t2,))
                e_t3 = k.op("tensor", lambda e, tb=tb: e.transpose(
                    out=ptr[0][0:128, 128:192], in_=_pad64(hi_b, tb), identity=s_id[0:64, 0:64]),
                    waits=(e_t2e, e_h1))
                e_t3e = k.op("vector", lambda e, tb=tb: e.tensor_copy(
                    out=csT[:, tb * 64: tb * 64 + 64], in_=ptr[0][0:128, 128:192]), waits=(e_t3,))
                e_t4 = k.op("tensor", lambda e, tb=tb: e.transpose(
                    out=ptr[0][0:128, 192:256], in_=_pad64(lo_b, tb), identity=s_id[0:64, 0:64]),
                    waits=(e_t3e, e_h3))
                e_t4e = k.op("vector", lambda e, tb=tb: e.tensor_tensor(
                    out=csT[:, tb * 64: tb * 64 + 64], in0=csT[:, tb * 64: tb * 64 + 64],
                    in1=ptr[0][0:128, 192:256], op=ALU.add), waits=(e_t4,))
                tr_evs.append(k.echo("vector"))
            sc_evs = []
            bt_evs = []
            for g in range(4):
                srow_ev = None
                for sb_ in range(2):
                    e_mm = k.op("tensor", lambda e, g=g, sb_=sb_, lsl=lsl: e.matmul(
                        out=psc[:, 0:256], lhsT=xbcc[:, (32 + g) * 512 + lsl + sb_ * 128:
                                                     (32 + g) * 512 + lsl + sb_ * 128 + 128],
                        rhs=xbcc[:, (36 + g) * 512 + lsl: (36 + g) * 512 + lsl + 256],
                        start=True, stop=True), waits=conv_evs + [k.last["vector"]])
                    e_ev = k.op("vector", lambda e, g=g, sb_=sb_: e.tensor_copy(
                        out=scT[:, g * 512 + sb_ * 256: g * 512 + sb_ * 256 + 256],
                        in_=psc[:, 0:256]), waits=(e_mm,))
                    srow_ev = k.echo("vector")
                sc_evs.append(srow_ev)
                for tb in range(2):
                    e_bt = k.op("tensor", lambda e, g=g, tb=tb, lsl=lsl: e.transpose(
                        out=ptr[1][0:128, 0:128],
                        in_=xbcc[:, (32 + g) * 512 + lsl + tb * 128: (32 + g) * 512 + lsl + tb * 128 + 128],
                        identity=s_id[:]), waits=conv_evs + [k.last["vector"], ev6])
                    e_bte = k.op("vector", lambda e, g=g, tb=tb: e.tensor_copy(
                        out=BT[:, g * 256 + tb * 128: g * 256 + tb * 128 + 128],
                        in_=ptr[1][0:128, 0:128]), waits=(e_bt,))
                    bt_evs.append(k.echo("vector"))

            for hp in range(32):
                g = hp // 8
                xsl = (0 + hp) * 512 + lsl
                exts = []
                for tb in range(2):
                    e_x = k.op("tensor", lambda e, hp=hp, tb=tb, xsl=xsl: e.transpose(
                        out=ptr[0][0:128, 0:128],
                        in_=xbcc[:, xsl + tb * 128: xsl + tb * 128 + 128],
                        identity=s_id[:]), waits=conv_evs + [k.last["vector"]])
                    e_xe = k.op("vector", lambda e, tb=tb: e.tensor_copy(
                        out=xT[:, tb * 128: tb * 128 + 128], in_=ptr[0][0:128, 0:128]), waits=(e_x,))
                    exts.append(k.echo("vector"))
                dt_view = bass.AP(dtT, 2 * hp, [[128, 128], [64, 2], [1, 2], [0, 64]])
                dec_view = bass.AP(decT, 2 * hp, [[128, 128], [64, 2], [1, 2], [0, 64]])
                e_xd = k.op("vector", lambda e, dt_view=dt_view: e.tensor_tensor(
                    out=xdtT[:], in0=xT[:], in1=dt_view, op=ALU.mult), waits=exts + tr_evs)
                e_xdd = k.op("vector", lambda e, dec_view=dec_view: e.tensor_tensor(
                    out=xddT[:], in0=xdtT[:], in1=dec_view, op=ALU.mult), waits=(e_xd,))
                e_xde = k.echo("vector")
                y_mms = []
                for hh in range(2):
                    h = 2 * hp + hh
                    e_c1 = k.op("tensor", lambda e, h=h: e.matmul(
                        out=pod[:, 256:512], lhsT=s_oh64[:, h * 128:(h + 1) * 128],
                        rhs=hi_b[:], start=True, stop=False), waits=(ev7, e_h1, k.last["vector"]))
                    e_c2 = k.op("tensor", lambda e, h=h: e.matmul(
                        out=pod[:, 256:512], lhsT=s_oh64[:, h * 128:(h + 1) * 128],
                        rhs=lo_b[:], start=False, stop=True), waits=(e_h3,))
                    for sb_ in range(2):
                        lrange = 256 - sb_ * 128
                        cs_col = bass.AP(csT, sb_ * 64 + 2 * hp + hh, [[128, 128], [1, 1]])
                        e_l1 = k.op("vector", lambda e, sb_=sb_, cs_col=cs_col, lrange=lrange: e.tensor_scalar(
                            out=LTe[:, 0:lrange], in0=pod[:, 256 + sb_ * 128: 512],
                            scalar1=cs_col, scalar2=0.0, op0=ALU.subtract, op1=ALU.min),
                            waits=(e_c2,) + tuple(tr_evs))
                        e_l2 = k.op("scalar", lambda e, lrange=lrange: e.activation(
                            out=LTe[:, 0:lrange] if False else ct1[:, 0:lrange],
                            in_=LTe[:, 0:lrange], func=AF.Exp), waits=(e_l1,))
                        msk = s_m0 if sb_ == 0 else s_m1
                        e_l3 = k.op("vector", lambda e, msk=msk, lrange=lrange: e.tensor_tensor(
                            out=LTm[:, 0:lrange], in0=ct1[:, 0:lrange], in1=msk[:, 0:lrange],
                            op=ALU.mult), waits=(e_l2,))
                        e_l4 = k.op("vector", lambda e, g=g, sb_=sb_, lrange=lrange: e.tensor_tensor(
                            out=slt[:, 0:lrange],
                            in0=bass.AP(scT, g * 512 + sb_ * 256 + sb_ * 128, [[4 * 512, 128], [1, lrange]]),
                            in1=LTm[:, 0:lrange], op=ALU.mult), waits=(e_l3,) + tuple(sc_evs))
                        e_l4e = k.echo("vector")
                        mmw = [e_l4e, e_xde, k.last["vector"]]
                        fn = lambda e, hh=hh, sb_=sb_, lrange=lrange: e.matmul(
                            out=pY[hh * 64: hh * 64 + 64, (256 - lrange):256],
                            lhsT=xdtT[:, sb_ * 128 + hh * 64: sb_ * 128 + hh * 64 + 64],
                            rhs=slt[:, 0:lrange], start=(sb_ == 0), stop=(sb_ == 1))
                        ev_y = k.op("tensor", fn, waits=mmw)
                        y_mms.append(ev_y)
                    ev_yo = k.op("tensor", lambda e, hp=hp, hh=hh, g=g, lsl=lsl: e.matmul(
                        out=pY[hh * 64: hh * 64 + 64, 256:512],
                        lhsT=Sb[:, hp * 128 + hh * 64: hp * 128 + hh * 64 + 64],
                        rhs=xbcc[:, (36 + g) * 512 + lsl: (36 + g) * 512 + lsl + 256],
                        start=True, stop=True), waits=(k.last["vector"],))
                    y_mms.append(ev_yo)
                e_op = k.op("tensor", lambda e, hp=hp: e.matmul(
                    out=pod[:, 0:256], lhsT=s_ohp[:, hp * 128:(hp + 1) * 128],
                    rhs=odf[:], start=True, stop=True), waits=(ev8, e_od, k.last["vector"]))
                e_ope = k.op("vector", lambda e: e.tensor_copy(out=od_sb[:], in_=pod[:, 0:256]),
                             waits=(e_op,))
                e_y1 = k.op("vector", lambda e: e.tensor_tensor(
                    out=ytmp[:], in0=pY[:, 256:512], in1=od_sb[:], op=ALU.mult),
                    waits=y_mms + [e_ope])
                e_y2 = k.op("vector", lambda e, hp=hp, xsl=xsl: e.scalar_tensor_tensor(
                    out=ydx[:], in0=xbcc[:, xsl: xsl + 256], scalar=dxc[:, hp:hp + 1],
                    in1=pY[:, 0:256], op0=ALU.mult, op1=ALU.add), waits=y_mms + [ev4])
                e_y3 = k.op("vector", lambda e: e.tensor_tensor(
                    out=ysum[:], in0=ydx[:], in1=ytmp[:], op=ALU.add), waits=(e_y1, e_y2))
                e_y3e = k.echo("vector")
                e_z = k.op("sync", lambda e, hp=hp, q=q: e.dma_start(
                    out=zp_b[:], in_=zx_d[hp * 128:(hp + 1) * 128, q * 256:(q + 1) * 256]),
                    inc=16, waits=(prev_esz,), slot=sl_z)
                e_sz = k.op("scalar", lambda e: e.activation(out=sz[:], in_=zp_b[:], func=AF.Silu),
                            waits=(e_z,))
                prev_esz = e_sz
                e_g = k.op("vector", lambda e: e.tensor_tensor(
                    out=yg[:], in0=ysum[:], in1=sz[:], op=ALU.mult), waits=(e_y3e, e_sz))
                e_q = k.op("scalar", lambda e: e.activation(out=sqb[:], in_=yg[:], func=AF.Square),
                           waits=(e_g,))
                e_sq = k.op("tensor", lambda e, hp=hp, q=q: e.matmul(
                    out=psq[0:1, (q % 2) * 256: (q % 2) * 256 + 256], lhsT=s_onec[:], rhs=sqb[:],
                    start=(hp == 0), stop=(hp == 31)),
                    waits=(e_q, evc) + ((prev_sse,) if hp == 0 else ()))
                e_nw = k.op("vector", lambda e, hp=hp: e.tensor_scalar(
                    out=ynt[:], in0=yg[:], scalar1=nwc[:, hp:hp + 1], scalar2=None,
                    op0=ALU.mult), waits=(e_g, ev5))
                e_nwe = k.echo("vector")
                e_yno = k.op("gpsimd", lambda e, hp=hp, q=q: e.dma_start(
                    out=ynw_d[hp * 128:(hp + 1) * 128, q * 256:(q + 1) * 256], in_=ynt[:]),
                    waits=(e_nwe,), inc=16, slot=sl_yo)
                con_mm = None
                for tb in range(2):
                    fn = lambda e, g=g, tb=tb: e.matmul(
                        out=pca[:, 0:128], lhsT=BT[:, g * 256 + tb * 128: g * 256 + tb * 128 + 128],
                        rhs=xddT[:, tb * 128: tb * 128 + 128],
                        start=(tb == 0), stop=(tb == 1))
                    w = [e_xde] + bt_evs + [k.last["vector"]]
                    if tb == 1:
                        con_mm = k.op("tensor", fn, waits=w)
                    else:
                        k.raw("tensor", fn, waits=w)
                e_af1 = k.op("vector", lambda e, hp=hp: e.tensor_scalar(
                    out=afr[:], in0=s_ohp[:, hp * 128:(hp + 1) * 128],
                    scalar1=eTc[:, 0:1], scalar2=None, op0=ALU.mult), waits=(e_eT, ev8))
                e_af1e = k.echo("vector")
                e_af2 = k.op("tensor", lambda e: e.matmul(
                    out=pca[:, 128:256], lhsT=s_o64[:], rhs=afr[:],
                    start=True, stop=True), waits=(e_af1e, ev9))
                e_s1 = k.op("vector", lambda e, hp=hp: e.tensor_tensor(
                    out=St[:], in0=Sf[:, hp * 128:(hp + 1) * 128], in1=pca[:, 128:256],
                    op=ALU.mult), waits=(e_af2, evz0))
                e_s2 = k.op("vector", lambda e, hp=hp: e.tensor_tensor(
                    out=Sf[:, hp * 128:(hp + 1) * 128], in0=St[:], in1=pca[:, 0:128],
                    op=ALU.add), waits=(e_s1, con_mm))
                e_s3 = k.op("vector", lambda e, hp=hp: e.tensor_copy(
                    out=Sb[:, hp * 128:(hp + 1) * 128], in_=Sf[:, hp * 128:(hp + 1) * 128]),
                    waits=(e_s2,))
                k.echo("vector")
            e_se = k.op("vector", lambda e, q=q: e.tensor_copy(
                out=ssqrow[:, q * 256:(q + 1) * 256],
                in_=psq[0:1, (q % 2) * 256: (q % 2) * 256 + 256]), waits=(k.last["tensor"],))
            prev_sse = k.echo("vector")
    if part2:
        k.op("gpsimd", lambda e: e.dma_start(out=o_ssq[0:1, :], in_=ssqrow[:, 0:1024]),
             waits=(k.last["vector"],), inc=16)
        ev_sso = k.op("gpsimd", lambda e: e.dma_start(out=o_ssq[1:2, :], in_=ssqrow[:, 1024:2048]),
                      waits=(k.last["vector"],), inc=16)
    else:
        ev_sso = k.op("gpsimd", lambda e: e.dma_start(out=o_ssq, in_=ssqrow[:]),
                      waits=(k.last["vector"],), inc=16)
    k.fence_all()
    es2.close()

    k.set_scope('C3')
    es3 = contextlib.ExitStack()
    ynall = _sb(nc, es3, "c3_yall", [128, 32 * TT], BF16)
    ws3 = WChunk(k, nc, es3, "c3_w", woutT, 256, 4)
    pot = _sb(nc, es3, "c3_ot", [128, 512], F32)
    ps3 = [_ps(nc, es3, f"c3_p{i}", [128, 512]) for i in range(8)]
    sl_yn = k.dslot()
    sl_po = k.dslot()
    y_evs = []
    prev_yl = None
    for kc in range(32):
        ev_y = k.op("sync", lambda e, kc=kc: e.dma_start(
            out=ynall[:, kc * TT:(kc + 1) * TT], in_=ynw_d[kc * 128:(kc + 1) * 128, :]), inc=16,
            waits=(prev_yl,), slot=sl_yn)
        prev_yl = ev_y
        y_evs.append(ev_y)
    bank_free3 = [None] * 8
    prev_pot = None
    for mcg in range(16):
        mm_done = [None] * 8
        for kc in range(32):
            wtile, ev_w = ws3.get()
            for mc2 in range(2):
                for tg in range(4):
                    b = mc2 * 4 + tg
                    fn = lambda e, wtile=wtile, kc=kc, mc2=mc2, tg=tg, b=b: e.matmul(
                        out=ps3[b][:], lhsT=wtile[:, mc2 * 128: mc2 * 128 + 128],
                        rhs=ynall[:, kc * TT + tg * 512: kc * TT + tg * 512 + 512],
                        start=(kc == 0), stop=(kc == 31))
                    w = [ev_w, y_evs[kc]]
                    if kc == 0:
                        w += [bank_free3[b]]
                    if mc2 == 1 and tg == 3:
                        mm_done[b] = k.op("tensor", fn, waits=w)
                    else:
                        k.raw("tensor", fn, waits=w)
            ws3.done(k.last["tensor"])
        last_mm = k.last["tensor"]
        for mc2 in range(2):
            for tg in range(4):
                b = mc2 * 4 + tg
                mc = mcg * 2 + mc2
                ev_e = k.op("vector", lambda e, b=b: e.tensor_copy(
                    out=pot[:], in_=ps3[b][:]), waits=(last_mm, prev_pot))
                ev_ee = k.echo("vector")
                bank_free3[b] = ev_ee
                if part2:
                    r0 = (tg // 2) * 4096 + mc * 128
                    c0 = (tg % 2) * 512
                    ev_o = k.op("gpsimd", lambda e, r0=r0, c0=c0: e.dma_start(
                        out=o_part[r0:r0 + 128, c0:c0 + 512], in_=pot[:]),
                        waits=(ev_e,), inc=16, slot=sl_po)
                else:
                    ev_o = k.op("gpsimd", lambda e, mc=mc, tg=tg: e.dma_start(
                        out=o_part[mc * 128:(mc + 1) * 128, tg * 512:(tg + 1) * 512], in_=pot[:]),
                        waits=(ev_e,), inc=16, slot=sl_po)
                prev_pot = ev_o
    k.fence_all()
    es3.close()
    return ev_sso


def _pad64(t, tb):
    return t[0:64, tb * 128: tb * 128 + 128]


def build_L4():
    nc = bass.Bass("TRN2", target_bir_lowering=False)
    es = contextlib.ExitStack()
    io = dict(
        x2a=nc.dram_tensor("x2a", [D, TOK], F32, kind="ExternalInput").ap(),
        x2b=nc.dram_tensor("x2b", [D, TOK], F32, kind="ExternalInput").ap(),
        wipT=nc.dram_tensor("wipT", [296 * 128, 1024], F32, kind="ExternalInput").ap(),
        convw=nc.dram_tensor("convw", [128, 160], F32, kind="ExternalInput").ap(),
        convb=nc.dram_tensor("convb", [128, 40], F32, kind="ExternalInput").ap(),
        dtb=nc.dram_tensor("dtb", [64, 1], F32, kind="ExternalInput").ap(),
        alog=nc.dram_tensor("alog", [64, 1], F32, kind="ExternalInput").ap(),
        dexp=nc.dram_tensor("dexp", [128, 32], F32, kind="ExternalInput").ap(),
        nw=nc.dram_tensor("nw", [128, 32], F32, kind="ExternalInput").ap(),
        woutT=nc.dram_tensor("woutT", [128 * 128, 1024], F32, kind="ExternalInput").ap(),
        oh64=nc.dram_tensor("oh64", [64, 64 * 128], BF16, kind="ExternalInput").ap(),
        ohpair=nc.dram_tensor("ohpair", [64, 32 * 128], BF16, kind="ExternalInput").ap(),
        ones64=nc.dram_tensor("ones64", [64, 128], BF16, kind="ExternalInput").ap(),
        mask0=nc.dram_tensor("mask0", [128, 256], BF16, kind="ExternalInput").ap(),
        mask1=nc.dram_tensor("mask1", [128, 128], BF16, kind="ExternalInput").ap(),
        idn=nc.dram_tensor("idn", [128, 128], BF16, kind="ExternalInput").ap(),
        ones_col=nc.dram_tensor("ones_col", [128, 1], BF16, kind="ExternalInput").ap(),
        o_part=nc.dram_tensor("o_part", [D, 2048], F32, kind="ExternalOutput").ap(),
        o_ssq=nc.dram_tensor("o_ssq", [1, 2048], F32, kind="ExternalOutput").ap(),
        zx_d=nc.dram_tensor("zx_d", [9344, 2048], BF16, kind="Internal").ap(),
        ynw_d=nc.dram_tensor("ynw_d", [4096, 2048], BF16, kind="Internal").ap(),
    )
    k = KB(nc, es)
    k.init_echo(nc, es)
    blk = es.enter_context(nc.Block())
    ev = emit_C(k, nc, es, io)
    k.fence_all()
    k.build(blk)
    es.close()
    return nc


def host_prep_C(inputs):
    ipw = inputs["L1_attn_in_proj_w"]
    convw = inputs["L1_attn_conv_w"]; convb = inputs["L1_attn_conv_b"]
    dtb = inputs["L1_attn_dt_bias"]; alog = inputs["L1_attn_A_log"]
    Dh = inputs["L1_attn_D"]; nw = inputs["L1_attn_norm_w"]
    wout = inputs["L1_attn_out_proj_w"]
    DI = 8192
    per_eta = []
    for eta in range(2):
        rows = np.concatenate([
            4096 * eta + np.arange(4096),
            DI + 4096 * eta + np.arange(4096),
            2 * DI + 512 * eta + np.arange(512),
            2 * DI + 1024 + 512 * eta + np.arange(512),
            2 * DI + 2048 + 64 * eta + np.arange(64),
        ])
        wip = np.zeros((D, 9344), dtype=np.float32)
        wip[:, 0:9280] = ipw[rows][:, PERM_D].T
        convsel = np.concatenate([4096 * eta + np.arange(4096),
                                  DI + 512 * eta + np.arange(512),
                                  DI + 1024 + 512 * eta + np.arange(512)])
        cw = convw[convsel].reshape(40, 128, 4).transpose(1, 0, 2).reshape(128, 160)
        cb = convb[convsel].reshape(40, 128).T
        woutT = np.ascontiguousarray(wout[PERM_D][:, 4096 * eta: 4096 * eta + 4096].T)
        wip_tiles = []
        for ocg in range(37):
            noc = 2 if ocg < 36 else 1
            for kc in range(32):
                wip_tiles.append(wip[kc * 128:(kc + 1) * 128, ocg * 256: ocg * 256 + 128 * noc])
        wout_tiles = [woutT[kc * 128:(kc + 1) * 128, mcg * 256: mcg * 256 + 256]
                      for mcg in range(16) for kc in range(32)]
        per_eta.append(dict(
            wipT=pack_tiles(wip_tiles, 256, 4),
            convw=np.ascontiguousarray(cw), convb=np.ascontiguousarray(cb),
            dtb=np.ascontiguousarray(dtb[64 * eta:64 * eta + 64].reshape(-1, 1)),
            alog=np.ascontiguousarray(alog[64 * eta:64 * eta + 64].reshape(-1, 1)),
            dexp=np.ascontiguousarray(np.repeat(Dh[64 * eta:64 * eta + 64], 64).reshape(32, 128).T).astype(np.float32),
            nw=np.ascontiguousarray(nw[4096 * eta:4096 * eta + 4096].reshape(32, 128).T),
            woutT=pack_tiles(wout_tiles, 256, 4),
        ))
    p = np.arange(128)
    oh64 = np.zeros((64, 64, 128), dtype=BF)
    for h in range(64):
        oh64[h, h, :] = 1
    oh64 = np.ascontiguousarray(oh64.transpose(1, 0, 2).reshape(64, 64 * 128))
    ohpair = np.zeros((64, 32, 128), dtype=BF)
    for hp in range(32):
        ohpair[2 * hp, hp, 0:64] = 1
        ohpair[2 * hp + 1, hp, 64:128] = 1
    ohpair = np.ascontiguousarray(ohpair.transpose(1, 0, 2).reshape(64, 32 * 128))
    ohpair2 = np.zeros((64, 32 * 128), dtype=BF)
    for hp in range(32):
        ohpair2[2 * hp, hp * 128: hp * 128 + 64] = 1
        ohpair2[2 * hp + 1, hp * 128 + 64: hp * 128 + 128] = 1
    oh64b = np.zeros((64, 64 * 128), dtype=BF)
    for h in range(64):
        oh64b[h, h * 128:(h + 1) * 128] = 1
    ones64 = np.ones((64, 128), dtype=BF)
    sidx = np.arange(128)
    m1 = (sidx[None, :] >= sidx[:, None]).astype(BF)
    mask0 = np.concatenate([m1, np.ones((128, 128), dtype=BF)], axis=1)
    shared = dict(oh64=oh64b, ohpair=ohpair2, ones64=ones64,
                  mask0=np.ascontiguousarray(mask0), mask1=np.ascontiguousarray(m1),
                  idn=np.eye(128, dtype=BF), ones_col=np.ones((128, 1), dtype=BF))
    return per_eta, shared


def emit_D(k, nc, es, io):
    partA = io["partA"]; partB = io.get("partB")
    ssqA = io["ssqA"]; ssqB = io.get("ssqB"); x2f = io["x2f"]
    wT = io["wT"]; bcol_d = io["bcol"]; wrT = io["wrT"]; aT = io["aT"]
    bmT = io["bmT"]; repmat_d = io["repmat"]
    ones41_d = io["ones41"]; ones132_d = io["ones132"]; onec_d = io["ones_col"]
    drow_d = io["drow_d"]; lnrow_d = io["lnrow_d"]
    x3pre_d = io["x3pre_d"]; x3_d = io["x3_d"]; x4pre_d = io["x4pre_d"]
    o_out = io["o_out"]

    k.set_scope('D0')
    es0 = contextlib.ExitStack()
    pAt = _sb(nc, es0, "d0_pa", [128, TOK], F32)
    pBt = _sb(nc, es0, "d0_pb", [128, TOK], F32)
    x2t = _sb(nc, es0, "d0_x2", [128, TOK], F32)
    t1 = _sb(nc, es0, "d0_t1", [128, TOK], F32)
    srow2 = _sb(nc, es0, "d0_sr", [1, 4 * 1024], F32)
    rrep = _sb(nc, es0, "d0_rr", [128, TOK], F32)
    sl_a = k.dslot(); sl_b = k.dslot(); sl_x = k.dslot()
    ev_sa = k.op("sync", lambda e: e.dma_start(out=srow2[:, 0:1024], in_=ssqA), inc=16)
    if ssqB is not None:
        ev_sb = k.op("sync", lambda e: e.dma_start(out=srow2[:, 1024:2048], in_=ssqB), inc=16)
        e1 = k.op("vector", lambda e: e.tensor_tensor(
            out=srow2[:, 2048:3072], in0=srow2[:, 0:1024], in1=srow2[:, 1024:2048],
            op=ALU.add), waits=(ev_sa, ev_sb))
    else:
        e1 = k.op("vector", lambda e: e.tensor_copy(
            out=srow2[:, 2048:3072], in_=srow2[:, 0:1024]), waits=(ev_sa,))
    e2 = k.op("vector", lambda e: e.tensor_scalar(
        out=srow2[:, 3072:4096], in0=srow2[:, 2048:3072], scalar1=1.0 / 8192.0,
        scalar2=1e-5, op0=ALU.mult, op1=ALU.add), waits=(e1,))
    e3 = k.op("scalar", lambda e: e.activation(
        out=srow2[:, 2048:3072], in_=srow2[:, 3072:4096], func=AF.Sqrt), waits=(e2,))
    e4 = k.op("vector", lambda e: e.reciprocal(
        out=srow2[:, 3072:4096], in_=srow2[:, 2048:3072]), waits=(e3,))
    e5 = k.op("gpsimd", lambda e: e.dma_start(out=drow_d, in_=srow2[:, 3072:4096]),
              waits=(e4,), inc=16)
    e6 = k.op("gpsimd", lambda e: e.dma_start(
        out=rrep[:], in_=bass.AP(drow_d.tensor, 0, [[0, 128], [1, 1024]])),
        waits=(e5,), inc=16)
    pva = pvb = pvx = pvo = pv2 = pv3 = None
    for mc in range(32):
        ev_a = k.op("sync", lambda e, mc=mc: e.dma_start(
            out=pAt[:], in_=partA[mc * 128:(mc + 1) * 128, :]), inc=16,
            waits=(pv3, pvo), slot=sl_a)
        if partB is not None:
            ev_b = k.op("sync", lambda e, mc=mc: e.dma_start(
                out=pBt[:], in_=partB[mc * 128:(mc + 1) * 128, :]), inc=16,
                waits=(pv2,), slot=sl_b)
        ev_x = k.op("sync", lambda e, mc=mc: e.dma_start(
            out=x2t[:], in_=x2f[mc * 128:(mc + 1) * 128, :]), inc=16,
            waits=(pv3,), slot=sl_x)
        if partB is not None:
            ee1 = k.op("vector", lambda e: e.tensor_tensor(
                out=t1[:], in0=pAt[:], in1=pBt[:], op=ALU.add), waits=(ev_a, ev_b, pvo))
            ee2 = k.op("vector", lambda e: e.tensor_tensor(
                out=pAt[:], in0=t1[:], in1=rrep[:], op=ALU.mult), waits=(ee1, e6))
        else:
            ee2 = k.op("vector", lambda e: e.tensor_tensor(
                out=t1[:], in0=pAt[:], in1=rrep[:], op=ALU.mult), waits=(ev_a, e6, pvo))
        if partB is not None:
            ee3 = k.op("vector", lambda e: e.scalar_tensor_tensor(
                out=t1[:], in0=x2t[:], scalar=2.0, in1=pAt[:], op0=ALU.mult, op1=ALU.add),
                waits=(ee2, ev_x))
            src_t = t1
        else:
            ee3 = k.op("vector", lambda e: e.scalar_tensor_tensor(
                out=pAt[:], in0=x2t[:], scalar=2.0, in1=t1[:], op0=ALU.mult, op1=ALU.add),
                waits=(ee2, ev_x))
            src_t = pAt
        ev_o = k.op("gpsimd", lambda e, mc=mc, src_t=src_t: e.dma_start(
            out=x3pre_d[mc * 128:(mc + 1) * 128, :], in_=src_t[:]), waits=(ee3,), inc=16)
        pva, pvx, pvo = ev_a, ev_x, ev_o
        pv2, pv3 = ee2, ee3
    k.fence_all()
    emit_ln(k, nc, es0, "d_ln1", x3pre_d, x3_d, lnrow_d, onec_d)
    k.fence_all()
    es0.close()

    k.set_scope('D1')
    es1 = contextlib.ExitStack()
    x3bf = _sb(nc, es1, "d1_x3b", [128, 32 * TOK], BF16)
    xld = _sb(nc, es1, "d1_xld", [128, TOK], F32)
    wrb = _sb(nc, es1, "d1_wrb", [128, 128], BF16)
    ab_sb = _sb(nc, es1, "d1_ab", [128, 1024], BF16)
    bm_f = _sb(nc, es1, "d1_bmf", [32, 4096], F32)
    bmb = _sb(nc, es1, "d1_bmb", [32, 4096], BF16)
    bcol = _sb(nc, es1, "d1_bc", [128, 32], F32)
    rep_sb = _sb(nc, es1, "d1_rep", [4, 32], F32)
    o41 = _sb(nc, es1, "d1_o41", [4, 1], F32)
    o132 = _sb(nc, es1, "d1_o132", [1, 32], F32)
    ex_sb = _sb(nc, es1, "d1_ex", [4, 1024], F32)
    pr_sb = _sb(nc, es1, "d1_pr", [32, 1024], F32)
    sumr = _sb(nc, es1, "d1_sum", [1, 1024], F32)
    rrow = _sb(nc, es1, "d1_rro", [1, 1024], F32)
    recip32 = _sb(nc, es1, "d1_r32", [32, 1024], F32)
    exrep = _sb(nc, es1, "d1_exr", [32, 1024], F32)
    t32 = _sb(nc, es1, "d1_t32", [32, 1024], F32)
    pg_f = _sb(nc, es1, "d1_pgf", [32, 1024], F32)
    pgb = _sb(nc, es1, "d1_pgb", [32, 1024], BF16)
    hsb = _sb(nc, es1, "d1_h", [128, TOK], F32)
    x4t = _sb(nc, es1, "d1_x4", [128, TOK], F32)
    x3r = _sb(nc, es1, "d1_x3r", [128, TOK], F32)
    pg = _ps(nc, es1, "d1_ppg", [4, 512])
    pr = _ps(nc, es1, "d1_ppr", [32, 512])
    p1 = _ps(nc, es1, "d1_pp1", [1, 512])
    prep = _ps(nc, es1, "d1_pprep", [32, 512])
    pD = [_ps(nc, es1, f"d1_pD{i}", [128, 512]) for i in range(2)]
    sl_x3 = k.dslot(); sl_xr = k.dslot()

    x3_cast = []
    pv_c = None
    for mc in range(32):
        ev_d = k.op("sync", lambda e, mc=mc: e.dma_start(
            out=xld[:], in_=x3_d[mc * 128:(mc + 1) * 128, :]), inc=16,
            waits=(pv_c,), slot=sl_x3)
        ev_c = k.op("scalar", lambda e, mc=mc: e.activation(
            out=x3bf[:, mc * TOK:(mc + 1) * TOK], in_=xld[:], func=AF.Copy),
            waits=(ev_d,))
        pv_c = ev_c
        x3_cast.append(ev_c)
    e_x3 = k.echo("scalar")
    wrf = _sb(nc, es1, "d1_wrf", [128, 128], F32)
    af_f = _sb(nc, es1, "d1_aff", [128, 1024], F32)
    ev_w1 = k.op("sync", lambda e: e.dma_start(out=wrf[:], in_=wrT), inc=16)
    ev_w2 = k.op("scalar", lambda e: e.activation(out=wrb[:], in_=wrf[:], func=AF.Copy),
                 waits=(ev_w1,))
    ev_a1 = k.op("sync", lambda e: e.dma_start(out=af_f[:], in_=aT), inc=16)
    ev_a2 = k.op("scalar", lambda e: e.activation(out=ab_sb[:], in_=af_f[:], func=AF.Copy),
                 waits=(ev_a1,))
    ev_b1 = k.op("sync", lambda e: e.dma_start(out=bm_f[:], in_=bmT), inc=16)
    ev_b2 = k.op("scalar", lambda e: e.activation(out=bmb[:], in_=bm_f[:], func=AF.Copy),
                 waits=(ev_b1,))
    ev_bc = k.op("sync", lambda e: e.dma_start(out=bcol[:], in_=bcol_d), inc=16)
    ev_rp = k.op("sync", lambda e: e.dma_start(out=rep_sb[:], in_=repmat_d), inc=16)
    ev_o41 = k.op("sync", lambda e: e.dma_start(out=o41[:], in_=ones41_d), inc=16)
    ev_o132 = k.op("sync", lambda e: e.dma_start(out=o132[:], in_=ones132_d), inc=16)
    ex_evs = []
    for tb in range(2):
        mm = None
        for mc in range(32):
            fn = lambda e, mc=mc, tb=tb: e.matmul(
                out=pg[:], lhsT=wrb[:, mc * 4:(mc + 1) * 4],
                rhs=x3bf[:, mc * TOK + tb * 512: mc * TOK + tb * 512 + 512],
                start=(mc == 0), stop=(mc == 31))
            w = [ev_w2, x3_cast[mc]] + (ex_evs[-1:] if (mc == 0 and tb == 1) else [])
            if mc == 31:
                mm = k.op("tensor", fn, waits=w)
            else:
                k.raw("tensor", fn, waits=w)
        ev_ex = k.op("scalar", lambda e, tb=tb: e.activation(
            out=ex_sb[:, tb * 512:(tb + 1) * 512], in_=pg[:], func=AF.Exp), waits=(mm,))
        ex_evs.append(ev_ex)
    pr_evs = []
    for tb in range(2):
        mm = None
        for mc in range(32):
            fn = lambda e, mc=mc, tb=tb: e.matmul(
                out=pr[:], lhsT=ab_sb[:, mc * 32:(mc + 1) * 32],
                rhs=x3bf[:, mc * TOK + tb * 512: mc * TOK + tb * 512 + 512],
                start=(mc == 0), stop=(mc == 31))
            w = [ev_a2, x3_cast[mc]] + (pr_evs[-1:] if (mc == 0 and tb == 1) else [])
            if mc == 31:
                mm = k.op("tensor", fn, waits=w)
            else:
                k.raw("tensor", fn, waits=w)
        ev_pr = k.op("vector", lambda e, tb=tb: e.tensor_copy(
            out=pr_sb[:, tb * 512:(tb + 1) * 512], in_=pr[:]), waits=(mm,))
        pr_evs.append(ev_pr)
    sum_evs = []
    for tb in range(2):
        mm = k.op("tensor", lambda e, tb=tb: e.matmul(
            out=p1[:], lhsT=o41[:], rhs=ex_sb[0:4, tb * 512:(tb + 1) * 512],
            start=True, stop=True), waits=[ev_o41, ex_evs[tb]] + sum_evs[-1:])
        ev_s = k.op("vector", lambda e, tb=tb: e.tensor_copy(
            out=sumr[:, tb * 512:(tb + 1) * 512], in_=p1[:]), waits=(mm,))
        sum_evs.append(ev_s)
    ev_rc = k.op("vector", lambda e: e.reciprocal(out=rrow[:], in_=sumr[:]),
                 waits=tuple(sum_evs))
    rep_evs = []
    for tb in range(2):
        mm1 = k.op("tensor", lambda e, tb=tb: e.matmul(
            out=prep[:], lhsT=o132[:], rhs=rrow[0:1, tb * 512:(tb + 1) * 512],
            start=True, stop=True), waits=[ev_o132, ev_rc] + rep_evs[-1:])
        ev_r1 = k.op("vector", lambda e, tb=tb: e.tensor_copy(
            out=recip32[:, tb * 512:(tb + 1) * 512], in_=prep[:]), waits=(mm1,))
        mm2 = k.op("tensor", lambda e, tb=tb: e.matmul(
            out=prep[:], lhsT=rep_sb[:], rhs=ex_sb[0:4, tb * 512:(tb + 1) * 512],
            start=True, stop=True), waits=(ev_rp, ex_evs[tb], ev_r1))
        ev_r2 = k.op("vector", lambda e, tb=tb: e.tensor_copy(
            out=exrep[:, tb * 512:(tb + 1) * 512], in_=prep[:]), waits=(mm2,))
        rep_evs.append(ev_r2)
    ev_g1 = k.op("vector", lambda e: e.tensor_tensor(
        out=t32[:], in0=pr_sb[:], in1=exrep[:], op=ALU.mult),
        waits=tuple(pr_evs + rep_evs))
    ev_g2 = k.op("vector", lambda e: e.scalar_tensor_tensor(
        out=pg_f[:], in0=t32[:], scalar=2.0, in1=recip32[:], op0=ALU.mult, op1=ALU.mult),
        waits=(ev_g1,))
    ev_g3 = k.op("scalar", lambda e: e.activation(out=pgb[:], in_=pg_f[:], func=AF.Copy),
                 waits=(ev_g2,))
    e_gd = k.echo("scalar")
    ws = WChunk(k, nc, es1, "d_w", wT, 128, 8)
    evac_prev = [e_gd, e_gd]
    pv_ev = [None, None]
    pv_res = None
    pv_xr = None
    pv_o = None
    for oc in range(32):
        mm_last = [None, None]
        for ic in range(32):
            tile, evc = ws.get()
            for tb in range(2):
                fn = lambda e, tile=tile, ic=ic, tb=tb: e.matmul(
                    out=pD[tb][:], lhsT=tile,
                    rhs=x3bf[:, ic * TOK + tb * 512: ic * TOK + tb * 512 + 512],
                    start=(ic == 0), stop=False)
                w = [evc, x3_cast[ic]] + ([evac_prev[tb]] if ic == 0 else [])
                if tb == 1:
                    mm_last[1] = k.op("tensor", fn, waits=w)
                else:
                    k.raw("tensor", fn, waits=w)
            ws.done(mm_last[1])
        lo_evs = []
        for tb in range(2):
            ev_lo = k.op("tensor", lambda e, oc=oc, tb=tb: e.matmul(
                out=pD[tb][:], lhsT=bmb[:, oc * 128:(oc + 1) * 128],
                rhs=pgb[:, tb * 512:(tb + 1) * 512], start=False, stop=True),
                waits=(ev_b2, e_gd))
            lo_evs.append(ev_lo)
        ev_xr = k.op("sync", lambda e, oc=oc: e.dma_start(
            out=x3r[:], in_=x3_d[oc * 128:(oc + 1) * 128, :]), inc=16,
            waits=(pv_res,), slot=sl_xr)
        res_evs = []
        for tb in range(2):
            e_ev = k.op("vector", lambda e, oc=oc, tb=tb: e.tensor_scalar(
                out=hsb[:, tb * 512:(tb + 1) * 512], in0=pD[tb][:],
                scalar1=bcol[:, oc:oc + 1], scalar2=None, op0=ALU.add),
                waits=[lo_evs[tb], ev_bc, pv_ev[tb]])
            evac_prev[tb] = e_ev
            e_rs = k.op("vector", lambda e, tb=tb: e.tensor_tensor(
                out=x4t[:, tb * 512:(tb + 1) * 512], in0=hsb[:, tb * 512:(tb + 1) * 512],
                in1=x3r[:, tb * 512:(tb + 1) * 512], op=ALU.add),
                waits=(e_ev, ev_xr, pv_o))
            pv_ev[tb] = e_rs
            res_evs.append(e_rs)
        pv_res = res_evs[1]
        pv_o = k.op("gpsimd", lambda e, oc=oc: e.dma_start(
            out=x4pre_d[oc * 128:(oc + 1) * 128, :], in_=x4t[:]),
            waits=tuple(res_evs), inc=16)
    k.fence_all()
    es1.close()

    k.set_scope('D2')
    es2 = contextlib.ExitStack()
    emit_ln(k, nc, es2, "d_ln2", x4pre_d, o_out, lnrow_d, onec_d)
    k.fence_all()
    es2.close()


def build_L5():
    nc = bass.Bass("TRN2", target_bir_lowering=False)
    es = contextlib.ExitStack()
    io = dict(
        partA=nc.dram_tensor("partA", [D, TOK], F32, kind="ExternalInput").ap(),
        partB=nc.dram_tensor("partB", [D, TOK], F32, kind="ExternalInput").ap(),
        ssqA=nc.dram_tensor("ssqA", [1, TOK], F32, kind="ExternalInput").ap(),
        ssqB=nc.dram_tensor("ssqB", [1, TOK], F32, kind="ExternalInput").ap(),
        x2f=nc.dram_tensor("x2f", [D, TOK], F32, kind="ExternalInput").ap(),
        wT=nc.dram_tensor("wT", [128 * 128, 1024], F32, kind="ExternalInput").ap(),
        bcol=nc.dram_tensor("bcol", [128, 32], F32, kind="ExternalInput").ap(),
        wrT=nc.dram_tensor("wrT", [128, 128], F32, kind="ExternalInput").ap(),
        aT=nc.dram_tensor("aT", [128, 1024], F32, kind="ExternalInput").ap(),
        bmT=nc.dram_tensor("bmT", [32, 4096], F32, kind="ExternalInput").ap(),
        repmat=nc.dram_tensor("repmat", [4, 32], F32, kind="ExternalInput").ap(),
        ones41=nc.dram_tensor("ones41", [4, 1], F32, kind="ExternalInput").ap(),
        ones132=nc.dram_tensor("ones132", [1, 32], F32, kind="ExternalInput").ap(),
        ones_col=nc.dram_tensor("ones_col", [128, 1], BF16, kind="ExternalInput").ap(),
        o_out=nc.dram_tensor("o_out", [D, TOK], F32, kind="ExternalOutput").ap(),
        drow_d=nc.dram_tensor("drow_d", [1, 1024], F32, kind="Internal").ap(),
        lnrow_d=nc.dram_tensor("lnrow_d", [2, 1024], F32, kind="Internal").ap(),
        x3pre_d=nc.dram_tensor("x3pre_d", [D, TOK], F32, kind="Internal").ap(),
        x3_d=nc.dram_tensor("x3_d", [D, TOK], F32, kind="Internal").ap(),
        x4pre_d=nc.dram_tensor("x4pre_d", [D, TOK], F32, kind="Internal").ap(),
    )
    k = KB(nc, es)
    k.init_echo(nc, es)
    blk = es.enter_context(nc.Block())
    emit_D(k, nc, es, io)
    k.fence_all()
    k.build(blk)
    es.close()
    return nc


def host_prep_D(inputs):
    W = inputs["L1_ffn_W"]; b = inputs["L1_ffn_b"]
    A = inputs["L1_ffn_A"]; Bm = inputs["L1_ffn_Bm"]; Wr = inputs["L1_ffn_Wr"]
    wT = np.ascontiguousarray(W[PERM_D][:, PERM_D].T)
    wT = pack_tiles([wT[ic * 128:(ic + 1) * 128, oc * 128:(oc + 1) * 128]
                     for oc in range(32) for ic in range(32)], 128, 8)
    bcol = np.ascontiguousarray(b[PERM_D].reshape(32, 128).T)
    wrT = Wr[:, PERM_D].T
    wrT = np.ascontiguousarray(wrT.reshape(32, 128, 4).transpose(1, 0, 2).reshape(128, 128))
    aT = A.reshape(32, 4096)[:, PERM_D].T
    aT = np.ascontiguousarray(aT.reshape(32, 128, 32).transpose(1, 0, 2).reshape(128, 1024))
    bmT = np.ascontiguousarray(Bm.transpose(0, 2, 1).reshape(32, 4096)[:, PERM_D])
    repmat = np.zeros((4, 32), dtype=np.float32)
    for e_ in range(4):
        repmat[e_, e_ * 8:(e_ + 1) * 8] = 1.0
    return dict(wT=wT, bcol=bcol, wrT=wrT, aT=aT, bmT=bmT, repmat=repmat,
                ones41=np.ones((4, 1), dtype=np.float32),
                ones132=np.ones((1, 32), dtype=np.float32),
                ones_col=np.ones((128, 1), dtype=BF))


CC_GROUPS = [[0, 1], [2, 3], [4, 5], [6, 7]]


def build_full():
    nc = bass.Bass("TRN2", target_bir_lowering=False)
    es = contextlib.ExitStack()

    def ein(name, shape, dt=F32):
        return nc.dram_tensor(name, shape, dt, kind="ExternalInput").ap()

    def intl(name, shape, dt=F32):
        return nc.dram_tensor(name, shape, dt, kind="Internal").ap()

    x0_fm = ein("x0_fm", [D, TOK])
    wqT = ein("wqT", [2 * 128 * 128, 1024]); wkT = ein("wkT", [D, 1024]); wvT = ein("wvT", [D, 1024])
    bq = ein("bq", [D, 1]); bk = ein("bk", [1, 1024]); bv = ein("bv", [1, 1024])
    bo = ein("bo", [D, 1])
    ones_bf = ein("ones_bf", [1, 128], BF16)
    ones_col = ein("ones_col", [128, 1], BF16)
    qw_gu8 = ein("qw_gu8", [86 * 128, 4096], U8)
    sc_gu = ein("sc_gu", [32, N_GU])
    qz_gu8 = ein("qz_gu8", [32, N_IN], U8)
    qw_dn8 = ein("qw_dn8", [4 * 86 * 128, 512], U8)
    sc_dn = ein("sc_dn", [86, D])
    qz_dn8 = ein("qz_dn8", [86, 2048], U8)
    ind2 = ein("ind2", [2, 128], BF16)
    ind32T = ein("ind32T", [32, 16 * 128], BF16)
    wipT = ein("wipT", [296 * 128, 1024])
    convw = ein("convw", [128, 160]); convb = ein("convb", [128, 40])
    dtb = ein("dtb", [64, 1]); alog = ein("alog", [64, 1])
    dexp = ein("dexp", [128, 32]); nw = ein("nw", [128, 32])
    woutT = ein("woutT", [128 * 128, 1024])
    oh64 = ein("oh64", [64, 64 * 128], BF16)
    ohpair = ein("ohpair", [64, 32 * 128], BF16)
    ones64 = ein("ones64", [64, 128], BF16)
    mask0 = ein("mask0", [128, 256], BF16); mask1 = ein("mask1", [128, 128], BF16)
    idn = ein("idn", [128, 128], BF16)
    wT = ein("wT", [128 * 128, 1024]); bcol = ein("bcol", [128, 32])
    wrT = ein("wrT", [128, 128]); aT = ein("aT", [128, 1024])
    bmT = ein("bmT", [32, 4096]); repmat = ein("repmat", [4, 32])
    ones41 = ein("ones41", [4, 1]); ones132 = ein("ones132", [1, 32])
    o_out = nc.dram_tensor("o_out", [D, TOK], F32, kind="ExternalOutput").ap()
    kv_part = intl("kv_part", [128, 1032]); kv_red = intl("kv_red", [128, 1032])
    x1_d = intl("x1_d", [D, TOK]); x1pre_d = intl("x1pre_d", [D, TOK])
    lnrowA = intl("lnrowA", [2, 1024])
    h_d = intl("h_d", [N_IN, TOK], BF16)
    sdn_d = intl("sdn_d", [N_IN, D], BF16)
    zdnx_d = intl("zdnx_d", [N_IN, D], BF16)
    zdn_d = intl("zdn_d", [86, D], BF16)
    x2pre_d = intl("x2pre_d", [D, TOK]); lnrowB = intl("lnrowB", [2, 1024])
    x2own_d = intl("x2own_d", [D, TOK])
    x2g_d = intl("x2g_d", [2 * D, TOK])
    zx_d = intl("zx_d", [9344, 2048], BF16)
    ynw_d = intl("ynw_d", [4096, 2048], BF16)
    part2_d = intl("part2_d", [2 * D, TOK])
    ssq2_d = intl("ssq2_d", [2, TOK])
    partr_d = intl("partr_d", [D, TOK])
    ssqr_d = intl("ssqr_d", [1, TOK])
    drow_d = intl("drow_d", [1, 1024]); lnrowD = intl("lnrowD", [2, 1024])
    x3pre_d = intl("x3pre_d", [D, TOK]); x3_d = intl("x3_d", [D, TOK])
    x4pre_d = intl("x4pre_d", [D, TOK])

    k = KB(nc, es)
    k.init_echo(nc, es)
    blk = es.enter_context(nc.Block())

    k.set_scope("A1")
    esA1 = contextlib.ExitStack()
    emit_A1(k, nc, esA1, dict(x0_fm=x0_fm, wkT=wkT, wvT=wvT, bk=bk, bv=bv,
                              ones_bf=ones_bf, ones_col=ones_col, o_kv=kv_part))
    k.fence_all()
    k.reclaim_slots()
    esA1.close()
    k.set_scope("cc_kv")
    k.op("gpsimd", lambda e: e.collective_compute(
        "AllReduce", ALU.add, CC_GROUPS, ins=[kv_part], outs=[kv_red]), inc=1)
    k.fence_all()
    k.reclaim_slots()

    k.set_scope("A2")
    esA2 = contextlib.ExitStack()
    emit_A2(k, nc, esA2, dict(x0_fm=x0_fm, wqT=wqT, bq=bq, bo=bo,
                              kvA=kv_red, ones_bf=ones_bf, ones_col=ones_col,
                              o_x1=x1_d, x1pre_d=x1pre_d, lnrow_d=lnrowA))
    k.fence_all()
    k.reclaim_slots()
    esA2.close()

    k.set_scope("Bff")
    esB = contextlib.ExitStack()
    emit_B(k, nc, esB, dict(x1=x1_d, qw_gu8=qw_gu8, sc_gu=sc_gu, qz_gu8=qz_gu8,
                            qw_dn8=qw_dn8, sc_dn=sc_dn, qz_dn8=qz_dn8, ind2=ind2,
                            ind32T=ind32T, ones_col=ones_col, o_x2=x2own_d,
                            h_d=h_d, sdn_d=sdn_d, zdnx_d=zdnx_d, zdn_d=zdn_d,
                            x2pre_d=x2pre_d, lnrow_d=lnrowB))
    k.fence_all()
    k.reclaim_slots()
    esB.close()
    k.set_scope("cc_x2")
    k.op("gpsimd", lambda e: e.collective_compute(
        "AllGather", ALU.bypass, CC_GROUPS, ins=[x2own_d], outs=[x2g_d]), inc=1)
    k.fence_all()
    k.reclaim_slots()

    k.set_scope("Cmm")
    esC = contextlib.ExitStack()
    emit_C(k, nc, esC, dict(x2a=x2g_d[0:D, :], x2b=x2g_d[D:2 * D, :], wipT=wipT,
                            convw=convw, convb=convb, dtb=dtb, alog=alog,
                            dexp=dexp, nw=nw, woutT=woutT, oh64=oh64,
                            ohpair=ohpair, ones64=ones64, mask0=mask0,
                            mask1=mask1, idn=idn, ones_col=ones_col,
                            zx_d=zx_d, ynw_d=ynw_d, o_part=part2_d,
                            o_ssq=ssq2_d, _part2=True))
    k.fence_all()
    k.reclaim_slots()
    esC.close()
    k.set_scope("cc_part")
    k.op("gpsimd", lambda e: e.collective_compute(
        "ReduceScatter", ALU.add, CC_GROUPS, ins=[part2_d], outs=[partr_d]), inc=1)
    k.op("gpsimd", lambda e: e.collective_compute(
        "ReduceScatter", ALU.add, CC_GROUPS, ins=[ssq2_d], outs=[ssqr_d]), inc=1)
    k.fence_all()
    k.reclaim_slots()

    k.set_scope("Dff")
    esD = contextlib.ExitStack()
    emit_D(k, nc, esD, dict(partA=partr_d, ssqA=ssqr_d, x2f=x2own_d, wT=wT,
                            bcol=bcol, wrT=wrT, aT=aT, bmT=bmT, repmat=repmat,
                            ones41=ones41, ones132=ones132, ones_col=ones_col,
                            drow_d=drow_d, lnrow_d=lnrowD, x3pre_d=x3pre_d,
                            x3_d=x3_d, x4pre_d=x4pre_d, o_out=o_out))
    k.fence_all()
    k.reclaim_slots()
    esD.close()
    k.build(blk)
    es.close()
    return nc


def kernel_fused(**inputs):
    shared, x0_fms = host_prep_A(inputs)
    shb = host_prep_B(inputs)
    per_eta, shc = host_prep_C(inputs)
    shd = host_prep_D(inputs)
    nc = build_full()
    maps = []
    for c in range(NCORE):
        beta, eta = c // 2, c % 2
        m = dict(x0_fm=x0_fms[c], wqT=shared["wqT"], wkT=shared["wkT"],
                 wvT=shared["wvT"], bq=shared["bq"],
                 bk=shared["bk"], bv=shared["bv"], bo=shared["bo"],
                 ones_bf=shared["ones_bf"], ones_col=shared["ones_col"])
        m.update(shb); m.pop("ones_col", None); m["ones_col"] = shared["ones_col"]
        m.update(per_eta[eta])
        m.update({kk: vv for kk, vv in shc.items() if kk != "ones_col"})
        m.update({kk: vv for kk, vv in shd.items() if kk != "ones_col"})
        maps.append(m)
    res = run_bass_kernel_spmd(nc, maps, core_ids=list(range(NCORE)))
    out = np.empty((B, T, D), dtype=np.float32)
    for c in range(NCORE):
        beta, eta = c // 2, c % 2
        o = np.asarray(res.results[c]["o_out"], dtype=np.float32)
        out[beta, eta * TOK:(eta + 1) * TOK, :] = o[INV_PERM_D].T
    return out


# version 07 (final)
# speedup vs baseline: 1.2651x; 1.0227x over previous
import contextlib
import numpy as np
import ml_dtypes

import concourse.bass as bass
import concourse.mybir as mybir
from concourse.bass_utils import run_bass_kernel_spmd

F32 = mybir.dt.float32
BF16 = mybir.dt.bfloat16
I32 = mybir.dt.int32
U8 = mybir.dt.uint8
AF = mybir.ActivationFunctionType
ALU = mybir.AluOpType
BF = ml_dtypes.bfloat16

B, T, D = 4, 2048, 4096
TOK = 1024
NCORE = 8

PERM_D = np.concatenate([np.arange(0, D, 2), np.arange(1, D, 2)])
INV_PERM_D = np.argsort(PERM_D)
N_GU = 22016
PERM_GU = np.concatenate([np.arange(0, N_GU, 2), np.arange(1, N_GU, 2)])
N_IN = 11008
PERM_H = np.concatenate([np.arange(0, N_IN, 2), np.arange(1, N_IN, 2)])

ENGINES = ("sync", "vector", "scalar", "tensor", "gpsimd")


class KB:
    NSEM = 90
    CAP = 50000

    def __init__(self, nc, es):
        self.nc = nc
        self.es = es
        self.steps = []
        self.h = [es.enter_context(nc.semaphore(f"sm{i}")) for i in range(self.NSEM)]
        self.cnt = [0] * self.NSEM
        self.free = list(range(self.NSEM))
        self.slot_sems = []
        self.eng_sem = {}
        self.hwm = {}
        self.last = {e: None for e in ENGINES}
        self.cur_scope = None

    def _cur(self, eng, amt):
        key = (eng, "dma" if amt >= 16 else "cmp")
        si = self.eng_sem.get(key)
        if si is None or self.cnt[si] + amt > self.CAP:
            si = self.free.pop(0)
            self.eng_sem[key] = si
        return si

    def wait(self, eng, evs):
        if evs is None:
            return
        if isinstance(evs, tuple) and len(evs) == 2 and isinstance(evs[0], int):
            evs = [evs]
        for ev in evs:
            if ev is None:
                continue
            si, val = ev
            if si == "SLOT":
                si = val
                val = self.cnt[si]
            if val == -1:
                val = self.cnt[si]
            if self.hwm.get((eng, si), 0) < val:
                hnd = self.h[si]
                self.steps.append((eng, lambda e, hnd=hnd, val=val: e.wait_ge(hnd, val), self.cur_scope))
                self.hwm[(eng, si)] = val

    def dslot(self):
        for idx, si in enumerate(self.free):
            if self.cnt[si] < 30000:
                self.free.pop(idx)
                self.slot_sems.append(si)
                return ("SLOT", si)
        raise RuntimeError("out of slot semaphores")

    def reclaim_slots(self):
        self.free = self.free + self.slot_sems
        self.slot_sems = []

    def op(self, eng, fn, waits=(), inc=1, slot=None):
        self.wait(eng, waits)
        if slot is not None:
            si = slot[1]
        else:
            si = self._cur(eng, inc)
        self.cnt[si] += inc
        val = self.cnt[si]
        hnd = self.h[si]
        self.steps.append((eng, lambda e, fn=fn, hnd=hnd, inc=inc: fn(e).then_inc(hnd, inc), self.cur_scope))
        if slot is None and inc >= 16:
            ev = (si, -1)
        else:
            ev = (si, val)
        self.last[eng] = (si, val)
        return ev

    def raw(self, eng, fn, waits=()):
        self.wait(eng, waits)
        self.steps.append((eng, fn, self.cur_scope))

    def set_scope(self, name):
        self.cur_scope = name

    def init_echo(self, nc, es):
        self._echo_buf = _sb(nc, es, "kb_echo", [1, 8], F32)

    def echo(self, eng):
        b = self._echo_buf
        if eng == "vector":
            return self.op("vector", lambda e: e.tensor_copy(out=b[:, 4:8], in_=b[:, 0:4]))
        if eng == "scalar":
            return self.op("scalar", lambda e: e.activation(out=b[:, 4:8], in_=b[:, 0:4], func=AF.Copy))
        if eng == "gpsimd":
            return self.op("gpsimd", lambda e: e.tensor_copy(out=b[:, 4:8], in_=b[:, 0:4]))
        return self.last[eng]

    def fence_all(self):
        for e in ("vector", "scalar", "gpsimd"):
            if self.last[e] is not None:
                self.echo(e)
        evs = [(si, self.cnt[si]) for si in range(self.NSEM) if self.cnt[si] > 0]
        for e in ENGINES:
            self.wait(e, evs)

    def build(self, block):
        i = 0
        n = len(self.steps)
        while i < n:
            eng = self.steps[i][0]
            j = i
            fns = []
            while j < n and self.steps[j][0] == eng:
                fns.append(self.steps[j][1:])
                j += 1
            def section(e, fns=fns):
                for fn, scope in fns:
                    if scope is not None:
                        with self.nc.named_scope(scope):
                            fn(e)
                    else:
                        fn(e)
            getattr(block, eng)(section)
            i = j


def _sb(nc, es, name, shape, dtype):
    return es.enter_context(nc.sbuf_tensor(name, shape, dtype))


def _ps(nc, es, name, shape, dtype=F32):
    return es.enter_context(nc.psum_tensor(name, shape, dtype))


class WStream:

    def __init__(self, k, nc, es, name, rows, cols, dq="sync", cast_eng="scalar"):
        self.k = k
        self.wf = [_sb(nc, es, f"{name}_f{i}", [rows, cols], F32) for i in range(2)]
        self.wb = [_sb(nc, es, f"{name}_b{i}", [rows, cols], BF16) for i in range(2)]
        self.rows, self.cols = rows, cols
        self.dq, self.cast_eng = dq, cast_eng
        self.sel = 0
        self.cast_ev = [None, None]
        self.use_ev = [None, None]
        self.slots = [k.dslot(), k.dslot()]

    def get(self, dram_ap, rows=None, cols=None):
        r = rows or self.rows
        c = cols or self.cols
        s = self.sel
        k = self.k
        ev_d = k.op(self.dq, lambda e, s=s, r=r, c=c, a=dram_ap: e.dma_start(
            out=self.wf[s][:r, :c], in_=a), inc=16,
            waits=(self.cast_ev[s],), slot=self.slots[s])
        ev_c = k.op(self.cast_eng, lambda e, s=s, r=r, c=c: e.activation(
            out=self.wb[s][:r, :c], in_=self.wf[s][:r, :c], func=AF.Copy),
            waits=(ev_d, self.use_ev[s]))
        self.cast_ev[s] = ev_c
        self.cur = s
        self.sel = 1 - s
        return self.wb[s], ev_c

    def done(self, ev):
        self.use_ev[self.cur] = ev


class WChunk:

    def __init__(self, k, nc, es, name, dram, tc, chunk, nbuf=3):
        self.k = k
        self.dram = dram
        self.tc, self.chunk, self.nbuf = tc, chunk, nbuf
        w = chunk * tc
        self.wf = [_sb(nc, es, f"{name}_f{i}", [128, w], F32) for i in range(nbuf)]
        self.wb = [_sb(nc, es, f"{name}_b{i}", [128, w], BF16) for i in range(nbuf)]
        self.slots = [k.dslot() for _ in range(nbuf)]
        self.cast_ev = [None] * nbuf
        self.free_ev = [None] * nbuf
        self.idx = 0

    def _load(self, ci):
        b = ci % self.nbuf
        k = self.k
        ev_d = k.op("sync", lambda e, ci=ci, b=b: e.dma_start(
            out=self.wf[b][:], in_=self.dram[ci * 128:(ci + 1) * 128, :]), inc=16,
            waits=(self.cast_ev[b],), slot=self.slots[b])
        self.cast_ev[b] = k.op("scalar", lambda e, b=b: e.activation(
            out=self.wb[b][:], in_=self.wf[b][:], func=AF.Copy),
            waits=(ev_d, self.free_ev[b]))
        self.loaded = ci

    def get(self):
        i = self.idx
        self.idx += 1
        ci, ti = divmod(i, self.chunk)
        b = ci % self.nbuf
        ntot = self.dram.ap[0][1] // 128
        if i == 0:
            for c2 in range(min(self.nbuf, ntot)):
                self._load(c2)
        elif ti == 0 and self.loaded < min(ci + self.nbuf - 1, ntot - 1):
            self._load(ci + self.nbuf - 1)
        self.cur = b
        self.last_tile = (ti == self.chunk - 1)
        return self.wb[b][:, ti * self.tc:(ti + 1) * self.tc], self.cast_ev[b]

    def done(self, ev):
        if self.last_tile:
            self.free_ev[self.cur] = ev


def pack_tiles(tiles, tc, chunk):
    n = len(tiles)
    nch = -(-n // chunk)
    out = np.zeros((nch * 128, chunk * tc), dtype=np.float32)
    for i, t in enumerate(tiles):
        ci, ti = divmod(i, chunk)
        out[ci * 128:(ci + 1) * 128, ti * tc: ti * tc + t.shape[1]] = t
    return np.ascontiguousarray(out)


def emit_phi_evac(k, psum_ap, out_ap, tmp1, tmpe, tmp2, bias_col, mm_ev, extra=()):
    if bias_col is not None:
        ev1 = k.op("vector", lambda e: e.tensor_scalar(
            out=tmp1, in0=psum_ap, scalar1=bias_col, scalar2=0.0,
            op0=ALU.add, op1=ALU.min), waits=(mm_ev,) + tuple(extra))
        ev3 = k.op("vector", lambda e: e.tensor_scalar(
            out=tmp2, in0=psum_ap, scalar1=bias_col, scalar2=0.0,
            op0=ALU.add, op1=ALU.max), waits=(mm_ev,) + tuple(extra))
    else:
        ev1 = k.op("vector", lambda e: e.tensor_scalar(
            out=tmp1, in0=psum_ap, scalar1=0.0, scalar2=None, op0=ALU.min),
            waits=(mm_ev,) + tuple(extra))
        ev3 = k.op("vector", lambda e: e.tensor_scalar(
            out=tmp2, in0=psum_ap, scalar1=0.0, scalar2=None, op0=ALU.max),
            waits=(mm_ev,) + tuple(extra))
    ev2 = k.op("scalar", lambda e: e.activation(out=tmpe, in_=tmp1, func=AF.Exp),
               waits=(ev1,))
    ev4 = k.op("vector", lambda e: e.tensor_tensor(
        out=out_ap, in0=tmp2, in1=tmpe, op=ALU.add), waits=(ev2, ev3))
    return ev4


def emit_A1(k, nc, es, io):
    x0_fm, wkT, wvT, bkr, bvr, ones_bf, ones_col, o_kv = (
        io["x0_fm"], io["wkT"], io["wvT"], io["bk"], io["bv"],
        io["ones_bf"], io["ones_col"], io["o_kv"])

    x0bf = _sb(nc, es, "a1_x0bf", [128, 32 * TOK], BF16)
    xf = _sb(nc, es, "a1_xf", [128, TOK], F32)
    sl_x0 = k.dslot()
    ws = WStream(k, nc, es, "a1_w", 128, 512)
    brow_f = _sb(nc, es, "a1_bf", [1, 1024], F32)
    brow_b = _sb(nc, es, "a1_bb", [1, 1024], BF16)
    s_ones = _sb(nc, es, "a1_ones", [1, 128], BF16)
    s_onec = _sb(nc, es, "a1_onec", [128, 1], BF16)
    kT = _sb(nc, es, "a1_kT", [128, 8 * 1024], BF16)
    vT = _sb(nc, es, "a1_vT", [128, 8 * 1024], BF16)
    tmp1 = _sb(nc, es, "a1_t1", [128, 512], F32)
    tmp2 = _sb(nc, es, "a1_t2", [128, 512], F32)
    tmpe = _sb(nc, es, "a1_te", [128, 512], F32)
    okv = _sb(nc, es, "a1_okv", [128, 1032], F32)
    psum = [_ps(nc, es, f"a1_p{i}", [128, 512]) for i in range(8)]

    ev_ones = k.op("sync", lambda e: e.dma_start(out=s_ones[:], in_=ones_bf), inc=16)
    ev_onec = k.op("sync", lambda e: e.dma_start(out=s_onec[:], in_=ones_col), inc=16)

    x_evs = []
    prev_cast = None
    for kc in range(32):
        ev_d = k.op("sync", lambda e, kc=kc: e.dma_start(
            out=xf[:], in_=x0_fm[kc * 128:(kc + 1) * 128, :]), inc=16,
            waits=(prev_cast,), slot=sl_x0)
        ev_c = k.op("scalar", lambda e, kc=kc: e.activation(
            out=x0bf[:, kc * TOK:(kc + 1) * TOK], in_=xf[:], func=AF.Copy),
            waits=(ev_d,))
        prev_cast = ev_c
        x_evs.append(ev_c)

    kv_out = {"k": kT, "v": vT}
    w_dram = {"k": wkT, "v": wvT}
    b_dram = {"k": bkr, "v": bvr}
    phi_evs = []
    bank_free = [None] * 8
    for src in ("k", "v"):
        ev_b = k.op("sync", lambda e, src=src: e.dma_start(out=brow_f[:], in_=b_dram[src]), inc=16,
                    waits=(k.last["vector"],))
        ev_bb = k.op("vector", lambda e: e.tensor_copy(out=brow_b[:], in_=brow_f[:]), waits=(ev_b,))
        for ng in range(2):
            for kc in range(32):
                wtile, ev_w = ws.get(w_dram[src][kc * 128:(kc + 1) * 128, ng * 512:(ng + 1) * 512])
                mm_ev = None
                for tc in range(8):
                    fn = lambda e, wtile=wtile, kc=kc, tc=tc: e.matmul(
                        out=psum[tc][:],
                        lhsT=x0bf[:, kc * TOK + tc * 128: kc * TOK + tc * 128 + 128],
                        rhs=wtile[:, 0:512], start=(kc == 0), stop=False)
                    w = [ev_w, x_evs[kc]]
                    if kc == 0:
                        w += [bank_free[tc]]
                    if tc == 7:
                        mm_ev = k.op("tensor", fn, waits=w)
                    else:
                        k.raw("tensor", fn, waits=w)
                ws.done(mm_ev)
            for tc in range(8):
                ev_mm = k.op("tensor", lambda e, tc=tc, ng=ng: e.matmul(
                    out=psum[tc][:], lhsT=s_ones[:],
                    rhs=brow_b[:, ng * 512:(ng + 1) * 512],
                    start=False, stop=True), waits=(ev_bb, ev_ones))
                dst = kv_out[src][:, tc * 1024 + ng * 512: tc * 1024 + ng * 512 + 512]
                if src == "k":
                    ev4 = emit_phi_evac(
                        k, psum[tc][:], dst,
                        tmp1[:], tmpe[:], tmp2[:], None, ev_mm)
                else:
                    ev4 = k.op("vector", lambda e, tc=tc, dst=dst: e.tensor_copy(
                        out=dst, in_=psum[tc][:]), waits=(ev_mm,))
                phi_evs.append(ev4)
                bank_free[tc] = k.echo("vector")

    kv_done = []
    for h in range(8):
        mm_ev = None
        for tc in range(8):
            fn = lambda e, h=h, tc=tc: e.matmul(
                out=psum[h // 4][:, (h % 4) * 128:(h % 4) * 128 + 128],
                lhsT=kT[:, tc * 1024 + h * 128: tc * 1024 + h * 128 + 128],
                rhs=vT[:, tc * 1024 + h * 128: tc * 1024 + h * 128 + 128],
                start=(tc == 0), stop=(tc == 7))
            w = list(phi_evs) + ([bank_free[h // 4]] if tc == 0 else [])
            if tc == 7:
                mm_ev = k.op("tensor", fn, waits=w)
            else:
                k.raw("tensor", fn, waits=w)
        kv_done.append(mm_ev)
    ks_done = []
    for h in range(8):
        mm_ev = None
        for tc in range(8):
            fn = lambda e, h=h, tc=tc: e.matmul(
                out=psum[2][:, h:h + 1],
                lhsT=kT[:, tc * 1024 + h * 128: tc * 1024 + h * 128 + 128],
                rhs=s_onec[:],
                start=(tc == 0), stop=(tc == 7))
            w = [bank_free[2], ev_onec] if (h == 0 and tc == 0) else ()
            if tc == 7:
                mm_ev = k.op("tensor", fn, waits=w)
            else:
                k.raw("tensor", fn, waits=w)
        ks_done.append(mm_ev)

    ev_e1 = k.op("vector", lambda e: e.tensor_copy(out=okv[:, 0:512], in_=psum[0][:]), waits=kv_done)
    ev_e2 = k.op("vector", lambda e: e.tensor_copy(out=okv[:, 512:1024], in_=psum[1][:]), waits=kv_done)
    ev_e3 = k.op("vector", lambda e: e.tensor_copy(out=okv[:, 1024:1032], in_=psum[2][:, 0:8]), waits=ks_done)
    ev_o = k.op("gpsimd", lambda e: e.dma_start(out=o_kv, in_=okv[:]),
                waits=(ev_e1, ev_e2, ev_e3), inc=16)
    return ev_o


def emit_A2(k, nc, es, io):
    x0_fm, wqT, woT, bq, bo, kvA, kvB, ones_bf, ones_col, o_x1, x1pre_d = (
        io["x0_fm"], io["wqT"], io.get("woT"), io["bq"], io["bo"], io["kvA"],
        io.get("kvB"), io["ones_bf"], io["ones_col"], io["o_x1"], io["x1pre_d"])
    lnrow_d = io["lnrow_d"]

    x0bf = _sb(nc, es, "a2_x0bf", [128, 32 * TOK], BF16)
    xf = _sb(nc, es, "a2_xf", [128, TOK], F32)
    sl_x0 = k.dslot(); sl_bq = k.dslot(); sl_bo = k.dslot(); sl_xr = k.dslot()
    kva = _sb(nc, es, "a2_kva", [128, 1032], F32)
    kvb = _sb(nc, es, "a2_kvb", [128, 1032], F32)
    kvf = _sb(nc, es, "a2_kvf", [128, 1032], F32)
    kvbf = _sb(nc, es, "a2_kvbf", [128, 1024], BF16)
    rk = _sb(nc, es, "a2_rk", [128, 8], F32)
    wsq = WChunk(k, nc, es, "a2_wq", wqT, 128, 8, nbuf=2)
    wso = wsq
    qb = _sb(nc, es, "a2_qb", [128, TOK], BF16)
    attnb = _sb(nc, es, "a2_attnb", [128, 32 * TOK], BF16)
    bqs = _sb(nc, es, "a2_bqs", [128, 1], F32)
    bos = _sb(nc, es, "a2_bos", [128, 1], F32)
    tmp1 = _sb(nc, es, "a2_t1", [128, 512], F32)
    tmp2 = _sb(nc, es, "a2_t2", [128, 512], F32)
    tmpe = _sb(nc, es, "a2_te", [128, 512], F32)
    x1t = _sb(nc, es, "a2_x1t", [128, 512], F32)
    x1bb = _sb(nc, es, "a2_x1b", [128, 512], BF16)
    xsq = _sb(nc, es, "a2_xsq", [128, 512], BF16)
    s_ones = _sb(nc, es, "a2_ones", [1, 128], BF16)
    s_onec = _sb(nc, es, "a2_onec", [128, 1], BF16)
    srow = _sb(nc, es, "a2_srow", [1, 6 * 1024], F32)
    arep = _sb(nc, es, "a2_arep", [128, TOK], F32)
    brep = _sb(nc, es, "a2_brep", [128, TOK], F32)
    psum = [_ps(nc, es, f"a2_p{i}", [128, 512]) for i in range(4)]
    pstat = [_ps(nc, es, f"a2_st{j}", [1, 512]) for j in range(4)]

    ev_ones = k.op("sync", lambda e: e.dma_start(out=s_ones[:], in_=ones_bf), inc=16)
    ev_onec = k.op("sync", lambda e: e.dma_start(out=s_onec[:], in_=ones_col), inc=16)

    ev_a = k.op("sync", lambda e: e.dma_start(out=kva[:], in_=kvA), inc=16)
    if kvB is not None:
        ev_b = k.op("sync", lambda e: e.dma_start(out=kvb[:], in_=kvB), inc=16)
        ev_s = k.op("vector", lambda e: e.tensor_tensor(out=kvf[:], in0=kva[:], in1=kvb[:], op=ALU.add),
                    waits=(ev_a, ev_b))
    else:
        ev_s = k.op("vector", lambda e: e.tensor_copy(out=kvf[:], in_=kva[:]), waits=(ev_a,))
    ev_kb = k.op("scalar", lambda e: e.activation(out=kvbf[:], in_=kvf[:, 0:1024], func=AF.Copy),
                 waits=(ev_s,))
    ev_ke = k.op("vector", lambda e: e.tensor_scalar(
        out=kva[:, 0:8], in0=kvf[:, 1024:1032], scalar1=1e-10, scalar2=None, op0=ALU.add),
        waits=(ev_s,))
    ev_rk = k.op("vector", lambda e: e.reciprocal(out=rk[:], in_=kva[:, 0:8]), waits=(ev_ke,))

    xf2 = _sb(nc, es, "a2_xf2", [128, TOK], F32)
    sl_x0b = k.dslot()
    x_evs = []
    prev_cast = [None, None]
    for kc in range(32):
        li = kc % 2
        buf = xf if li == 0 else xf2
        ev_d = k.op("sync" if li == 0 else "gpsimd", lambda e, kc=kc, buf=buf: e.dma_start(
            out=buf[:], in_=x0_fm[kc * 128:(kc + 1) * 128, :]), inc=16,
            waits=(prev_cast[li],), slot=(sl_x0 if li == 0 else sl_x0b))
        ev_c = k.op("scalar", lambda e, kc=kc, buf=buf: e.activation(
            out=x0bf[:, kc * TOK:(kc + 1) * TOK], in_=buf[:], func=AF.Copy),
            waits=(ev_d,))
        prev_cast[li] = ev_c
        x_evs.append(ev_c)

    k.set_scope('A2_q')
    attn_evs = []
    prev_bq = None
    prev_phi = [None, None]
    bank_free = [None] * 4
    for c in range(32):
        ev_bqc = k.op("sync", lambda e, c=c: e.dma_start(
            out=bqs[:], in_=bq[c * 128:(c + 1) * 128, :]), inc=16, waits=(prev_bq,), slot=sl_bq)
        q_done = None
        for kc in range(32):
            wtile, ev_w = wsq.get()
            mm_ev = None
            for tg in range(2):
                fn = lambda e, wtile=wtile, kc=kc, tg=tg: e.matmul(
                    out=psum[tg][:], lhsT=wtile,
                    rhs=x0bf[:, kc * TOK + tg * 512: kc * TOK + tg * 512 + 512],
                    start=(kc == 0), stop=(kc == 31))
                w = [ev_w, x_evs[kc]]
                if kc == 0:
                    w += [bank_free[tg]]
                if tg == 1:
                    mm_ev = k.op("tensor", fn, waits=w)
                else:
                    k.raw("tensor", fn, waits=w)
            wsq.done(mm_ev)
            if kc == 31:
                q_done = mm_ev
        for tg in range(2):
            ev4 = emit_phi_evac(
                k, psum[tg][:], qb[:, tg * 512:(tg + 1) * 512],
                tmp1[:], tmpe[:], tmp2[:], bqs[:, 0:1], q_done,
                extra=(ev_bqc, prev_phi[tg]))
            bank_free[tg] = k.echo("vector")
            prev_phi[tg] = ev4
            ev_mm = k.op("tensor", lambda e, c=c, tg=tg: e.matmul(
                out=psum[2 + tg][:], lhsT=kvbf[:, (c // 4) * 128:(c // 4) * 128 + 128],
                rhs=qb[:, tg * 512:(tg + 1) * 512], start=True, stop=True),
                waits=(ev4, ev_kb, bank_free[2 + tg]))
            ev_dv = k.op("vector", lambda e, c=c, tg=tg: e.tensor_scalar(
                out=attnb[:, c * TOK + tg * 512: c * TOK + tg * 512 + 512],
                in0=psum[2 + tg][:], scalar1=rk[:, (c // 4):(c // 4) + 1], scalar2=None,
                op0=ALU.mult), waits=(ev_mm, ev_rk))
            bank_free[2 + tg] = k.echo("vector")
            attn_evs.append(ev_dv)
        prev_bq = prev_phi[1]

    k.set_scope('A2_wo')
    stat_done = []
    prev_bo = None
    prev_s1 = prev_s2 = None
    ev_x1_prev = [None, None]
    ev_st_prev = [None, None]
    ev_cb_prev = [None, None]
    for mc in range(32):
        ev_boc = k.op("sync", lambda e, mc=mc: e.dma_start(
            out=bos[:], in_=bo[mc * 128:(mc + 1) * 128, :]), inc=16, waits=(prev_bo,), slot=sl_bo)
        o_done = None
        for kc in range(32):
            wtile, ev_w = wso.get()
            mm_ev = None
            for tg in range(2):
                fn = lambda e, wtile=wtile, kc=kc, tg=tg: e.matmul(
                    out=psum[tg][:], lhsT=wtile,
                    rhs=attnb[:, kc * TOK + tg * 512: kc * TOK + tg * 512 + 512],
                    start=(kc == 0), stop=(kc == 31))
                w = list(attn_evs) if (kc == 0 and mc == 0) else []
                w += [ev_w]
                if kc == 0:
                    w += [bank_free[tg]]
                if tg == 1:
                    mm_ev = k.op("tensor", fn, waits=w)
                else:
                    k.raw("tensor", fn, waits=w)
            wso.done(mm_ev)
            if kc == 31:
                o_done = mm_ev
        for tg in range(2):
            ev_xd = k.op("scalar", lambda e, mc=mc, tg=tg: e.dma_start(
                out=xf[:, tg * 512:(tg + 1) * 512],
                in_=x0_fm[mc * 128:(mc + 1) * 128, tg * 512:(tg + 1) * 512]),
                inc=16, waits=(ev_x1_prev[tg], x_evs[-1]), slot=sl_xr)
            ev_x1 = k.op("vector", lambda e, tg=tg: e.scalar_tensor_tensor(
                out=x1t[:], in0=psum[tg][:], scalar=bos[:, 0:1], in1=xf[:, tg * 512:(tg + 1) * 512],
                op0=ALU.add, op1=ALU.add),
                waits=(o_done, ev_boc, ev_xd, ev_st_prev[0], ev_st_prev[1],
                       ev_cb_prev[0], ev_cb_prev[1]))
            bank_free[tg] = k.echo("vector")
            ev_x1_prev[tg] = ev_x1
            ev_st = k.op("gpsimd", lambda e, mc=mc, tg=tg: e.dma_start(
                out=x1pre_d[mc * 128:(mc + 1) * 128, tg * 512:(tg + 1) * 512], in_=x1t[:]),
                waits=(ev_x1,), inc=16)
            ev_cb = k.op("scalar", lambda e: e.activation(out=x1bb[:], in_=x1t[:], func=AF.Copy),
                         waits=(ev_x1, prev_s1))
            ev_sq = k.op("scalar", lambda e: e.activation(out=xsq[:], in_=x1t[:], func=AF.Square),
                         waits=(ev_x1, prev_s2))
            ev_s1 = k.op("tensor", lambda e, mc=mc, tg=tg: e.matmul(
                out=pstat[tg][:], lhsT=s_onec[:], rhs=x1bb[:],
                start=(mc == 0), stop=(mc == 31)),
                waits=(ev_cb, ev_onec))
            ev_s2 = k.op("tensor", lambda e, mc=mc, tg=tg: e.matmul(
                out=pstat[2 + tg][:], lhsT=s_onec[:], rhs=xsq[:],
                start=(mc == 0), stop=(mc == 31)),
                waits=(ev_sq,))
            prev_s1, prev_s2 = ev_s1, ev_s2
            ev_cb_prev[tg] = ev_sq
            ev_st_prev[tg] = ev_st
            if mc == 31:
                stat_done += [ev_s1, ev_s2]
        prev_bo = ev_x1_prev[1]

    ev_r1a = k.op("vector", lambda e: e.tensor_copy(out=srow[:, 0:512], in_=pstat[0][:]), waits=stat_done)
    ev_r1b = k.op("vector", lambda e: e.tensor_copy(out=srow[:, 512:1024], in_=pstat[1][:]), waits=stat_done)
    ev_r2a = k.op("vector", lambda e: e.tensor_copy(out=srow[:, 1024:1536], in_=pstat[2][:]), waits=stat_done)
    ev_r2b = k.op("vector", lambda e: e.tensor_copy(out=srow[:, 1536:2048], in_=pstat[3][:]), waits=stat_done)
    ev_m = k.op("vector", lambda e: e.tensor_scalar(
        out=srow[:, 2048:3072], in0=srow[:, 0:1024], scalar1=1.0 / D, scalar2=None, op0=ALU.mult),
        waits=(ev_r1a, ev_r1b))
    ev_msq = k.op("vector", lambda e: e.tensor_tensor(
        out=srow[:, 3072:4096], in0=srow[:, 2048:3072], in1=srow[:, 2048:3072], op=ALU.mult),
        waits=(ev_m,))
    ev_v = k.op("vector", lambda e: e.scalar_tensor_tensor(
        out=srow[:, 4096:5120], in0=srow[:, 1024:2048], scalar=1.0 / D, in1=srow[:, 3072:4096],
        op0=ALU.mult, op1=ALU.subtract), waits=(ev_msq, ev_r2a, ev_r2b))
    ev_ve = k.op("vector", lambda e: e.tensor_scalar(
        out=srow[:, 5120:6144], in0=srow[:, 4096:5120], scalar1=1e-5, scalar2=None, op0=ALU.add),
        waits=(ev_v,))
    ev_sq2 = k.op("scalar", lambda e: e.activation(
        out=srow[:, 3072:4096], in_=srow[:, 5120:6144], func=AF.Sqrt), waits=(ev_ve,))
    ev_rc = k.op("vector", lambda e: e.reciprocal(
        out=srow[:, 4096:5120], in_=srow[:, 3072:4096]), waits=(ev_sq2,))
    ev_bb2 = k.op("vector", lambda e: e.scalar_tensor_tensor(
        out=srow[:, 5120:6144], in0=srow[:, 2048:3072], scalar=-1.0, in1=srow[:, 4096:5120],
        op0=ALU.mult, op1=ALU.mult), waits=(ev_rc,))
    ev_wa = k.op("gpsimd", lambda e: e.dma_start(
        out=lnrow_d[0:1, 0:1024], in_=srow[:, 4096:5120]), waits=(ev_rc,), inc=16)
    ev_wb = k.op("gpsimd", lambda e: e.dma_start(
        out=lnrow_d[1:2, 0:1024], in_=srow[:, 5120:6144]), waits=(ev_bb2,), inc=16)
    ev_ar = k.op("gpsimd", lambda e: e.dma_start(
        out=arep[:], in_=bass.AP(lnrow_d.tensor, 0, [[0, 128], [1, 1024]])),
        waits=(ev_wa,), inc=16)
    ev_br = k.op("gpsimd", lambda e: e.dma_start(
        out=brep[:], in_=bass.AP(lnrow_d.tensor, 1024, [[0, 128], [1, 1024]])),
        waits=(ev_wb,), inc=16)

    last_out = None
    prev_ops = [None, None]
    prev_a = [None, None]
    for mc in range(32):
        ev_xd = k.op("sync", lambda e, mc=mc: e.dma_start(
            out=xf[:], in_=x1pre_d[mc * 128:(mc + 1) * 128, :]), inc=16,
            waits=[prev_ops[0], prev_ops[1], ev_st_prev[0], ev_st_prev[1],
                   ev_x1_prev[0], ev_x1_prev[1], ev_cb_prev[0], ev_cb_prev[1],
                   prev_cast[0], prev_cast[1]],
            slot=sl_x0)
        ev_m1 = k.op("vector", lambda e: e.tensor_tensor(
            out=tmp1[:], in0=xf[:, 0:512], in1=arep[:, 0:512], op=ALU.mult),
            waits=(ev_xd, ev_ar, prev_a[0]))
        ev_m2 = k.op("vector", lambda e: e.tensor_tensor(
            out=tmp2[:], in0=xf[:, 512:1024], in1=arep[:, 512:1024], op=ALU.mult),
            waits=(ev_xd, ev_ar, prev_a[1]))
        ev_a1 = k.op("vector", lambda e: e.tensor_tensor(
            out=xf[:, 0:512], in0=tmp1[:], in1=brep[:, 0:512], op=ALU.add),
            waits=(ev_m1, ev_br))
        ev_a2 = k.op("vector", lambda e: e.tensor_tensor(
            out=xf[:, 512:1024], in0=tmp2[:], in1=brep[:, 512:1024], op=ALU.add),
            waits=(ev_m2, ev_br))
        last_out = k.op("gpsimd", lambda e, mc=mc: e.dma_start(
            out=o_x1[mc * 128:(mc + 1) * 128, :], in_=xf[:]), waits=(ev_a1, ev_a2), inc=16)
        prev_ops = [last_out, ev_a2]
        prev_a = [ev_a1, ev_a2]
    return last_out


def build_L1():
    nc = bass.Bass("TRN2", target_bir_lowering=False)
    es = contextlib.ExitStack()
    io = dict(
        x0_fm=nc.dram_tensor("x0_fm", [D, TOK], F32, kind="ExternalInput").ap(),
        wkT=nc.dram_tensor("wkT", [D, 1024], F32, kind="ExternalInput").ap(),
        wvT=nc.dram_tensor("wvT", [D, 1024], F32, kind="ExternalInput").ap(),
        bk=nc.dram_tensor("bk", [1, 1024], F32, kind="ExternalInput").ap(),
        bv=nc.dram_tensor("bv", [1, 1024], F32, kind="ExternalInput").ap(),
        ones_bf=nc.dram_tensor("ones_bf", [1, 128], BF16, kind="ExternalInput").ap(),
        ones_col=nc.dram_tensor("ones_col", [128, 1], BF16, kind="ExternalInput").ap(),
        o_kv=nc.dram_tensor("o_kv", [128, 1032], F32, kind="ExternalOutput").ap(),
    )
    k = KB(nc, es)
    k.init_echo(nc, es)
    blk = es.enter_context(nc.Block())
    ev = emit_A1(k, nc, es, io)
    k.wait("gpsimd", (ev,))
    k.build(blk)
    es.close()
    return nc


def build_L2(debug=False):
    nc = bass.Bass("TRN2", target_bir_lowering=False)
    es = contextlib.ExitStack()
    io = dict(
        x0_fm=nc.dram_tensor("x0_fm", [D, TOK], F32, kind="ExternalInput").ap(),
        wqT=nc.dram_tensor("wqT", [2 * 128 * 128, 1024], F32, kind="ExternalInput").ap(),
        bq=nc.dram_tensor("bq", [D, 1], F32, kind="ExternalInput").ap(),
        bo=nc.dram_tensor("bo", [D, 1], F32, kind="ExternalInput").ap(),
        kvA=nc.dram_tensor("kvA", [128, 1032], F32, kind="ExternalInput").ap(),
        kvB=nc.dram_tensor("kvB", [128, 1032], F32, kind="ExternalInput").ap(),
        ones_bf=nc.dram_tensor("ones_bf", [1, 128], BF16, kind="ExternalInput").ap(),
        ones_col=nc.dram_tensor("ones_col", [128, 1], BF16, kind="ExternalInput").ap(),
        o_x1=nc.dram_tensor("o_x1", [D, TOK], F32, kind="ExternalOutput").ap(),
        x1pre_d=nc.dram_tensor("x1pre_d", [D, TOK], F32,
                               kind="ExternalOutput" if debug else "Internal").ap(),
        lnrow_d=nc.dram_tensor("lnrow_d", [2, 1024], F32, kind="Internal").ap(),
    )
    k = KB(nc, es)
    k.init_echo(nc, es)
    blk = es.enter_context(nc.Block())
    ev = emit_A2(k, nc, es, io)
    k.wait("gpsimd", (ev,))
    k.build(blk)
    es.close()
    return nc


def emit_ln(k, nc, es, name, src_d, out_d, lnrow_d, onec_bf_d, ndim=D, wait_evs=()):
    xf2b = [_sb(nc, es, f"{name}_xf{i}", [128, TOK], F32) for i in range(2)]
    xf = xf2b[0]
    cbb = _sb(nc, es, f"{name}_cb", [128, 512], BF16)
    sqb = _sb(nc, es, f"{name}_sq", [128, 512], BF16)
    tmp1 = [_sb(nc, es, f"{name}_t1{i}", [128, 512], F32) for i in range(2)]
    tmp2 = [_sb(nc, es, f"{name}_t2{i}", [128, 512], F32) for i in range(2)]
    srow = _sb(nc, es, f"{name}_sr", [1, 6 * 1024], F32)
    arep = _sb(nc, es, f"{name}_ar", [128, TOK], F32)
    brep = _sb(nc, es, f"{name}_br", [128, TOK], F32)
    s_onec = _sb(nc, es, f"{name}_oc", [128, 1], BF16)
    pstat = [_ps(nc, es, f"{name}_p{j}", [1, 512]) for j in range(4)]
    nch = ndim // 128
    sl_xf = k.dslot()
    sl_xf2 = k.dslot()

    ev_onec = k.op("sync", lambda e: e.dma_start(out=s_onec[:], in_=onec_bf_d), inc=16)
    prev_cb = [None, None]
    prev_sq = [None, None]
    prev_s = [None, None, None, None]
    stat_done = []
    for mc in range(nch):
        li = mc % 2
        xb = xf2b[li]
        ev_xd = k.op("sync" if li == 0 else "gpsimd", lambda e, mc=mc, xb=xb: e.dma_start(
            out=xb[:], in_=src_d[mc * 128:(mc + 1) * 128, :]), inc=16,
            waits=[prev_cb[li], prev_sq[li]] + list(wait_evs if mc == 0 else ()),
            slot=(sl_xf if li == 0 else sl_xf2))
        for tg in range(2):
            ev_cb = k.op("scalar", lambda e, tg=tg, xb=xb: e.activation(
                out=cbb[:], in_=xb[:, tg * 512:(tg + 1) * 512], func=AF.Copy),
                waits=(ev_xd, prev_s[0], prev_s[1]))
            ev_sq = k.op("scalar", lambda e, tg=tg, xb=xb: e.activation(
                out=sqb[:], in_=xb[:, tg * 512:(tg + 1) * 512], func=AF.Square),
                waits=(ev_xd, prev_s[2], prev_s[3]))
            ev_s1 = k.op("tensor", lambda e, mc=mc, tg=tg: e.matmul(
                out=pstat[tg][:], lhsT=s_onec[:], rhs=cbb[:],
                start=(mc == 0), stop=(mc == nch - 1)), waits=(ev_cb, ev_onec))
            ev_s2 = k.op("tensor", lambda e, mc=mc, tg=tg: e.matmul(
                out=pstat[2 + tg][:], lhsT=s_onec[:], rhs=sqb[:],
                start=(mc == 0), stop=(mc == nch - 1)), waits=(ev_sq,))
            prev_s[tg] = ev_s1
            prev_s[2 + tg] = ev_s2
            prev_cb[li], prev_sq[li] = ev_cb, ev_sq
            if mc == nch - 1:
                stat_done += [ev_s1, ev_s2]
    ev_r1a = k.op("vector", lambda e: e.tensor_copy(out=srow[:, 0:512], in_=pstat[0][:]), waits=stat_done)
    ev_r1b = k.op("vector", lambda e: e.tensor_copy(out=srow[:, 512:1024], in_=pstat[1][:]), waits=stat_done)
    ev_r2a = k.op("vector", lambda e: e.tensor_copy(out=srow[:, 1024:1536], in_=pstat[2][:]), waits=stat_done)
    ev_r2b = k.op("vector", lambda e: e.tensor_copy(out=srow[:, 1536:2048], in_=pstat[3][:]), waits=stat_done)
    ev_m = k.op("vector", lambda e: e.tensor_scalar(
        out=srow[:, 2048:3072], in0=srow[:, 0:1024], scalar1=1.0 / ndim, scalar2=None, op0=ALU.mult),
        waits=(ev_r1a, ev_r1b))
    ev_msq = k.op("vector", lambda e: e.tensor_tensor(
        out=srow[:, 3072:4096], in0=srow[:, 2048:3072], in1=srow[:, 2048:3072], op=ALU.mult),
        waits=(ev_m,))
    ev_v = k.op("vector", lambda e: e.scalar_tensor_tensor(
        out=srow[:, 4096:5120], in0=srow[:, 1024:2048], scalar=1.0 / ndim, in1=srow[:, 3072:4096],
        op0=ALU.mult, op1=ALU.subtract), waits=(ev_msq, ev_r2a, ev_r2b))
    ev_ve = k.op("vector", lambda e: e.tensor_scalar(
        out=srow[:, 5120:6144], in0=srow[:, 4096:5120], scalar1=1e-5, scalar2=None, op0=ALU.add),
        waits=(ev_v,))
    ev_sq2 = k.op("scalar", lambda e: e.activation(
        out=srow[:, 3072:4096], in_=srow[:, 5120:6144], func=AF.Sqrt), waits=(ev_ve,))
    ev_rc = k.op("vector", lambda e: e.reciprocal(
        out=srow[:, 4096:5120], in_=srow[:, 3072:4096]), waits=(ev_sq2,))
    ev_bb2 = k.op("vector", lambda e: e.scalar_tensor_tensor(
        out=srow[:, 5120:6144], in0=srow[:, 2048:3072], scalar=-1.0, in1=srow[:, 4096:5120],
        op0=ALU.mult, op1=ALU.mult), waits=(ev_rc,))
    ev_wa = k.op("gpsimd", lambda e: e.dma_start(
        out=lnrow_d[0:1, 0:1024], in_=srow[:, 4096:5120]), waits=(ev_rc,), inc=16)
    ev_wb = k.op("gpsimd", lambda e: e.dma_start(
        out=lnrow_d[1:2, 0:1024], in_=srow[:, 5120:6144]), waits=(ev_bb2,), inc=16)
    ev_ar = k.op("gpsimd", lambda e: e.dma_start(
        out=arep[:], in_=bass.AP(lnrow_d.tensor, 0, [[0, 128], [1, 1024]])),
        waits=(ev_wa,), inc=16)
    ev_br = k.op("gpsimd", lambda e: e.dma_start(
        out=brep[:], in_=bass.AP(lnrow_d.tensor, 1024, [[0, 128], [1, 1024]])),
        waits=(ev_wb,), inc=16)
    last_out = None
    prev_ops = [None, None]
    prev_a = [[None, None], [None, None]]
    for mc in range(nch):
        li = mc % 2
        xb = xf2b[li]
        ev_xd = k.op("sync" if li == 0 else "gpsimd", lambda e, mc=mc, xb=xb: e.dma_start(
            out=xb[:], in_=src_d[mc * 128:(mc + 1) * 128, :]), inc=16,
            waits=[prev_ops[li], prev_cb[li], prev_sq[li]],
            slot=(sl_xf if li == 0 else sl_xf2))
        ev_m1 = k.op("vector", lambda e, xb=xb, li=li: e.tensor_tensor(
            out=tmp1[li][:], in0=xb[:, 0:512], in1=arep[:, 0:512], op=ALU.mult),
            waits=(ev_xd, ev_ar, prev_a[li][0]))
        ev_m2 = k.op("vector", lambda e, xb=xb, li=li: e.tensor_tensor(
            out=tmp2[li][:], in0=xb[:, 512:1024], in1=arep[:, 512:1024], op=ALU.mult),
            waits=(ev_xd, ev_ar, prev_a[li][1]))
        ev_a1 = k.op("vector", lambda e, xb=xb, li=li: e.tensor_tensor(
            out=xb[:, 0:512], in0=tmp1[li][:], in1=brep[:, 0:512], op=ALU.add),
            waits=(ev_m1, ev_br))
        ev_a2 = k.op("vector", lambda e, xb=xb, li=li: e.tensor_tensor(
            out=xb[:, 512:1024], in0=tmp2[li][:], in1=brep[:, 512:1024], op=ALU.add),
            waits=(ev_m2, ev_br))
        last_out = k.op("scalar", lambda e, mc=mc, xb=xb: e.dma_start(
            out=out_d[mc * 128:(mc + 1) * 128, :], in_=xb[:]), waits=(ev_a1, ev_a2), inc=16)
        prev_ops[li] = last_out
        prev_cb[li] = ev_a1
        prev_sq[li] = ev_a2
        prev_a[li] = [ev_a1, ev_a2]
    return last_out


GATE_CHUNKS = list(range(0, 43)) + list(range(86, 129))


def emit_B(k, nc, es, io, phases=('b0','gate','down','ln')):
    x1_d = io["x1"]
    qgu = io["qw_gu8"]; scgu = io["sc_gu"]; qzgu = io["qz_gu8"]
    qdn = io["qw_dn8"]; scdn = io["sc_dn"]; qzdn = io["qz_dn8"]
    ind32T_d = io["ind32T"]
    h_d = io["h_d"]; sdn_d = io["sdn_d"]; zdnx_d = io["zdnx_d"]; zdn_d = io["zdn_d"]
    x2pre_d = io["x2pre_d"]
    o_x2 = io["o_x2"]; lnrow_d = io["lnrow_d"]; onec_d = io["ones_col"]

    x1bf = _sb(nc, es, "b_x1bf", [128, 32 * TOK], BF16)
    zbgu = _sb(nc, es, "b_zbgu", [32, N_GU], BF16)
    ind32T = _sb(nc, es, "b_i32t", [32, 16 * 128], BF16)

    es0 = contextlib.ExitStack()
    xf0 = _sb(nc, es0, "b0_xf", [128, TOK], F32)
    sl_x1 = k.dslot()
    x_evs = []
    prev_cast = None
    for kc in range(32):
        ev_d = k.op("sync", lambda e, kc=kc: e.dma_start(
            out=xf0[:], in_=x1_d[kc * 128:(kc + 1) * 128, :]), inc=16, waits=(prev_cast,), slot=sl_x1)
        ev_c = k.op("scalar", lambda e, kc=kc: e.activation(
            out=x1bf[:, kc * TOK:(kc + 1) * TOK], in_=xf0[:], func=AF.Copy),
            waits=(ev_d,))
        prev_cast = ev_c
        x_evs.append(ev_c)

    ev_i32t = k.op("sync", lambda e: e.dma_start(out=ind32T[:], in_=ind32T_d), inc=16)

    zqgu = _sb(nc, es0, "b0_zqgu", [32, N_IN], U8)
    zq8 = _sb(nc, es0, "b0_zq8", [86, 2048], U8)
    zu8 = _sb(nc, es0, "b0_zu8", [86, N_IN], U8)
    zdnb = _sb(nc, es0, "b0_zdnb", [86, D], BF16)
    ev_zg = k.op("sync", lambda e: e.dma_start(out=zqgu[:], in_=qzgu), inc=16)
    ev_zd = k.op("sync", lambda e: e.dma_start(out=zq8[:], in_=qzdn), inc=16)
    for half in range(2):
        op0 = ALU.bitwise_and if half == 0 else ALU.logical_shift_right
        sc1 = 15 if half == 0 else 4
        k.fence_all()
        ev_u = k.op("vector", lambda e, op0=op0, sc1=sc1: e.tensor_scalar(
            out=zu8[0:32, 0:N_IN], in0=zqgu[:], scalar1=sc1, scalar2=None, op0=op0),
            waits=(ev_zg,))
        k.fence_all()
        k.op("vector", lambda e, half=half: e.tensor_copy(
            out=zbgu[:, half * N_IN:(half + 1) * N_IN], in_=zu8[0:32, 0:N_IN]),
            waits=(ev_u,))
        k.fence_all()
        ev_u2 = k.op("vector", lambda e, op0=op0, sc1=sc1: e.tensor_scalar(
            out=zu8[0:86, 0:2048], in0=zq8[:], scalar1=sc1, scalar2=None, op0=op0),
            waits=(ev_zd,))
        k.fence_all()
        k.op("vector", lambda e, half=half: e.tensor_copy(
            out=zdnb[:, half * 2048:(half + 1) * 2048], in_=zu8[0:86, 0:2048]),
            waits=(ev_u2,))
        k.fence_all()
    ev_zw = k.op("gpsimd", lambda e: e.dma_start(out=zdn_d, in_=zdnb[:]),
                 waits=[k.last["vector"]], inc=16)

    ps_pre = [_ps(nc, es0, f"b0_ps{i}", [128, 512]) for i in range(4)]
    scr2 = [_sb(nc, es0, "b0_scr2", [2, D], F32)] * 2
    scb2 = [_sb(nc, es0, "b0_scb2", [2, D], BF16)] * 2
    zr2 = [_sb(nc, es0, "b0_zr2", [2, D], BF16)] * 2
    sfull = [_sb(nc, es0, f"b0_sf{i}", [128, D], BF16) for i in range(2)]
    zfull = [_sb(nc, es0, "b0_zf", [128, D], BF16)] * 2
    ind2sb = _sb(nc, es0, "b0_i2", [2, 128], BF16)
    ev_i2 = k.op("sync", lambda e: e.dma_start(out=ind2sb[:], in_=io["ind2"]), inc=16)
    sl_sc2 = [k.dslot(), k.dslot()]
    sl_zr2 = [k.dslot(), k.dslot()]
    sl_sdt = [k.dslot(), k.dslot()]
    sl_zdt = [k.dslot(), k.dslot()]
    prev_sc = [None, None]
    prev_zr = [None, None]
    prev_sdt = [None, None]
    prev_zdt = [None, None]
    prev_smm = [None, None]
    prev_zmm = [None, None]
    prev_sev = [None, None]
    prev_zev = [None, None]
    for hc in range(86):
        g0 = 2 * (hc % 43)
        hp = hc % 2
        ev_sd = k.op("scalar", lambda e, g0=g0, hp=hp: e.dma_start(
            out=scr2[hp][:], in_=scdn[g0:g0 + 2, :]), inc=16,
            waits=(prev_sc[0], prev_sc[1]), slot=sl_sc2[hp])
        ev_sc = k.op("scalar", lambda e, hp=hp: e.activation(
            out=scb2[hp][:], in_=scr2[hp][:], func=AF.Copy),
            waits=(ev_sd, prev_smm[0], prev_smm[1]))
        prev_sc[hp] = ev_sc
        ev_zr = k.op("gpsimd", lambda e, g0=g0, hp=hp: e.dma_start(
            out=zr2[hp][:], in_=zdn_d[g0:g0 + 2, :]), inc=16,
            waits=(ev_zw, prev_zmm[0], prev_zmm[1], prev_zr[hp]), slot=sl_zr2[hp])
        prev_zr[hp] = ev_zr
        ev_e = ev_ez = None
        for sl in range(8):
            par = sl % 2
            ev_mm = k.op("tensor", lambda e, sl=sl, par=par, hp=hp: e.matmul(
                out=ps_pre[par][:], lhsT=ind2sb[:], rhs=scb2[hp][:, sl * 512:(sl + 1) * 512],
                start=True, stop=True), waits=(ev_sc, ev_i2, prev_sev[par]))
            prev_smm[par] = ev_mm
            ev_e = k.op("scalar", lambda e, sl=sl, par=par, hp=hp: e.activation(
                out=sfull[hp][:, sl * 512:(sl + 1) * 512], in_=ps_pre[par][:], func=AF.Copy),
                waits=(ev_mm, prev_sdt[hp] if sl == 0 else None))
            prev_sev[par] = ev_e
            ev_mz = k.op("tensor", lambda e, sl=sl, par=par, hp=hp: e.matmul(
                out=ps_pre[2 + par][:], lhsT=ind2sb[:], rhs=zr2[hp][:, sl * 512:(sl + 1) * 512],
                start=True, stop=True), waits=(ev_zr, prev_zev[par]))
            prev_zmm[par] = ev_mz
            ev_ez = k.op("vector", lambda e, sl=sl, par=par, hp=hp: e.tensor_copy(
                out=zfull[hp][:, sl * 512:(sl + 1) * 512], in_=ps_pre[2 + par][:]),
                waits=(ev_mz,) + ((prev_zdt[0], prev_zdt[1]) if sl == 0 else ()))
            prev_zev[par] = ev_ez
        ev_o = k.op("gpsimd", lambda e, hc=hc, hp=hp: e.dma_start(
            out=sdn_d[hc * 128:(hc + 1) * 128, :], in_=sfull[hp][:]),
            waits=(ev_e,), inc=16, slot=sl_sdt[hp])
        prev_sdt[hp] = ev_o
        ev_oz = k.op("gpsimd", lambda e, hc=hc, hp=hp: e.dma_start(
            out=zdnx_d[hc * 128:(hc + 1) * 128, :], in_=zfull[hp][:]),
            waits=(ev_ez,), inc=16, slot=sl_zdt[hp])
        prev_zdt[hp] = ev_oz
    k.fence_all()
    es0.close()
    k.set_scope('B_gate')
    if 'gate' not in phases:
        return k.last["gpsimd"] or k.last["vector"]

    esg = contextlib.ExitStack()
    qsl = [_sb(nc, esg, f"bg_q{i}", [128, 4096], U8) for i in range(2)]
    pl8 = [_sb(nc, esg, f"bg_pl{i}", [128, 4096], U8) for i in range(2)]
    tb = [_sb(nc, esg, f"bg_t{i}", [128, 4096], BF16) for i in range(2)]
    wsl = [_sb(nc, esg, f"bg_w{i}", [128, 4096], BF16) for i in range(2)]
    Ssb = [_sb(nc, esg, f"bg_S{i}", [128, 2048], BF16) for i in range(2)]
    scb = [_sb(nc, esg, f"bg_sc{i}", [32, 128], BF16) for i in range(2)]
    scbf = [_sb(nc, esg, f"bg_scf{i}", [32, 128], F32) for i in range(2)]
    sg = _sb(nc, esg, "bg_sg", [128, 1024], BF16)
    hsb = [_sb(nc, esg, f"bg_h{i}", [128, 1024], BF16) for i in range(2)]
    pacc = [_ps(nc, esg, f"bg_a{i}", [128, 512]) for i in range(4)]
    pZ = _ps(nc, esg, "bg_pZ", [128, 1024])
    pS2 = _ps(nc, esg, "bg_pS2", [128, 1024])
    sl_q = [k.dslot(), k.dslot()]
    sl_sc = [k.dslot(), k.dslot()]
    sl_h = [k.dslot(), k.dslot()]

    use_q = [None, None]
    use_pl = [None, None]
    use_t = [None, None]
    use_w = [None, None]
    use_S = [None, None]
    use_sc = [None, None]
    use_pZ = None
    use_pS2 = None
    use_sgw = None
    ev_sg = None
    use_h_pe = [None, None]
    use_h_dma = [None, None]
    acc_free = [None, None]
    prev_scl = None
    h_count = 0
    chunk_seq = []
    for jg in GATE_CHUNKS:
        chunk_seq.append((jg, True))
        chunk_seq.append((jg + 43, False))

    st = {"acc_free": [None, None, None, None], "use_sgw": None, "ev_sg": None,
          "use_h_pe": [None, None], "use_h_dma": [None, None], "h_count": 0,
          "pending": None}

    def make_main(par, is_gate, ec_w):
        def emit_main():
            b0_, b1_ = par * 2, par * 2 + 1
            mm_last = None
            for kc in range(32):
                for tg in range(2):
                    fn = lambda e, par=par, kc=kc, tg=tg: e.matmul(
                        out=pacc[par * 2 + tg][:],
                        lhsT=wsl[par][:, kc * 128:(kc + 1) * 128],
                        rhs=x1bf[:, kc * TOK + tg * 512: kc * TOK + tg * 512 + 512],
                        start=(kc == 0), stop=(kc == 31))
                    w = ([ec_w] + x_evs) if kc == 0 else []
                    if kc == 0:
                        w += [st["acc_free"][par * 2 + tg]]
                    if kc == 31 and tg == 1:
                        mm_last = k.op("tensor", fn, waits=w)
                    else:
                        k.raw("tensor", fn, waits=w)
            use_w[par] = mm_last
            if is_gate:
                ev_sg = None
                for tg in range(2):
                    evs = k.op("scalar", lambda e, tg=tg, b0_=b0_: e.activation(
                        out=sg[:, tg * 512:(tg + 1) * 512], in_=pacc[b0_ + tg][:], func=AF.Silu),
                        waits=(mm_last, st["use_sgw"]))
                    if tg == 1:
                        ev_sg = evs
                st["ev_sg"] = ev_sg
                ec = k.echo("scalar")
                st["acc_free"][b0_] = st["acc_free"][b1_] = ec
            else:
                hc = st["h_count"]
                hpar = hc % 2
                evh = None
                for tg in range(2):
                    evh = k.op("vector", lambda e, tg=tg, hpar=hpar, b0_=b0_: e.tensor_tensor(
                        out=hsb[hpar][:, tg * 512:(tg + 1) * 512],
                        in0=pacc[b0_ + tg][:], in1=sg[:, tg * 512:(tg + 1) * 512],
                        op=ALU.mult),
                        waits=(mm_last, st["ev_sg"], st["use_h_pe"][hpar], st["use_h_dma"][hpar]))
                ec = k.echo("vector")
                st["acc_free"][b0_] = st["acc_free"][b1_] = ec
                st["use_sgw"] = ec
                ev_ho = k.op("gpsimd", lambda e, hc=hc, hpar=hpar: e.dma_start(
                    out=h_d[hc * 128:(hc + 1) * 128, :], in_=hsb[hpar][:]), waits=(evh,), inc=16,
                    slot=sl_h[hpar])
                st["use_h_pe"][hpar] = evh
                st["use_h_dma"][hpar] = ev_ho
                st["h_count"] += 1
        return emit_main

    for ci, (c, is_gate) in enumerate(chunk_seq):
        par = ci % 2
        lo = c < 86
        r = c % 86
        ap_in = bass.AP(qgu.tensor, r * 128 * 4096, [[4096, 128], [1, 4096]])
        evq = k.op("sync", lambda e, par=par, ap_in=ap_in: e.dma_start(
            out=qsl[par][:], in_=ap_in), inc=16, waits=(use_q[par],), slot=sl_q[par])
        op0 = ALU.bitwise_and if lo else ALU.logical_shift_right
        sc1 = 15 if lo else 4
        evp = k.op("vector", lambda e, par=par, op0=op0, sc1=sc1: e.tensor_scalar(
            out=pl8[par][:], in0=qsl[par][:], scalar1=sc1, scalar2=None, op0=op0),
            waits=(evq, use_pl[par]))
        use_q[par] = evp
        evsd = k.op("scalar", lambda e, c=c, par=par: e.dma_start(
            out=scbf[par][:], in_=scgu[:, c * 128:(c + 1) * 128]), inc=16, waits=(prev_scl,),
            slot=sl_sc[par])
        evsc = k.op("scalar", lambda e, par=par: e.activation(
            out=scb[par][:], in_=scbf[par][:], func=AF.Copy), waits=(evsd, use_sc[par]))
        prev_scl = evsc
        sub_last = None
        for hs in range(2):
            pslc = slice(hs * 512, hs * 512 + 512)
            z_done = None
            s_done = None
            for j in range(8):
                k16 = 8 * hs + j
                fnz = lambda e, k16=k16, j=j, c=c, par=par: e.matmul(
                    out=pZ[:, j * 128:(j + 1) * 128],
                    lhsT=ind32T[:, k16 * 128:(k16 + 1) * 128],
                    rhs=zbgu[:, c * 128:(c + 1) * 128], start=True, stop=True)
                wz = [ev_i32t, use_pZ]
                if j == 7:
                    z_done = k.op("tensor", fnz, waits=wz)
                else:
                    k.raw("tensor", fnz, waits=wz)
            for j in range(8):
                k16 = 8 * hs + j
                fns = lambda e, k16=k16, j=j, par=par: e.matmul(
                    out=pS2[:, j * 128:(j + 1) * 128],
                    lhsT=ind32T[:, k16 * 128:(k16 + 1) * 128],
                    rhs=scb[par][:, 0:128], start=True, stop=True)
                ws_ = [evsc, use_pS2]
                if j == 7:
                    s_done = k.op("tensor", fns, waits=ws_)
                else:
                    k.raw("tensor", fns, waits=ws_)
            use_sc[par] = s_done
            evS = k.op("scalar", lambda e, hs=hs, par=par: e.activation(
                out=Ssb[par][:, hs * 1024:(hs + 1) * 1024], in_=pS2[:], func=AF.Copy),
                waits=(s_done, use_S[par]))
            use_pS2 = k.echo("scalar")
            sub_evs = []
            for rr in (hs, hs + 2):
                evsub = k.op("vector", lambda e, rr=rr, par=par: e.tensor_tensor(
                    out=tb[par][:, rr * 1024:(rr + 1) * 1024],
                    in0=pl8[par][:, rr * 1024:(rr + 1) * 1024], in1=pZ[:], op=ALU.subtract),
                    waits=(z_done, evp, use_t[par]))
                sub_evs.append(evsub)
            use_pZ = k.echo("vector")
            sub_last = sub_evs[-1]
            for i_r, rr in enumerate((hs, hs + 2)):
                evm = k.op("vector", lambda e, rr=rr, hs=hs, par=par: e.tensor_tensor(
                    out=wsl[par][:, rr * 1024:(rr + 1) * 1024],
                    in0=tb[par][:, rr * 1024:(rr + 1) * 1024],
                    in1=Ssb[par][:, hs * 1024:(hs + 1) * 1024], op=ALU.mult),
                    waits=(sub_evs[i_r], evS, use_w[par]))
        ec_w = k.echo("vector")
        use_t[par] = ec_w
        use_S[par] = ec_w
        use_pl[par] = sub_last
        if st["pending"] is not None:
            st["pending"]()
        st["pending"] = make_main(par, is_gate, ec_w)
    st["pending"]()
    k.fence_all()
    esg.close()
    k.set_scope('B_down')
    if 'down' not in phases:
        return k.last["gpsimd"] or k.last["vector"]

    esd = contextlib.ExitStack()
    NB = 4
    qd = [_sb(nc, esd, f"bd_q{i}", [128, 512], U8) for i in range(NB)]
    pld = [_sb(nc, esd, f"bd_pl{i}", [128, 512], U8) for i in range(NB)]
    tdn = [_sb(nc, esd, f"bd_t{i}", [128, 512], BF16) for i in range(NB)]
    sdn_t = [_sb(nc, esd, f"bd_sd{i}", [128, 512], BF16) for i in range(NB)]
    zdn_t = [_sb(nc, esd, f"bd_zd{i}", [128, 512], BF16) for i in range(NB)]
    wdn = [_sb(nc, esd, f"bd_w{i}", [128, 512], BF16) for i in range(NB)]
    hdn = [_sb(nc, esd, f"bd_h{i}", [128, 1024], BF16) for i in range(NB)]
    xres = _sb(nc, esd, "bd_xr", [128, 512], F32)
    x2t = _sb(nc, esd, "bd_x2", [128, 512], F32)
    psum8 = [_ps(nc, esd, f"bd_p{i}", [128, 512]) for i in range(8)]

    sl_qd = [k.dslot() for _ in range(NB)]
    sl_zd = [k.dslot() for _ in range(NB)]
    sl_sd = [k.dslot() for _ in range(NB)]
    sl_hd = [k.dslot() for _ in range(NB)]
    sl_xres = k.dslot()
    sl_x2o = k.dslot()
    use_qd = [None] * NB
    use_pld = [None] * NB
    use_tdn = [None] * NB
    use_sdn = [None] * NB
    use_zdn = [None] * NB
    use_wdn = [None] * NB
    use_hdn = [None] * NB
    acc_free2 = [None] * 8
    prev_res = None
    prev_x2 = None
    for mb in range(8):
        rb = mb % 4
        op0 = ALU.bitwise_and if mb < 4 else ALU.logical_shift_right
        sc1 = 15 if mb < 4 else 4
        mm_last = None
        for hc in range(86):
            par = hc % NB
            if io.get("_rhstest") == 4:
                evw = None
            else:
                evq = k.op("sync", lambda e, hc=hc, rb=rb, par=par: e.dma_start(
                    out=qd[par][:], in_=qdn[(rb * 86 + hc) * 128:(rb * 86 + hc + 1) * 128, :]),
                    inc=16, waits=(use_qd[par],), slot=sl_qd[par])
                evp = k.op("vector", lambda e, par=par, op0=op0, sc1=sc1: e.tensor_scalar(
                    out=pld[par][:], in0=qd[par][:], scalar1=sc1, scalar2=None, op0=op0),
                    waits=(evq, use_pld[par]))
                use_qd[par] = evp
                evzd = k.op("scalar", lambda e, hc=hc, mb=mb, par=par: e.dma_start(
                    out=zdn_t[par][:], in_=zdnx_d[hc * 128:(hc + 1) * 128, mb * 512:(mb + 1) * 512]),
                    inc=16, waits=(use_zdn[par],), slot=sl_zd[par])
                evsub = k.op("vector", lambda e, par=par: e.tensor_tensor(
                    out=tdn[par][:], in0=pld[par][:], in1=zdn_t[par][:], op=ALU.subtract),
                    waits=(evp, evzd, use_tdn[par]))
                use_pld[par] = evsub
                use_zdn[par] = evsub
                evsd = k.op("scalar", lambda e, hc=hc, mb=mb, par=par: e.dma_start(
                    out=sdn_t[par][:], in_=sdn_d[hc * 128:(hc + 1) * 128, mb * 512:(mb + 1) * 512]),
                    inc=16, waits=(use_sdn[par],), slot=sl_sd[par])
                evw = k.op("vector", lambda e, par=par: e.tensor_tensor(
                    out=wdn[par][:], in0=tdn[par][:], in1=sdn_t[par][:], op=ALU.mult),
                    waits=(evsub, evsd, use_wdn[par]))
                evw = k.echo("vector")
                use_tdn[par] = evw
                use_sdn[par] = evw
            evhd = k.op("sync", lambda e, hc=hc, par=par: e.dma_start(
                out=hdn[par][:], in_=h_d[hc * 128:(hc + 1) * 128, :]), inc=16,
                waits=(use_hdn[par],), slot=sl_hd[par])
            if io.get("_paranoid") in (True, "down"):
                k.fence_all()
            mm_last = None
            for mc4 in range(4):
                for tg in range(2):
                    rhs_sel = (x1bf[:, tg * 512:tg * 512 + 512] if io.get("_rhstest") in (1, 2)
                               else hdn[par][:, tg * 512:(tg + 1) * 512])
                    lhs_sel = (x1bf[:, 0:128] if io.get("_rhstest") in (2, 3, 4)
                               else wdn[par][:, mc4 * 128:(mc4 + 1) * 128])
                    fn = lambda e, mc4=mc4, tg=tg, par=par, hc=hc, rhs_sel=rhs_sel, lhs_sel=lhs_sel: e.matmul(
                        out=psum8[mc4 * 2 + tg][:], lhsT=lhs_sel,
                        rhs=rhs_sel,
                        start=(hc == 0), stop=(hc == 85))
                    w = [evw, evhd]
                    if hc == 0:
                        w += [acc_free2[mc4 * 2 + tg]]
                    if mc4 == 3 and tg == 1:
                        mm_last = k.op("tensor", fn, waits=w)
                    else:
                        k.raw("tensor", fn, waits=w)
            use_wdn[par] = mm_last
            use_hdn[par] = mm_last
        for mc4 in range(4):
            mc = 4 * mb + mc4
            for tg in range(2):
                ev_rd = k.op("scalar", lambda e, mc=mc, tg=tg: e.dma_start(
                    out=xres[:], in_=x1_d[mc * 128:(mc + 1) * 128, tg * 512:(tg + 1) * 512]),
                    inc=16, waits=(prev_res,), slot=sl_xres)
                ev_x2 = k.op("vector", lambda e, mc4=mc4, tg=tg: e.tensor_tensor(
                    out=x2t[:], in0=psum8[mc4 * 2 + tg][:], in1=xres[:], op=ALU.add),
                    waits=(mm_last, ev_rd, prev_x2))
                prev_res = ev_x2
                ev_o = k.op("gpsimd", lambda e, mc=mc, tg=tg: e.dma_start(
                    out=x2pre_d[mc * 128:(mc + 1) * 128, tg * 512:(tg + 1) * 512], in_=x2t[:]),
                    waits=(ev_x2,), inc=16, slot=sl_x2o)
                prev_x2 = ev_o
                acc_free2[mc4 * 2 + tg] = None
        ec2 = k.echo("vector")
        for j8 in range(8):
            acc_free2[j8] = ec2
    k.fence_all()
    esd.close()
    k.set_scope('B_ln')
    if 'ln' not in phases:
        return k.last["gpsimd"] or k.last["vector"]

    esl = contextlib.ExitStack()
    ev = emit_ln(k, nc, esl, "bln", x2pre_d, o_x2, lnrow_d, onec_d)
    k.fence_all()
    esl.close()
    return ev


def build_L3(debug=False, phases=('b0','gate','down','ln'), paranoid=False, rhstest=False):
    nc = bass.Bass("TRN2", target_bir_lowering=False)
    es = contextlib.ExitStack()
    io = dict(
        x1=nc.dram_tensor("x1", [D, TOK], F32, kind="ExternalInput").ap(),
        qw_gu8=nc.dram_tensor("qw_gu8", [86 * 128, 4096], U8, kind="ExternalInput").ap(),
        sc_gu=nc.dram_tensor("sc_gu", [32, N_GU], F32, kind="ExternalInput").ap(),
        qz_gu8=nc.dram_tensor("qz_gu8", [32, N_IN], U8, kind="ExternalInput").ap(),
        qw_dn8=nc.dram_tensor("qw_dn8", [4 * 86 * 128, 512], U8, kind="ExternalInput").ap(),
        sc_dn=nc.dram_tensor("sc_dn", [86, D], F32, kind="ExternalInput").ap(),
        qz_dn8=nc.dram_tensor("qz_dn8", [86, 2048], U8, kind="ExternalInput").ap(),
        ind2=nc.dram_tensor("ind2", [2, 128], BF16, kind="ExternalInput").ap(),
        ind32T=nc.dram_tensor("ind32T", [32, 16 * 128], BF16, kind="ExternalInput").ap(),
        ones_col=nc.dram_tensor("ones_col", [128, 1], BF16, kind="ExternalInput").ap(),
        o_x2=nc.dram_tensor("o_x2", [D, TOK], F32, kind="ExternalOutput").ap(),
        h_d=nc.dram_tensor("h_d", [N_IN, TOK], BF16,
                           kind="ExternalOutput" if debug else "Internal").ap(),
        sdn_d=nc.dram_tensor("sdn_d", [N_IN, D], BF16,
                             kind="ExternalOutput" if debug else "Internal").ap(),
        zdnx_d=nc.dram_tensor("zdnx_d", [N_IN, D], BF16, kind="Internal").ap(),
        zdn_d=nc.dram_tensor("zdn_d", [86, D], BF16, kind="Internal").ap(),
        x2pre_d=nc.dram_tensor("x2pre_d", [D, TOK], F32,
                               kind="ExternalOutput" if debug else "Internal").ap(),
        lnrow_d=nc.dram_tensor("lnrow_d", [2, 1024], F32, kind="Internal").ap(),
    )
    io["_paranoid"] = paranoid
    io["_rhstest"] = rhstest
    k = KB(nc, es)
    k.init_echo(nc, es)
    blk = es.enter_context(nc.Block())
    ev = emit_B(k, nc, es, io, phases=phases)
    k.fence_all()
    k.build(blk)
    es.close()
    return nc


def host_prep_A(inputs):
    x0 = inputs["L0_attn_hidden_states"]
    wq = inputs["L0_attn_Wq"]; bq = inputs["L0_attn_bq"]
    wk = inputs["L0_attn_Wk"]; bk = inputs["L0_attn_bk"]
    wv = inputs["L0_attn_Wv"]; bv = inputs["L0_attn_bv"]
    wo = inputs["L0_attn_Wo"]; bo = inputs["L0_attn_bo"]
    wqT = np.ascontiguousarray(wq[:, PERM_D].T)
    wkT = np.ascontiguousarray(wk[:, PERM_D].T)
    wvT = np.ascontiguousarray(wv[:, PERM_D].T)
    woT = np.ascontiguousarray(wo[PERM_D, :].T)
    wqT = np.concatenate([
        pack_tiles([wqT[kc * 128:(kc + 1) * 128, c * 128:(c + 1) * 128]
                    for c in range(32) for kc in range(32)], 128, 8),
        pack_tiles([woT[kc * 128:(kc + 1) * 128, mc * 128:(mc + 1) * 128]
                    for mc in range(32) for kc in range(32)], 128, 8)], axis=0)
    woT = None
    ones_bf = np.ones((1, 128), dtype=BF)
    ones_col = np.ones((128, 1), dtype=BF)
    shared = dict(wqT=wqT, wkT=wkT, wvT=wvT,
                  bq=np.ascontiguousarray(bq.reshape(-1, 1)),
                  bk=np.ascontiguousarray(bk.reshape(1, -1)),
                  bv=np.ascontiguousarray(bv.reshape(1, -1)),
                  bo=np.ascontiguousarray(bo[PERM_D].reshape(-1, 1)),
                  ones_bf=ones_bf, ones_col=ones_col)
    x0_fms = []
    for c in range(NCORE):
        beta, eta = c // 2, c % 2
        x0_fms.append(np.ascontiguousarray(x0[beta, eta * TOK:(eta + 1) * TOK, :][:, PERM_D].T))
    return shared, x0_fms


def host_prep_B(inputs):
    qw_gu8 = np.ascontiguousarray(inputs["L0_ffn_qweight_gu"][PERM_D]).view(np.uint8)
    qw_gu8 = np.ascontiguousarray(
        qw_gu8.reshape(32, 128, 86, 128).transpose(2, 1, 0, 3)).reshape(86 * 128, 32 * 128)
    sc_gu = np.ascontiguousarray(inputs["L0_ffn_scales_gu"][:, PERM_GU])
    qz_gu8 = np.ascontiguousarray(inputs["L0_ffn_qzeros_gu"]).view(np.uint8)
    qw_dn8 = np.ascontiguousarray(inputs["L0_ffn_qweight_dn"][PERM_H]).view(np.uint8)
    qw_dn8 = np.ascontiguousarray(
        qw_dn8.reshape(86, 128, 4, 512).transpose(2, 0, 1, 3)).reshape(4 * 86 * 128, 512)
    sc_dn = np.ascontiguousarray(inputs["L0_ffn_scales_dn"][:, PERM_D])
    qz_dn8 = np.ascontiguousarray(inputs["L0_ffn_qzeros_dn"]).view(np.uint8)
    ind2 = np.zeros((2, 128), dtype=BF); ind2[0, :64] = 1; ind2[1, 64:] = 1
    p = np.arange(128)
    ind32T = np.zeros((32, 16, 128), dtype=BF)
    for k16 in range(16):
        ind32T[2 * k16 + (p >= 64), k16, p] = 1
    ind32T = np.ascontiguousarray(ind32T.reshape(32, 16 * 128))
    ones_col = np.ones((128, 1), dtype=BF)
    return dict(qw_gu8=qw_gu8, sc_gu=sc_gu, qz_gu8=qz_gu8, qw_dn8=qw_dn8,
                sc_dn=sc_dn, qz_dn8=qz_dn8, ind2=ind2, ind32T=ind32T,
                ones_col=ones_col)


def run_stage_A(inputs):
    shared, x0_fms = host_prep_A(inputs)
    nc1 = build_L1()
    maps1 = [dict(x0_fm=x0_fms[c], wkT=shared["wkT"], wvT=shared["wvT"],
                  bk=shared["bk"], bv=shared["bv"], ones_bf=shared["ones_bf"],
                  ones_col=shared["ones_col"]) for c in range(NCORE)]
    res1 = run_bass_kernel_spmd(nc1, maps1, core_ids=list(range(NCORE)))
    kvs = [np.asarray(res1.results[c]["o_kv"], dtype=np.float32) for c in range(NCORE)]
    nc2 = build_L2()
    maps2 = [dict(x0_fm=x0_fms[c], wqT=shared["wqT"],
                  bq=shared["bq"], bo=shared["bo"],
                  kvA=kvs[(c // 2) * 2], kvB=kvs[(c // 2) * 2 + 1],
                  ones_bf=shared["ones_bf"], ones_col=shared["ones_col"])
             for c in range(NCORE)]
    res2 = run_bass_kernel_spmd(nc2, maps2, core_ids=list(range(NCORE)))
    x1s = [np.asarray(res2.results[c]["o_x1"], dtype=np.float32) for c in range(NCORE)]
    return x1s


def kernel(**inputs):
    return kernel_fused(**inputs)


def kernel_unfused(**inputs):
    x1s = run_stage_A(inputs)
    shb = host_prep_B(inputs)
    nc3 = build_L3()
    maps3 = [dict(x1=x1s[c], **shb) for c in range(NCORE)]
    res3 = run_bass_kernel_spmd(nc3, maps3, core_ids=list(range(NCORE)))
    x2s = [np.asarray(res3.results[c]["o_x2"], dtype=np.float32) for c in range(NCORE)]
    per_eta, shc = host_prep_C(inputs)
    nc4 = build_L4()
    maps4 = []
    for c in range(NCORE):
        beta, eta = c // 2, c % 2
        maps4.append(dict(x2a=x2s[2 * beta], x2b=x2s[2 * beta + 1],
                          **per_eta[eta], **shc))
    res4 = run_bass_kernel_spmd(nc4, maps4, core_ids=list(range(NCORE)))
    parts = [np.asarray(res4.results[c]["o_part"], dtype=np.float32) for c in range(NCORE)]
    ssqs = [np.asarray(res4.results[c]["o_ssq"], dtype=np.float32) for c in range(NCORE)]
    shd = host_prep_D(inputs)
    nc5 = build_L5()
    maps5 = []
    for c in range(NCORE):
        beta, eta = c // 2, c % 2
        sl = slice(eta * TOK, (eta + 1) * TOK)
        maps5.append(dict(
            partA=np.ascontiguousarray(parts[2 * beta][:, sl]),
            partB=np.ascontiguousarray(parts[2 * beta + 1][:, sl]),
            ssqA=np.ascontiguousarray(ssqs[2 * beta][:, sl]),
            ssqB=np.ascontiguousarray(ssqs[2 * beta + 1][:, sl]),
            x2f=x2s[c], **shd))
    res5 = run_bass_kernel_spmd(nc5, maps5, core_ids=list(range(NCORE)))
    out = np.empty((B, T, D), dtype=np.float32)
    for c in range(NCORE):
        beta, eta = c // 2, c % 2
        o = np.asarray(res5.results[c]["o_out"], dtype=np.float32)
        out[beta, eta * TOK:(eta + 1) * TOK, :] = o[INV_PERM_D].T
    return out


def emit_C(k, nc, es, io):
    x2a, x2b = io["x2a"], io["x2b"]
    wipT = io["wipT"]
    convw = io["convw"]; convb = io["convb"]
    dtb = io["dtb"]; alog = io["alog"]
    dexp = io["dexp"]; nwv = io["nw"]
    woutT = io["woutT"]
    oh64 = io["oh64"]; ohpair = io["ohpair"]
    ones64 = io["ones64"]
    mask0 = io["mask0"]; mask1 = io["mask1"]
    idn = io["idn"]
    zx_d = io["zx_d"]; ynw_d = io["ynw_d"]
    o_part = io["o_part"]; o_ssq = io["o_ssq"]
    part2 = io.get("_part2", False)

    TT = 2048

    k.set_scope('C1')
    es1 = contextlib.ExitStack()
    x2ab = _sb(nc, es1, "c1_x2", [128, 32 * TT], BF16)
    xf = [_sb(nc, es1, f"c1_xf{i}", [128, 1024], F32) for i in range(4)]
    sl_xa = [k.dslot() for _ in range(4)]
    prev_cast = [None] * 4
    x_evs = []
    for kc in range(32):
        for half, src in ((0, x2a), (1, x2b)):
            li = (2 * kc + half) % 4
            dq = "sync" if li % 2 == 0 else "gpsimd"
            ev_d = k.op(dq, lambda e, kc=kc, src=src, li=li: e.dma_start(
                out=xf[li][:], in_=src[kc * 128:(kc + 1) * 128, :]), inc=16,
                waits=(prev_cast[li],), slot=sl_xa[li])
            ev_c = k.op("scalar", lambda e, kc=kc, half=half, li=li: e.activation(
                out=x2ab[:, kc * TT + half * 1024: kc * TT + half * 1024 + 1024],
                in_=xf[li][:], func=AF.Copy), waits=(ev_d,))
            prev_cast[li] = ev_c
            x_evs.append(ev_c)
    ws1 = WChunk(k, nc, es1, "c1_w", wipT, 256, 4)
    zxt = [_sb(nc, es1, f"c1_zt{i}", [128, 512], BF16) for i in range(2)]
    sl_zx = [k.dslot(), k.dslot()]
    ps1 = [_ps(nc, es1, f"c1_p{i}", [128, 512]) for i in range(8)]
    bank_free = [None] * 8
    use_zxt = [None, None]
    zpar = 0
    for ocg in range(37):
        noc = 2 if ocg < 36 else 1
        mm_done = [None] * 8
        for kc in range(32):
            wtile, ev_w = ws1.get()
            for oc4 in range(noc):
                for tg in range(4):
                    b = oc4 * 4 + tg
                    fn = lambda e, wtile=wtile, kc=kc, oc4=oc4, tg=tg, b=b: e.matmul(
                        out=ps1[b][:], lhsT=wtile[:, oc4 * 128: oc4 * 128 + 128],
                        rhs=x2ab[:, kc * TT + tg * 512: kc * TT + tg * 512 + 512],
                        start=(kc == 0), stop=(kc == 31))
                    w = [ev_w, x_evs[2 * kc], x_evs[2 * kc + 1]]
                    if kc == 0:
                        w += [bank_free[b]]
                    if oc4 == noc - 1 and tg == 3:
                        mm_done[b] = k.op("tensor", fn, waits=w)
                    else:
                        k.raw("tensor", fn, waits=w)
            ws1.done(k.last["tensor"])
        last_mm = k.last["tensor"]
        for oc4 in range(noc):
            for tg in range(4):
                b = oc4 * 4 + tg
                oc = ocg * 2 + oc4
                ev_e = k.op("vector", lambda e, b=b, zp=zpar: e.tensor_copy(
                    out=zxt[zp][:], in_=ps1[b][:]), waits=(last_mm, use_zxt[zpar]))
                ev_ec = k.echo("vector")
                bank_free[b] = ev_ec
                ev_o = k.op("gpsimd", lambda e, oc=oc, tg=tg, zp=zpar: e.dma_start(
                    out=zx_d[oc * 128:(oc + 1) * 128, tg * 512:(tg + 1) * 512], in_=zxt[zp][:]),
                    waits=(ev_e,), inc=16, slot=sl_zx[zpar])
                use_zxt[zpar] = ev_o
                zpar = 1 - zpar
    k.fence_all()
    es1.close()

    k.set_scope('C2')
    es2 = contextlib.ExitStack()
    xbcr = _sb(nc, es2, "c2_xr", [128, 41 * 515], BF16)
    xbcc = _sb(nc, es2, "c2_xc", [128, 40 * 512], BF16)
    cwc = _sb(nc, es2, "c2_cw", [128, 160], F32)
    cbc = _sb(nc, es2, "c2_cb", [128, 40], F32)
    ct1 = _sb(nc, es2, "c2_t1", [128, 512], F32)
    ct2 = _sb(nc, es2, "c2_t2", [128, 512], F32)
    dtc = _sb(nc, es2, "c2_dt", [64, 8 * 512], F32)
    dAf = _sb(nc, es2, "c2_dA", [64, 256], F32)
    csf = [_sb(nc, es2, f"c2_cs{i}", [64, 256], F32) for i in range(2)]
    odf = _sb(nc, es2, "c2_od", [64, 256], BF16)
    decf = _sb(nc, es2, "c2_dec", [64, 256], BF16)
    Tcol = _sb(nc, es2, "c2_T", [64, 1], F32)
    eTc = _sb(nc, es2, "c2_eT", [64, 1], F32)
    hi_b = _sb(nc, es2, "c2_hi", [64, 256], BF16)
    lo_f = _sb(nc, es2, "c2_lof", [64, 256], F32)
    lo_b = _sb(nc, es2, "c2_lo", [64, 256], BF16)
    dt_b = _sb(nc, es2, "c2_dtb", [64, 256], BF16)
    csT = _sb(nc, es2, "c2_csT", [128, 128], F32)
    dtT = _sb(nc, es2, "c2_dtT", [128, 128], BF16)
    decT = _sb(nc, es2, "c2_decT", [128, 128], BF16)
    s_id = _sb(nc, es2, "c2_id", [128, 128], BF16)
    s_oh64 = _sb(nc, es2, "c2_oh64", [64, 64 * 128], BF16)
    s_ohp = _sb(nc, es2, "c2_ohp", [64, 32 * 128], BF16)
    s_o64 = _sb(nc, es2, "c2_o64", [64, 128], BF16)
    s_m0 = _sb(nc, es2, "c2_m0", [128, 256], BF16)
    s_m1 = _sb(nc, es2, "c2_m1", [128, 128], BF16)
    s_onec = _sb(nc, es2, "c2_oc", [128, 1], BF16)
    aCol = _sb(nc, es2, "c2_acol", [64, 1], F32)
    dtbc = _sb(nc, es2, "c2_dtbc", [64, 1], F32)
    dxc = _sb(nc, es2, "c2_dx", [128, 32], F32)
    nwc = _sb(nc, es2, "c2_nw", [128, 32], F32)
    scT = _sb(nc, es2, "c2_scT", [128, 4 * 512], BF16)
    xT = _sb(nc, es2, "c2_xT", [128, 256], BF16)
    xdtT = _sb(nc, es2, "c2_xdtT", [128, 256], BF16)
    xddT = _sb(nc, es2, "c2_xddT", [128, 256], BF16)
    BT = _sb(nc, es2, "c2_BT", [128, 4 * 256], BF16)
    LTe = _sb(nc, es2, "c2_LTe", [128, 256], F32)
    LTm = _sb(nc, es2, "c2_LTm", [128, 256], BF16)
    slt = _sb(nc, es2, "c2_slt", [128, 384], BF16)
    od_sb = _sb(nc, es2, "c2_odsb", [128, 256], F32)
    ytmp = _sb(nc, es2, "c2_yt", [128, 256], F32)
    ydx = _sb(nc, es2, "c2_ydx", [128, 256], F32)
    ysum = _sb(nc, es2, "c2_ys", [128, 256], F32)
    zp_b = _sb(nc, es2, "c2_zp", [128, 256], BF16)
    sz = _sb(nc, es2, "c2_sz", [128, 256], F32)
    yg = _sb(nc, es2, "c2_yg", [128, 256], F32)
    sqb = _sb(nc, es2, "c2_sq", [128, 256], BF16)
    ynt = _sb(nc, es2, "c2_ynt", [128, 256], BF16)
    afr = _sb(nc, es2, "c2_afr", [64, 128], BF16)
    Sf = _sb(nc, es2, "c2_Sf", [128, 32 * 128], F32)
    Sb = _sb(nc, es2, "c2_Sb", [128, 32 * 128], BF16)
    St = _sb(nc, es2, "c2_St", [128, 128], F32)
    ssqrow = _sb(nc, es2, "c2_ssq", [1, 2048], F32)
    pY = _ps(nc, es2, "c2_pY", [128, 512])
    pod = _ps(nc, es2, "c2_pod", [128, 512])
    pca = _ps(nc, es2, "c2_pca", [128, 512])
    ptr = [es2.enter_context(nc.psum_tensor(f"c2_ptr{i}", [128, 512], BF16)) for i in range(2)]
    psq = _ps(nc, es2, "c2_psq", [1, 512])
    psc = _ps(nc, es2, "c2_psc", [128, 512])

    sl_xbcr = k.dslot()
    sl_z = k.dslot()
    sl_yo = k.dslot()
    prev_sse = None
    prev_esz = None

    ev0 = k.op("sync", lambda e: e.dma_start(out=cwc[:], in_=convw), inc=16)
    ev1 = k.op("sync", lambda e: e.dma_start(out=cbc[:], in_=convb), inc=16)
    ev2 = k.op("sync", lambda e: e.dma_start(out=dtbc[:], in_=dtb), inc=16)
    ev3 = k.op("sync", lambda e: e.dma_start(out=aCol[:], in_=alog), inc=16)
    ev4 = k.op("sync", lambda e: e.dma_start(out=dxc[:], in_=dexp), inc=16)
    ev5 = k.op("sync", lambda e: e.dma_start(out=nwc[:], in_=nwv), inc=16)
    ev6 = k.op("sync", lambda e: e.dma_start(out=s_id[:], in_=idn), inc=16)
    ev7 = k.op("sync", lambda e: e.dma_start(out=s_oh64[:], in_=oh64), inc=16)
    ev8 = k.op("sync", lambda e: e.dma_start(out=s_ohp[:], in_=ohpair), inc=16)
    ev9 = k.op("sync", lambda e: e.dma_start(out=s_o64[:], in_=ones64), inc=16)
    eva = k.op("sync", lambda e: e.dma_start(out=s_m0[:], in_=mask0), inc=16)
    evb = k.op("sync", lambda e: e.dma_start(out=s_m1[:], in_=mask1), inc=16)
    evc = k.op("sync", lambda e: e.dma_start(out=s_onec[:], in_=io["ones_col"]), inc=16)
    evA1 = k.op("scalar", lambda e: e.activation(out=eTc[:], in_=aCol[:], func=AF.Exp), waits=(ev3,))
    evA2 = k.op("vector", lambda e: e.tensor_scalar(
        out=aCol[:], in0=eTc[:], scalar1=-1.0, scalar2=None, op0=ALU.mult), waits=(evA1,))
    evz0 = k.op("vector", lambda e: e.memset(Sf[:], 0.0), waits=())
    evz1 = k.op("vector", lambda e: e.memset(Sb[:], 0.0), waits=())
    evz2 = k.op("vector", lambda e: e.memset(xbcr[:], 0.0), waits=())
    evz3 = k.op("vector", lambda e: e.memset(ssqrow[:], 0.0), waits=())
    k.fence_all()

    for q2 in range(4):
        col0 = 512 * q2 - 3
        if q2 == 0:
            ap_in = bass.AP(zx_d.tensor, 32 * 128 * TT, [[TT, 128], [128 * TT, 41], [1, 512]])
            ev_xr = k.op("sync", lambda e, ap_in=ap_in: e.dma_start(
                out=bass.AP(xbcr, 3, [[41 * 515, 128], [515, 41], [1, 512]]), in_=ap_in),
                inc=16, waits=(evz2, k.last["vector"]), slot=sl_xbcr)
        else:
            ap_in = bass.AP(zx_d.tensor, 32 * 128 * TT + col0, [[TT, 128], [128 * TT, 41], [1, 515]])
            ev_xr = k.op("sync", lambda e, ap_in=ap_in: e.dma_start(
                out=bass.AP(xbcr, 0, [[41 * 515, 128], [515, 41], [1, 515]]), in_=ap_in),
                inc=16, waits=(k.last["vector"],), slot=sl_xbcr)
        conv_evs = []
        prev_c1 = prev_c2 = None
        for ci in range(40):
            evt0 = k.op("vector", lambda e, ci=ci: e.tensor_scalar(
                out=ct1[:], in0=bass.AP(xbcr, ci * 515 + 0, [[41 * 515, 128], [1, 512]]),
                scalar1=cwc[:, ci * 4: ci * 4 + 1], scalar2=None, op0=ALU.mult),
                waits=(ev_xr, ev0, prev_c1))
            evt1 = k.op("vector", lambda e, ci=ci: e.scalar_tensor_tensor(
                out=ct2[:], in0=bass.AP(xbcr, ci * 515 + 1, [[41 * 515, 128], [1, 512]]),
                scalar=cwc[:, ci * 4 + 1: ci * 4 + 2], in1=ct1[:], op0=ALU.mult, op1=ALU.add),
                waits=(evt0, prev_c2))
            evt2 = k.op("vector", lambda e, ci=ci: e.scalar_tensor_tensor(
                out=ct1[:], in0=bass.AP(xbcr, ci * 515 + 2, [[41 * 515, 128], [1, 512]]),
                scalar=cwc[:, ci * 4 + 2: ci * 4 + 3], in1=ct2[:], op0=ALU.mult, op1=ALU.add),
                waits=(evt1,))
            evt3 = k.op("vector", lambda e, ci=ci: e.scalar_tensor_tensor(
                out=ct2[:], in0=bass.AP(xbcr, ci * 515 + 3, [[41 * 515, 128], [1, 512]]),
                scalar=cwc[:, ci * 4 + 3: ci * 4 + 4], in1=ct1[:], op0=ALU.mult, op1=ALU.add),
                waits=(evt2,))
            prev_c1 = evt2
            evs = k.op("scalar", lambda e, ci=ci: e.activation(
                out=xbcc[:, ci * 512:(ci + 1) * 512], in_=ct2[:], func=AF.Silu,
                bias=cbc[:, ci:ci + 1]), waits=(evt3, ev1))
            prev_c2 = evs
            conv_evs.append(evs)
        evd0 = k.op("vector", lambda e, q2=q2: e.tensor_scalar(
            out=dtc[:, q2 * 1024: q2 * 1024 + 512],
            in0=bass.AP(xbcr, 40 * 515 + 3, [[41 * 515, 64], [1, 512]]),
            scalar1=dtbc[:, 0:1], scalar2=None, op0=ALU.add), waits=(ev_xr, ev2))
        evd1 = k.op("vector", lambda e, q2=q2: e.tensor_scalar(
            out=dtc[:, q2 * 1024 + 512: q2 * 1024 + 1024],
            in0=dtc[:, q2 * 1024: q2 * 1024 + 512],
            scalar1=0.0, scalar2=None, op0=ALU.max), waits=(evd0,))
        evd1b = k.op("vector", lambda e, q2=q2: e.scalar_tensor_tensor(
            out=ct1[0:64, 0:512], in0=dtc[0:64, q2 * 1024 + 512: q2 * 1024 + 1024],
            scalar=-2.0, in1=dtc[0:64, q2 * 1024: q2 * 1024 + 512],
            op0=ALU.mult, op1=ALU.add), waits=(evd1,))
        evd2 = k.op("scalar", lambda e: e.activation(
            out=ct2[0:64, 0:512], in_=ct1[0:64, 0:512], func=AF.Exp), waits=(evd1b,))
        evd3 = k.op("scalar", lambda e: e.activation(
            out=ct1[0:64, 0:512], in_=ct2[0:64, 0:512], func=AF.Ln, bias=1.0), waits=(evd2,))
        evd4 = k.op("vector", lambda e, q2=q2: e.tensor_tensor(
            out=dtc[:, q2 * 1024: q2 * 1024 + 512],
            in0=dtc[:, q2 * 1024 + 512: q2 * 1024 + 1024],
            in1=ct1[0:64, 0:512], op=ALU.add), waits=(evd3,))
        evd4 = k.echo("vector")

        for qq in range(2):
            q = 2 * q2 + qq
            lsl = qq * 256
            e_dA = k.op("vector", lambda e, q2=q2, lsl=lsl: e.tensor_scalar(
                out=dAf[:], in0=dtc[0:64, q2 * 1024 + lsl: q2 * 1024 + lsl + 256],
                scalar1=aCol[:, 0:1], scalar2=None, op0=ALU.mult), waits=(evd4, evA2))
            cur, oth = 0, 1
            e_cp = k.op("vector", lambda e: e.tensor_copy(out=csf[0][:], in_=dAf[:]), waits=(e_dA,))
            prev = e_cp
            sft = 1
            while sft < 256:
                e_s1 = k.op("vector", lambda e, cur=cur, oth=oth, sft=sft: e.tensor_tensor(
                    out=csf[oth][:, sft:256], in0=csf[cur][:, sft:256],
                    in1=csf[cur][:, 0:256 - sft], op=ALU.add), waits=(prev,))
                e_s2 = k.op("vector", lambda e, cur=cur, oth=oth, sft=sft: e.tensor_copy(
                    out=csf[oth][:, 0:sft], in_=csf[cur][:, 0:sft]), waits=(e_s1,))
                prev = e_s2
                cur, oth = oth, cur
                sft *= 2
            cs = csf[cur]
            e_od = k.op("scalar", lambda e, cur=cur: e.activation(
                out=odf[:], in_=csf[cur][:], func=AF.Exp), waits=(prev,))
            e_T = k.op("vector", lambda e, cur=cur: e.tensor_copy(
                out=Tcol[:], in_=csf[cur][:, 255:256]), waits=(prev,))
            e_d1 = k.op("vector", lambda e, cur=cur: e.tensor_scalar(
                out=lo_f[:], in0=csf[cur][:], scalar1=Tcol[:, 0:1], scalar2=None,
                op0=ALU.subtract), waits=(e_T,))
            e_dec = k.op("scalar", lambda e: e.activation(
                out=decf[:], in_=lo_f[:], func=AF.Exp, scale=-1.0), waits=(e_d1,))
            e_eT = k.op("scalar", lambda e: e.activation(
                out=eTc[:], in_=Tcol[:], func=AF.Exp), waits=(e_T,))
            e_h1 = k.op("vector", lambda e, cur=cur: e.tensor_copy(out=hi_b[:], in_=csf[cur][:]),
                        waits=(prev, k.last["scalar"]))
            e_h2 = k.op("vector", lambda e, cur=cur: e.tensor_tensor(
                out=lo_f[:], in0=csf[cur][:], in1=hi_b[:], op=ALU.subtract), waits=(e_h1, e_d1, e_dec))
            e_h3 = k.op("vector", lambda e: e.tensor_copy(out=lo_b[:], in_=lo_f[:]), waits=(e_h2,))
            e_db = k.op("vector", lambda e, q2=q2, lsl=lsl: e.tensor_copy(
                out=dt_b[:], in_=dtc[0:64, q2 * 1024 + lsl: q2 * 1024 + lsl + 256]), waits=(evd4,))
            ech = k.echo("vector")
            tr_evs = []
            for tb in range(2):
                e_t1 = k.op("tensor", lambda e, tb=tb: e.transpose(
                    out=ptr[0][0:128, 0:64], in_=_pad64(dt_b, tb), identity=s_id[0:64, 0:64]),
                    waits=(ech, ev6, k.last["vector"]))
                e_t1e = k.op("vector", lambda e, tb=tb: e.tensor_copy(
                    out=dtT[:, tb * 64: tb * 64 + 64], in_=ptr[0][0:128, 0:64]), waits=(e_t1,))
                e_t2 = k.op("tensor", lambda e, tb=tb: e.transpose(
                    out=ptr[0][0:128, 64:128], in_=_pad64(decf, tb), identity=s_id[0:64, 0:64]),
                    waits=(e_t1e,))
                e_t2e = k.op("vector", lambda e, tb=tb: e.tensor_copy(
                    out=decT[:, tb * 64: tb * 64 + 64], in_=ptr[0][0:128, 64:128]), waits=(e_t2,))
                e_t3 = k.op("tensor", lambda e, tb=tb: e.transpose(
                    out=ptr[0][0:128, 128:192], in_=_pad64(hi_b, tb), identity=s_id[0:64, 0:64]),
                    waits=(e_t2e, e_h1))
                e_t3e = k.op("vector", lambda e, tb=tb: e.tensor_copy(
                    out=csT[:, tb * 64: tb * 64 + 64], in_=ptr[0][0:128, 128:192]), waits=(e_t3,))
                e_t4 = k.op("tensor", lambda e, tb=tb: e.transpose(
                    out=ptr[0][0:128, 192:256], in_=_pad64(lo_b, tb), identity=s_id[0:64, 0:64]),
                    waits=(e_t3e, e_h3))
                e_t4e = k.op("vector", lambda e, tb=tb: e.tensor_tensor(
                    out=csT[:, tb * 64: tb * 64 + 64], in0=csT[:, tb * 64: tb * 64 + 64],
                    in1=ptr[0][0:128, 192:256], op=ALU.add), waits=(e_t4,))
                tr_evs.append(k.echo("vector"))
            sc_evs = []
            bt_evs = []
            for g in range(4):
                srow_ev = None
                for sb_ in range(2):
                    e_mm = k.op("tensor", lambda e, g=g, sb_=sb_, lsl=lsl: e.matmul(
                        out=psc[:, 0:256], lhsT=xbcc[:, (32 + g) * 512 + lsl + sb_ * 128:
                                                     (32 + g) * 512 + lsl + sb_ * 128 + 128],
                        rhs=xbcc[:, (36 + g) * 512 + lsl: (36 + g) * 512 + lsl + 256],
                        start=True, stop=True), waits=conv_evs + [k.last["vector"]])
                    e_ev = k.op("vector", lambda e, g=g, sb_=sb_: e.tensor_copy(
                        out=scT[:, g * 512 + sb_ * 256: g * 512 + sb_ * 256 + 256],
                        in_=psc[:, 0:256]), waits=(e_mm,))
                    srow_ev = k.echo("vector")
                sc_evs.append(srow_ev)
                for tb in range(2):
                    e_bt = k.op("tensor", lambda e, g=g, tb=tb, lsl=lsl: e.transpose(
                        out=ptr[1][0:128, 0:128],
                        in_=xbcc[:, (32 + g) * 512 + lsl + tb * 128: (32 + g) * 512 + lsl + tb * 128 + 128],
                        identity=s_id[:]), waits=conv_evs + [k.last["vector"], ev6])
                    e_bte = k.op("vector", lambda e, g=g, tb=tb: e.tensor_copy(
                        out=BT[:, g * 256 + tb * 128: g * 256 + tb * 128 + 128],
                        in_=ptr[1][0:128, 0:128]), waits=(e_bt,))
                    bt_evs.append(k.echo("vector"))

            for hp in range(32):
                g = hp // 8
                xsl = (0 + hp) * 512 + lsl
                exts = []
                for tb in range(2):
                    e_x = k.op("tensor", lambda e, hp=hp, tb=tb, xsl=xsl: e.transpose(
                        out=ptr[0][0:128, 0:128],
                        in_=xbcc[:, xsl + tb * 128: xsl + tb * 128 + 128],
                        identity=s_id[:]), waits=conv_evs + [k.last["vector"]])
                    e_xe = k.op("vector", lambda e, tb=tb: e.tensor_copy(
                        out=xT[:, tb * 128: tb * 128 + 128], in_=ptr[0][0:128, 0:128]), waits=(e_x,))
                    exts.append(k.echo("vector"))
                dt_view = bass.AP(dtT, 2 * hp, [[128, 128], [64, 2], [1, 2], [0, 64]])
                dec_view = bass.AP(decT, 2 * hp, [[128, 128], [64, 2], [1, 2], [0, 64]])
                e_xd = k.op("vector", lambda e, dt_view=dt_view: e.tensor_tensor(
                    out=xdtT[:], in0=xT[:], in1=dt_view, op=ALU.mult), waits=exts + tr_evs)
                e_xdd = k.op("vector", lambda e, dec_view=dec_view: e.tensor_tensor(
                    out=xddT[:], in0=xdtT[:], in1=dec_view, op=ALU.mult), waits=(e_xd,))
                e_xde = k.echo("vector")
                y_mms = []
                for hh in range(2):
                    h = 2 * hp + hh
                    e_c1 = k.op("tensor", lambda e, h=h: e.matmul(
                        out=pod[:, 256:512], lhsT=s_oh64[:, h * 128:(h + 1) * 128],
                        rhs=hi_b[:], start=True, stop=False), waits=(ev7, e_h1, k.last["vector"]))
                    e_c2 = k.op("tensor", lambda e, h=h: e.matmul(
                        out=pod[:, 256:512], lhsT=s_oh64[:, h * 128:(h + 1) * 128],
                        rhs=lo_b[:], start=False, stop=True), waits=(e_h3,))
                    for sb_ in range(2):
                        lrange = 256 - sb_ * 128
                        cs_col = bass.AP(csT, sb_ * 64 + 2 * hp + hh, [[128, 128], [1, 1]])
                        e_l1 = k.op("vector", lambda e, sb_=sb_, cs_col=cs_col, lrange=lrange: e.tensor_scalar(
                            out=LTe[:, 0:lrange], in0=pod[:, 256 + sb_ * 128: 512],
                            scalar1=cs_col, scalar2=0.0, op0=ALU.subtract, op1=ALU.min),
                            waits=(e_c2,) + tuple(tr_evs))
                        e_l2 = k.op("scalar", lambda e, lrange=lrange: e.activation(
                            out=LTe[:, 0:lrange] if False else ct1[:, 0:lrange],
                            in_=LTe[:, 0:lrange], func=AF.Exp), waits=(e_l1,))
                        msk = s_m0 if sb_ == 0 else s_m1
                        e_l3 = k.op("vector", lambda e, msk=msk, lrange=lrange: e.tensor_tensor(
                            out=LTm[:, 0:lrange], in0=ct1[:, 0:lrange], in1=msk[:, 0:lrange],
                            op=ALU.mult), waits=(e_l2,))
                        e_l4 = k.op("vector", lambda e, g=g, sb_=sb_, lrange=lrange: e.tensor_tensor(
                            out=slt[:, 0:lrange],
                            in0=bass.AP(scT, g * 512 + sb_ * 256 + sb_ * 128, [[4 * 512, 128], [1, lrange]]),
                            in1=LTm[:, 0:lrange], op=ALU.mult), waits=(e_l3,) + tuple(sc_evs))
                        e_l4e = k.echo("vector")
                        mmw = [e_l4e, e_xde, k.last["vector"]]
                        fn = lambda e, hh=hh, sb_=sb_, lrange=lrange: e.matmul(
                            out=pY[hh * 64: hh * 64 + 64, (256 - lrange):256],
                            lhsT=xdtT[:, sb_ * 128 + hh * 64: sb_ * 128 + hh * 64 + 64],
                            rhs=slt[:, 0:lrange], start=(sb_ == 0), stop=(sb_ == 1))
                        ev_y = k.op("tensor", fn, waits=mmw)
                        y_mms.append(ev_y)
                    ev_yo = k.op("tensor", lambda e, hp=hp, hh=hh, g=g, lsl=lsl: e.matmul(
                        out=pY[hh * 64: hh * 64 + 64, 256:512],
                        lhsT=Sb[:, hp * 128 + hh * 64: hp * 128 + hh * 64 + 64],
                        rhs=xbcc[:, (36 + g) * 512 + lsl: (36 + g) * 512 + lsl + 256],
                        start=True, stop=True), waits=(k.last["vector"],))
                    y_mms.append(ev_yo)
                e_op = k.op("tensor", lambda e, hp=hp: e.matmul(
                    out=pod[:, 0:256], lhsT=s_ohp[:, hp * 128:(hp + 1) * 128],
                    rhs=odf[:], start=True, stop=True), waits=(ev8, e_od, k.last["vector"]))
                e_ope = k.op("vector", lambda e: e.tensor_copy(out=od_sb[:], in_=pod[:, 0:256]),
                             waits=(e_op,))
                e_y1 = k.op("vector", lambda e: e.tensor_tensor(
                    out=ytmp[:], in0=pY[:, 256:512], in1=od_sb[:], op=ALU.mult),
                    waits=y_mms + [e_ope])
                e_y2 = k.op("vector", lambda e, hp=hp, xsl=xsl: e.scalar_tensor_tensor(
                    out=ydx[:], in0=xbcc[:, xsl: xsl + 256], scalar=dxc[:, hp:hp + 1],
                    in1=pY[:, 0:256], op0=ALU.mult, op1=ALU.add), waits=y_mms + [ev4])
                e_y3 = k.op("vector", lambda e: e.tensor_tensor(
                    out=ysum[:], in0=ydx[:], in1=ytmp[:], op=ALU.add), waits=(e_y1, e_y2))
                e_y3e = k.echo("vector")
                e_z = k.op("sync", lambda e, hp=hp, q=q: e.dma_start(
                    out=zp_b[:], in_=zx_d[hp * 128:(hp + 1) * 128, q * 256:(q + 1) * 256]),
                    inc=16, waits=(prev_esz,), slot=sl_z)
                e_sz = k.op("scalar", lambda e: e.activation(out=sz[:], in_=zp_b[:], func=AF.Silu),
                            waits=(e_z,))
                prev_esz = e_sz
                e_g = k.op("vector", lambda e: e.tensor_tensor(
                    out=yg[:], in0=ysum[:], in1=sz[:], op=ALU.mult), waits=(e_y3e, e_sz))
                e_q = k.op("scalar", lambda e: e.activation(out=sqb[:], in_=yg[:], func=AF.Square),
                           waits=(e_g,))
                e_sq = k.op("tensor", lambda e, hp=hp, q=q: e.matmul(
                    out=psq[0:1, (q % 2) * 256: (q % 2) * 256 + 256], lhsT=s_onec[:], rhs=sqb[:],
                    start=(hp == 0), stop=(hp == 31)),
                    waits=(e_q, evc) + ((prev_sse,) if hp == 0 else ()))
                e_nw = k.op("vector", lambda e, hp=hp: e.tensor_scalar(
                    out=ynt[:], in0=yg[:], scalar1=nwc[:, hp:hp + 1], scalar2=None,
                    op0=ALU.mult), waits=(e_g, ev5))
                e_nwe = k.echo("vector")
                e_yno = k.op("gpsimd", lambda e, hp=hp, q=q: e.dma_start(
                    out=ynw_d[hp * 128:(hp + 1) * 128, q * 256:(q + 1) * 256], in_=ynt[:]),
                    waits=(e_nwe,), inc=16, slot=sl_yo)
                con_mm = None
                for tb in range(2):
                    fn = lambda e, g=g, tb=tb: e.matmul(
                        out=pca[:, 0:128], lhsT=BT[:, g * 256 + tb * 128: g * 256 + tb * 128 + 128],
                        rhs=xddT[:, tb * 128: tb * 128 + 128],
                        start=(tb == 0), stop=(tb == 1))
                    w = [e_xde] + bt_evs + [k.last["vector"]]
                    if tb == 1:
                        con_mm = k.op("tensor", fn, waits=w)
                    else:
                        k.raw("tensor", fn, waits=w)
                e_af1 = k.op("vector", lambda e, hp=hp: e.tensor_scalar(
                    out=afr[:], in0=s_ohp[:, hp * 128:(hp + 1) * 128],
                    scalar1=eTc[:, 0:1], scalar2=None, op0=ALU.mult), waits=(e_eT, ev8))
                e_af1e = k.echo("vector")
                e_af2 = k.op("tensor", lambda e: e.matmul(
                    out=pca[:, 128:256], lhsT=s_o64[:], rhs=afr[:],
                    start=True, stop=True), waits=(e_af1e, ev9))
                e_s1 = k.op("vector", lambda e, hp=hp: e.tensor_tensor(
                    out=St[:], in0=Sf[:, hp * 128:(hp + 1) * 128], in1=pca[:, 128:256],
                    op=ALU.mult), waits=(e_af2, evz0))
                e_s2 = k.op("vector", lambda e, hp=hp: e.tensor_tensor(
                    out=Sf[:, hp * 128:(hp + 1) * 128], in0=St[:], in1=pca[:, 0:128],
                    op=ALU.add), waits=(e_s1, con_mm))
                e_s3 = k.op("vector", lambda e, hp=hp: e.tensor_copy(
                    out=Sb[:, hp * 128:(hp + 1) * 128], in_=Sf[:, hp * 128:(hp + 1) * 128]),
                    waits=(e_s2,))
                k.echo("vector")
            e_se = k.op("vector", lambda e, q=q: e.tensor_copy(
                out=ssqrow[:, q * 256:(q + 1) * 256],
                in_=psq[0:1, (q % 2) * 256: (q % 2) * 256 + 256]), waits=(k.last["tensor"],))
            prev_sse = k.echo("vector")
    if part2:
        k.op("gpsimd", lambda e: e.dma_start(out=o_ssq[0:1, :], in_=ssqrow[:, 0:1024]),
             waits=(k.last["vector"],), inc=16)
        ev_sso = k.op("gpsimd", lambda e: e.dma_start(out=o_ssq[1:2, :], in_=ssqrow[:, 1024:2048]),
                      waits=(k.last["vector"],), inc=16)
    else:
        ev_sso = k.op("gpsimd", lambda e: e.dma_start(out=o_ssq, in_=ssqrow[:]),
                      waits=(k.last["vector"],), inc=16)
    k.fence_all()
    es2.close()

    k.set_scope('C3')
    es3 = contextlib.ExitStack()
    ynall = _sb(nc, es3, "c3_yall", [128, 32 * TT], BF16)
    ws3 = WChunk(k, nc, es3, "c3_w", woutT, 256, 4)
    pot = _sb(nc, es3, "c3_ot", [128, 512], F32)
    ps3 = [_ps(nc, es3, f"c3_p{i}", [128, 512]) for i in range(8)]
    sl_yn = k.dslot()
    sl_po = k.dslot()
    y_evs = []
    prev_yl = None
    for kc in range(32):
        ev_y = k.op("sync", lambda e, kc=kc: e.dma_start(
            out=ynall[:, kc * TT:(kc + 1) * TT], in_=ynw_d[kc * 128:(kc + 1) * 128, :]), inc=16,
            waits=(prev_yl,), slot=sl_yn)
        prev_yl = ev_y
        y_evs.append(ev_y)
    bank_free3 = [None] * 8
    prev_pot = None
    for mcg in range(16):
        mm_done = [None] * 8
        for kc in range(32):
            wtile, ev_w = ws3.get()
            for mc2 in range(2):
                for tg in range(4):
                    b = mc2 * 4 + tg
                    fn = lambda e, wtile=wtile, kc=kc, mc2=mc2, tg=tg, b=b: e.matmul(
                        out=ps3[b][:], lhsT=wtile[:, mc2 * 128: mc2 * 128 + 128],
                        rhs=ynall[:, kc * TT + tg * 512: kc * TT + tg * 512 + 512],
                        start=(kc == 0), stop=(kc == 31))
                    w = [ev_w, y_evs[kc]]
                    if kc == 0:
                        w += [bank_free3[b]]
                    if mc2 == 1 and tg == 3:
                        mm_done[b] = k.op("tensor", fn, waits=w)
                    else:
                        k.raw("tensor", fn, waits=w)
            ws3.done(k.last["tensor"])
        last_mm = k.last["tensor"]
        for mc2 in range(2):
            for tg in range(4):
                b = mc2 * 4 + tg
                mc = mcg * 2 + mc2
                ev_e = k.op("vector", lambda e, b=b: e.tensor_copy(
                    out=pot[:], in_=ps3[b][:]), waits=(last_mm, prev_pot))
                ev_ee = k.echo("vector")
                bank_free3[b] = ev_ee
                if part2:
                    r0 = (tg // 2) * 4096 + mc * 128
                    c0 = (tg % 2) * 512
                    ev_o = k.op("gpsimd", lambda e, r0=r0, c0=c0: e.dma_start(
                        out=o_part[r0:r0 + 128, c0:c0 + 512], in_=pot[:]),
                        waits=(ev_e,), inc=16, slot=sl_po)
                else:
                    ev_o = k.op("gpsimd", lambda e, mc=mc, tg=tg: e.dma_start(
                        out=o_part[mc * 128:(mc + 1) * 128, tg * 512:(tg + 1) * 512], in_=pot[:]),
                        waits=(ev_e,), inc=16, slot=sl_po)
                prev_pot = ev_o
    k.fence_all()
    es3.close()
    return ev_sso


def _pad64(t, tb):
    return t[0:64, tb * 128: tb * 128 + 128]


def build_L4():
    nc = bass.Bass("TRN2", target_bir_lowering=False)
    es = contextlib.ExitStack()
    io = dict(
        x2a=nc.dram_tensor("x2a", [D, TOK], F32, kind="ExternalInput").ap(),
        x2b=nc.dram_tensor("x2b", [D, TOK], F32, kind="ExternalInput").ap(),
        wipT=nc.dram_tensor("wipT", [296 * 128, 1024], F32, kind="ExternalInput").ap(),
        convw=nc.dram_tensor("convw", [128, 160], F32, kind="ExternalInput").ap(),
        convb=nc.dram_tensor("convb", [128, 40], F32, kind="ExternalInput").ap(),
        dtb=nc.dram_tensor("dtb", [64, 1], F32, kind="ExternalInput").ap(),
        alog=nc.dram_tensor("alog", [64, 1], F32, kind="ExternalInput").ap(),
        dexp=nc.dram_tensor("dexp", [128, 32], F32, kind="ExternalInput").ap(),
        nw=nc.dram_tensor("nw", [128, 32], F32, kind="ExternalInput").ap(),
        woutT=nc.dram_tensor("woutT", [128 * 128, 1024], F32, kind="ExternalInput").ap(),
        oh64=nc.dram_tensor("oh64", [64, 64 * 128], BF16, kind="ExternalInput").ap(),
        ohpair=nc.dram_tensor("ohpair", [64, 32 * 128], BF16, kind="ExternalInput").ap(),
        ones64=nc.dram_tensor("ones64", [64, 128], BF16, kind="ExternalInput").ap(),
        mask0=nc.dram_tensor("mask0", [128, 256], BF16, kind="ExternalInput").ap(),
        mask1=nc.dram_tensor("mask1", [128, 128], BF16, kind="ExternalInput").ap(),
        idn=nc.dram_tensor("idn", [128, 128], BF16, kind="ExternalInput").ap(),
        ones_col=nc.dram_tensor("ones_col", [128, 1], BF16, kind="ExternalInput").ap(),
        o_part=nc.dram_tensor("o_part", [D, 2048], F32, kind="ExternalOutput").ap(),
        o_ssq=nc.dram_tensor("o_ssq", [1, 2048], F32, kind="ExternalOutput").ap(),
        zx_d=nc.dram_tensor("zx_d", [9344, 2048], BF16, kind="Internal").ap(),
        ynw_d=nc.dram_tensor("ynw_d", [4096, 2048], BF16, kind="Internal").ap(),
    )
    k = KB(nc, es)
    k.init_echo(nc, es)
    blk = es.enter_context(nc.Block())
    ev = emit_C(k, nc, es, io)
    k.fence_all()
    k.build(blk)
    es.close()
    return nc


def host_prep_C(inputs):
    ipw = inputs["L1_attn_in_proj_w"]
    convw = inputs["L1_attn_conv_w"]; convb = inputs["L1_attn_conv_b"]
    dtb = inputs["L1_attn_dt_bias"]; alog = inputs["L1_attn_A_log"]
    Dh = inputs["L1_attn_D"]; nw = inputs["L1_attn_norm_w"]
    wout = inputs["L1_attn_out_proj_w"]
    DI = 8192
    per_eta = []
    for eta in range(2):
        rows = np.concatenate([
            4096 * eta + np.arange(4096),
            DI + 4096 * eta + np.arange(4096),
            2 * DI + 512 * eta + np.arange(512),
            2 * DI + 1024 + 512 * eta + np.arange(512),
            2 * DI + 2048 + 64 * eta + np.arange(64),
        ])
        wip = np.zeros((D, 9344), dtype=np.float32)
        wip[:, 0:9280] = ipw[rows][:, PERM_D].T
        convsel = np.concatenate([4096 * eta + np.arange(4096),
                                  DI + 512 * eta + np.arange(512),
                                  DI + 1024 + 512 * eta + np.arange(512)])
        cw = convw[convsel].reshape(40, 128, 4).transpose(1, 0, 2).reshape(128, 160)
        cb = convb[convsel].reshape(40, 128).T
        woutT = np.ascontiguousarray(wout[PERM_D][:, 4096 * eta: 4096 * eta + 4096].T)
        wip_tiles = []
        for ocg in range(37):
            noc = 2 if ocg < 36 else 1
            for kc in range(32):
                wip_tiles.append(wip[kc * 128:(kc + 1) * 128, ocg * 256: ocg * 256 + 128 * noc])
        wout_tiles = [woutT[kc * 128:(kc + 1) * 128, mcg * 256: mcg * 256 + 256]
                      for mcg in range(16) for kc in range(32)]
        per_eta.append(dict(
            wipT=pack_tiles(wip_tiles, 256, 4),
            convw=np.ascontiguousarray(cw), convb=np.ascontiguousarray(cb),
            dtb=np.ascontiguousarray(dtb[64 * eta:64 * eta + 64].reshape(-1, 1)),
            alog=np.ascontiguousarray(alog[64 * eta:64 * eta + 64].reshape(-1, 1)),
            dexp=np.ascontiguousarray(np.repeat(Dh[64 * eta:64 * eta + 64], 64).reshape(32, 128).T).astype(np.float32),
            nw=np.ascontiguousarray(nw[4096 * eta:4096 * eta + 4096].reshape(32, 128).T),
            woutT=pack_tiles(wout_tiles, 256, 4),
        ))
    p = np.arange(128)
    oh64 = np.zeros((64, 64, 128), dtype=BF)
    for h in range(64):
        oh64[h, h, :] = 1
    oh64 = np.ascontiguousarray(oh64.transpose(1, 0, 2).reshape(64, 64 * 128))
    ohpair = np.zeros((64, 32, 128), dtype=BF)
    for hp in range(32):
        ohpair[2 * hp, hp, 0:64] = 1
        ohpair[2 * hp + 1, hp, 64:128] = 1
    ohpair = np.ascontiguousarray(ohpair.transpose(1, 0, 2).reshape(64, 32 * 128))
    ohpair2 = np.zeros((64, 32 * 128), dtype=BF)
    for hp in range(32):
        ohpair2[2 * hp, hp * 128: hp * 128 + 64] = 1
        ohpair2[2 * hp + 1, hp * 128 + 64: hp * 128 + 128] = 1
    oh64b = np.zeros((64, 64 * 128), dtype=BF)
    for h in range(64):
        oh64b[h, h * 128:(h + 1) * 128] = 1
    ones64 = np.ones((64, 128), dtype=BF)
    sidx = np.arange(128)
    m1 = (sidx[None, :] >= sidx[:, None]).astype(BF)
    mask0 = np.concatenate([m1, np.ones((128, 128), dtype=BF)], axis=1)
    shared = dict(oh64=oh64b, ohpair=ohpair2, ones64=ones64,
                  mask0=np.ascontiguousarray(mask0), mask1=np.ascontiguousarray(m1),
                  idn=np.eye(128, dtype=BF), ones_col=np.ones((128, 1), dtype=BF))
    return per_eta, shared


def emit_D(k, nc, es, io):
    partA = io["partA"]; partB = io.get("partB")
    ssqA = io["ssqA"]; ssqB = io.get("ssqB"); x2f = io["x2f"]
    wT = io["wT"]; bcol_d = io["bcol"]; wrT = io["wrT"]; aT = io["aT"]
    bmT = io["bmT"]; repmat_d = io["repmat"]
    ones41_d = io["ones41"]; ones132_d = io["ones132"]; onec_d = io["ones_col"]
    drow_d = io["drow_d"]; lnrow_d = io["lnrow_d"]
    x3pre_d = io["x3pre_d"]; x3_d = io["x3_d"]; x4pre_d = io["x4pre_d"]
    o_out = io["o_out"]

    k.set_scope('D0')
    es0 = contextlib.ExitStack()
    pAt = _sb(nc, es0, "d0_pa", [128, TOK], F32)
    pBt = _sb(nc, es0, "d0_pb", [128, TOK], F32)
    x2t = _sb(nc, es0, "d0_x2", [128, TOK], F32)
    t1 = _sb(nc, es0, "d0_t1", [128, TOK], F32)
    srow2 = _sb(nc, es0, "d0_sr", [1, 4 * 1024], F32)
    rrep = _sb(nc, es0, "d0_rr", [128, TOK], F32)
    sl_a = k.dslot(); sl_b = k.dslot(); sl_x = k.dslot()
    ev_sa = k.op("sync", lambda e: e.dma_start(out=srow2[:, 0:1024], in_=ssqA), inc=16)
    if ssqB is not None:
        ev_sb = k.op("sync", lambda e: e.dma_start(out=srow2[:, 1024:2048], in_=ssqB), inc=16)
        e1 = k.op("vector", lambda e: e.tensor_tensor(
            out=srow2[:, 2048:3072], in0=srow2[:, 0:1024], in1=srow2[:, 1024:2048],
            op=ALU.add), waits=(ev_sa, ev_sb))
    else:
        e1 = k.op("vector", lambda e: e.tensor_copy(
            out=srow2[:, 2048:3072], in_=srow2[:, 0:1024]), waits=(ev_sa,))
    e2 = k.op("vector", lambda e: e.tensor_scalar(
        out=srow2[:, 3072:4096], in0=srow2[:, 2048:3072], scalar1=1.0 / 8192.0,
        scalar2=1e-5, op0=ALU.mult, op1=ALU.add), waits=(e1,))
    e3 = k.op("scalar", lambda e: e.activation(
        out=srow2[:, 2048:3072], in_=srow2[:, 3072:4096], func=AF.Sqrt), waits=(e2,))
    e4 = k.op("vector", lambda e: e.reciprocal(
        out=srow2[:, 3072:4096], in_=srow2[:, 2048:3072]), waits=(e3,))
    e5 = k.op("gpsimd", lambda e: e.dma_start(out=drow_d, in_=srow2[:, 3072:4096]),
              waits=(e4,), inc=16)
    e6 = k.op("gpsimd", lambda e: e.dma_start(
        out=rrep[:], in_=bass.AP(drow_d.tensor, 0, [[0, 128], [1, 1024]])),
        waits=(e5,), inc=16)
    pva = pvb = pvx = pvo = pv2 = pv3 = None
    for mc in range(32):
        ev_a = k.op("sync", lambda e, mc=mc: e.dma_start(
            out=pAt[:], in_=partA[mc * 128:(mc + 1) * 128, :]), inc=16,
            waits=(pv3, pvo), slot=sl_a)
        if partB is not None:
            ev_b = k.op("sync", lambda e, mc=mc: e.dma_start(
                out=pBt[:], in_=partB[mc * 128:(mc + 1) * 128, :]), inc=16,
                waits=(pv2,), slot=sl_b)
        ev_x = k.op("sync", lambda e, mc=mc: e.dma_start(
            out=x2t[:], in_=x2f[mc * 128:(mc + 1) * 128, :]), inc=16,
            waits=(pv3,), slot=sl_x)
        if partB is not None:
            ee1 = k.op("vector", lambda e: e.tensor_tensor(
                out=t1[:], in0=pAt[:], in1=pBt[:], op=ALU.add), waits=(ev_a, ev_b, pvo))
            ee2 = k.op("vector", lambda e: e.tensor_tensor(
                out=pAt[:], in0=t1[:], in1=rrep[:], op=ALU.mult), waits=(ee1, e6))
        else:
            ee2 = k.op("vector", lambda e: e.tensor_tensor(
                out=t1[:], in0=pAt[:], in1=rrep[:], op=ALU.mult), waits=(ev_a, e6, pvo))
        if partB is not None:
            ee3 = k.op("vector", lambda e: e.scalar_tensor_tensor(
                out=t1[:], in0=x2t[:], scalar=2.0, in1=pAt[:], op0=ALU.mult, op1=ALU.add),
                waits=(ee2, ev_x))
            src_t = t1
        else:
            ee3 = k.op("vector", lambda e: e.scalar_tensor_tensor(
                out=pAt[:], in0=x2t[:], scalar=2.0, in1=t1[:], op0=ALU.mult, op1=ALU.add),
                waits=(ee2, ev_x))
            src_t = pAt
        ev_o = k.op("gpsimd", lambda e, mc=mc, src_t=src_t: e.dma_start(
            out=x3pre_d[mc * 128:(mc + 1) * 128, :], in_=src_t[:]), waits=(ee3,), inc=16)
        pva, pvx, pvo = ev_a, ev_x, ev_o
        pv2, pv3 = ee2, ee3
    k.fence_all()
    emit_ln(k, nc, es0, "d_ln1", x3pre_d, x3_d, lnrow_d, onec_d)
    k.fence_all()
    es0.close()

    k.set_scope('D1')
    es1 = contextlib.ExitStack()
    x3bf = _sb(nc, es1, "d1_x3b", [128, 32 * TOK], BF16)
    xld = _sb(nc, es1, "d1_xld", [128, TOK], F32)
    wrb = _sb(nc, es1, "d1_wrb", [128, 128], BF16)
    ab_sb = _sb(nc, es1, "d1_ab", [128, 1024], BF16)
    bm_f = _sb(nc, es1, "d1_bmf", [32, 4096], F32)
    bmb = _sb(nc, es1, "d1_bmb", [32, 4096], BF16)
    bcol = _sb(nc, es1, "d1_bc", [128, 32], F32)
    rep_sb = _sb(nc, es1, "d1_rep", [4, 32], F32)
    o41 = _sb(nc, es1, "d1_o41", [4, 1], F32)
    o132 = _sb(nc, es1, "d1_o132", [1, 32], F32)
    ex_sb = _sb(nc, es1, "d1_ex", [4, 1024], F32)
    pr_sb = _sb(nc, es1, "d1_pr", [32, 1024], F32)
    sumr = _sb(nc, es1, "d1_sum", [1, 1024], F32)
    rrow = _sb(nc, es1, "d1_rro", [1, 1024], F32)
    recip32 = _sb(nc, es1, "d1_r32", [32, 1024], F32)
    exrep = _sb(nc, es1, "d1_exr", [32, 1024], F32)
    t32 = _sb(nc, es1, "d1_t32", [32, 1024], F32)
    pg_f = _sb(nc, es1, "d1_pgf", [32, 1024], F32)
    pgb = _sb(nc, es1, "d1_pgb", [32, 1024], BF16)
    hsb = _sb(nc, es1, "d1_h", [128, TOK], F32)
    x4t = _sb(nc, es1, "d1_x4", [128, TOK], F32)
    x3r = _sb(nc, es1, "d1_x3r", [128, TOK], F32)
    pg = _ps(nc, es1, "d1_ppg", [4, 512])
    pr = _ps(nc, es1, "d1_ppr", [32, 512])
    p1 = _ps(nc, es1, "d1_pp1", [1, 512])
    prep = _ps(nc, es1, "d1_pprep", [32, 512])
    pD = [_ps(nc, es1, f"d1_pD{i}", [128, 512]) for i in range(2)]
    sl_x3 = k.dslot(); sl_xr = k.dslot()

    x3_cast = []
    pv_c = None
    for mc in range(32):
        ev_d = k.op("sync", lambda e, mc=mc: e.dma_start(
            out=xld[:], in_=x3_d[mc * 128:(mc + 1) * 128, :]), inc=16,
            waits=(pv_c,), slot=sl_x3)
        ev_c = k.op("scalar", lambda e, mc=mc: e.activation(
            out=x3bf[:, mc * TOK:(mc + 1) * TOK], in_=xld[:], func=AF.Copy),
            waits=(ev_d,))
        pv_c = ev_c
        x3_cast.append(ev_c)
    e_x3 = k.echo("scalar")
    wrf = _sb(nc, es1, "d1_wrf", [128, 128], F32)
    af_f = _sb(nc, es1, "d1_aff", [128, 1024], F32)
    ev_w1 = k.op("sync", lambda e: e.dma_start(out=wrf[:], in_=wrT), inc=16)
    ev_w2 = k.op("scalar", lambda e: e.activation(out=wrb[:], in_=wrf[:], func=AF.Copy),
                 waits=(ev_w1,))
    ev_a1 = k.op("sync", lambda e: e.dma_start(out=af_f[:], in_=aT), inc=16)
    ev_a2 = k.op("scalar", lambda e: e.activation(out=ab_sb[:], in_=af_f[:], func=AF.Copy),
                 waits=(ev_a1,))
    ev_b1 = k.op("sync", lambda e: e.dma_start(out=bm_f[:], in_=bmT), inc=16)
    ev_b2 = k.op("scalar", lambda e: e.activation(out=bmb[:], in_=bm_f[:], func=AF.Copy),
                 waits=(ev_b1,))
    ev_bc = k.op("sync", lambda e: e.dma_start(out=bcol[:], in_=bcol_d), inc=16)
    ev_rp = k.op("sync", lambda e: e.dma_start(out=rep_sb[:], in_=repmat_d), inc=16)
    ev_o41 = k.op("sync", lambda e: e.dma_start(out=o41[:], in_=ones41_d), inc=16)
    ev_o132 = k.op("sync", lambda e: e.dma_start(out=o132[:], in_=ones132_d), inc=16)
    ex_evs = []
    for tb in range(2):
        mm = None
        for mc in range(32):
            fn = lambda e, mc=mc, tb=tb: e.matmul(
                out=pg[:], lhsT=wrb[:, mc * 4:(mc + 1) * 4],
                rhs=x3bf[:, mc * TOK + tb * 512: mc * TOK + tb * 512 + 512],
                start=(mc == 0), stop=(mc == 31))
            w = [ev_w2, x3_cast[mc]] + (ex_evs[-1:] if (mc == 0 and tb == 1) else [])
            if mc == 31:
                mm = k.op("tensor", fn, waits=w)
            else:
                k.raw("tensor", fn, waits=w)
        ev_ex = k.op("scalar", lambda e, tb=tb: e.activation(
            out=ex_sb[:, tb * 512:(tb + 1) * 512], in_=pg[:], func=AF.Exp), waits=(mm,))
        ex_evs.append(ev_ex)
    pr_evs = []
    for tb in range(2):
        mm = None
        for mc in range(32):
            fn = lambda e, mc=mc, tb=tb: e.matmul(
                out=pr[:], lhsT=ab_sb[:, mc * 32:(mc + 1) * 32],
                rhs=x3bf[:, mc * TOK + tb * 512: mc * TOK + tb * 512 + 512],
                start=(mc == 0), stop=(mc == 31))
            w = [ev_a2, x3_cast[mc]] + (pr_evs[-1:] if (mc == 0 and tb == 1) else [])
            if mc == 31:
                mm = k.op("tensor", fn, waits=w)
            else:
                k.raw("tensor", fn, waits=w)
        ev_pr = k.op("vector", lambda e, tb=tb: e.tensor_copy(
            out=pr_sb[:, tb * 512:(tb + 1) * 512], in_=pr[:]), waits=(mm,))
        pr_evs.append(ev_pr)
    sum_evs = []
    for tb in range(2):
        mm = k.op("tensor", lambda e, tb=tb: e.matmul(
            out=p1[:], lhsT=o41[:], rhs=ex_sb[0:4, tb * 512:(tb + 1) * 512],
            start=True, stop=True), waits=[ev_o41, ex_evs[tb]] + sum_evs[-1:])
        ev_s = k.op("vector", lambda e, tb=tb: e.tensor_copy(
            out=sumr[:, tb * 512:(tb + 1) * 512], in_=p1[:]), waits=(mm,))
        sum_evs.append(ev_s)
    ev_rc = k.op("vector", lambda e: e.reciprocal(out=rrow[:], in_=sumr[:]),
                 waits=tuple(sum_evs))
    rep_evs = []
    for tb in range(2):
        mm1 = k.op("tensor", lambda e, tb=tb: e.matmul(
            out=prep[:], lhsT=o132[:], rhs=rrow[0:1, tb * 512:(tb + 1) * 512],
            start=True, stop=True), waits=[ev_o132, ev_rc] + rep_evs[-1:])
        ev_r1 = k.op("vector", lambda e, tb=tb: e.tensor_copy(
            out=recip32[:, tb * 512:(tb + 1) * 512], in_=prep[:]), waits=(mm1,))
        mm2 = k.op("tensor", lambda e, tb=tb: e.matmul(
            out=prep[:], lhsT=rep_sb[:], rhs=ex_sb[0:4, tb * 512:(tb + 1) * 512],
            start=True, stop=True), waits=(ev_rp, ex_evs[tb], ev_r1))
        ev_r2 = k.op("vector", lambda e, tb=tb: e.tensor_copy(
            out=exrep[:, tb * 512:(tb + 1) * 512], in_=prep[:]), waits=(mm2,))
        rep_evs.append(ev_r2)
    ev_g1 = k.op("vector", lambda e: e.tensor_tensor(
        out=t32[:], in0=pr_sb[:], in1=exrep[:], op=ALU.mult),
        waits=tuple(pr_evs + rep_evs))
    ev_g2 = k.op("vector", lambda e: e.scalar_tensor_tensor(
        out=pg_f[:], in0=t32[:], scalar=2.0, in1=recip32[:], op0=ALU.mult, op1=ALU.mult),
        waits=(ev_g1,))
    ev_g3 = k.op("scalar", lambda e: e.activation(out=pgb[:], in_=pg_f[:], func=AF.Copy),
                 waits=(ev_g2,))
    e_gd = k.echo("scalar")
    ws = WChunk(k, nc, es1, "d_w", wT, 128, 8)
    evac_prev = [e_gd, e_gd]
    pv_ev = [None, None]
    pv_res = None
    pv_xr = None
    pv_o = None
    for oc in range(32):
        mm_last = [None, None]
        for ic in range(32):
            tile, evc = ws.get()
            for tb in range(2):
                fn = lambda e, tile=tile, ic=ic, tb=tb: e.matmul(
                    out=pD[tb][:], lhsT=tile,
                    rhs=x3bf[:, ic * TOK + tb * 512: ic * TOK + tb * 512 + 512],
                    start=(ic == 0), stop=False)
                w = [evc, x3_cast[ic]] + ([evac_prev[tb]] if ic == 0 else [])
                if tb == 1:
                    mm_last[1] = k.op("tensor", fn, waits=w)
                else:
                    k.raw("tensor", fn, waits=w)
            ws.done(mm_last[1])
        lo_evs = []
        for tb in range(2):
            ev_lo = k.op("tensor", lambda e, oc=oc, tb=tb: e.matmul(
                out=pD[tb][:], lhsT=bmb[:, oc * 128:(oc + 1) * 128],
                rhs=pgb[:, tb * 512:(tb + 1) * 512], start=False, stop=True),
                waits=(ev_b2, e_gd))
            lo_evs.append(ev_lo)
        ev_xr = k.op("sync", lambda e, oc=oc: e.dma_start(
            out=x3r[:], in_=x3_d[oc * 128:(oc + 1) * 128, :]), inc=16,
            waits=(pv_res,), slot=sl_xr)
        res_evs = []
        for tb in range(2):
            e_ev = k.op("vector", lambda e, oc=oc, tb=tb: e.tensor_scalar(
                out=hsb[:, tb * 512:(tb + 1) * 512], in0=pD[tb][:],
                scalar1=bcol[:, oc:oc + 1], scalar2=None, op0=ALU.add),
                waits=[lo_evs[tb], ev_bc, pv_ev[tb]])
            evac_prev[tb] = e_ev
            e_rs = k.op("vector", lambda e, tb=tb: e.tensor_tensor(
                out=x4t[:, tb * 512:(tb + 1) * 512], in0=hsb[:, tb * 512:(tb + 1) * 512],
                in1=x3r[:, tb * 512:(tb + 1) * 512], op=ALU.add),
                waits=(e_ev, ev_xr, pv_o))
            pv_ev[tb] = e_rs
            res_evs.append(e_rs)
        pv_res = res_evs[1]
        pv_o = k.op("gpsimd", lambda e, oc=oc: e.dma_start(
            out=x4pre_d[oc * 128:(oc + 1) * 128, :], in_=x4t[:]),
            waits=tuple(res_evs), inc=16)
    k.fence_all()
    es1.close()

    k.set_scope('D2')
    es2 = contextlib.ExitStack()
    emit_ln(k, nc, es2, "d_ln2", x4pre_d, o_out, lnrow_d, onec_d)
    k.fence_all()
    es2.close()


def build_L5():
    nc = bass.Bass("TRN2", target_bir_lowering=False)
    es = contextlib.ExitStack()
    io = dict(
        partA=nc.dram_tensor("partA", [D, TOK], F32, kind="ExternalInput").ap(),
        partB=nc.dram_tensor("partB", [D, TOK], F32, kind="ExternalInput").ap(),
        ssqA=nc.dram_tensor("ssqA", [1, TOK], F32, kind="ExternalInput").ap(),
        ssqB=nc.dram_tensor("ssqB", [1, TOK], F32, kind="ExternalInput").ap(),
        x2f=nc.dram_tensor("x2f", [D, TOK], F32, kind="ExternalInput").ap(),
        wT=nc.dram_tensor("wT", [128 * 128, 1024], F32, kind="ExternalInput").ap(),
        bcol=nc.dram_tensor("bcol", [128, 32], F32, kind="ExternalInput").ap(),
        wrT=nc.dram_tensor("wrT", [128, 128], F32, kind="ExternalInput").ap(),
        aT=nc.dram_tensor("aT", [128, 1024], F32, kind="ExternalInput").ap(),
        bmT=nc.dram_tensor("bmT", [32, 4096], F32, kind="ExternalInput").ap(),
        repmat=nc.dram_tensor("repmat", [4, 32], F32, kind="ExternalInput").ap(),
        ones41=nc.dram_tensor("ones41", [4, 1], F32, kind="ExternalInput").ap(),
        ones132=nc.dram_tensor("ones132", [1, 32], F32, kind="ExternalInput").ap(),
        ones_col=nc.dram_tensor("ones_col", [128, 1], BF16, kind="ExternalInput").ap(),
        o_out=nc.dram_tensor("o_out", [D, TOK], F32, kind="ExternalOutput").ap(),
        drow_d=nc.dram_tensor("drow_d", [1, 1024], F32, kind="Internal").ap(),
        lnrow_d=nc.dram_tensor("lnrow_d", [2, 1024], F32, kind="Internal").ap(),
        x3pre_d=nc.dram_tensor("x3pre_d", [D, TOK], F32, kind="Internal").ap(),
        x3_d=nc.dram_tensor("x3_d", [D, TOK], F32, kind="Internal").ap(),
        x4pre_d=nc.dram_tensor("x4pre_d", [D, TOK], F32, kind="Internal").ap(),
    )
    k = KB(nc, es)
    k.init_echo(nc, es)
    blk = es.enter_context(nc.Block())
    emit_D(k, nc, es, io)
    k.fence_all()
    k.build(blk)
    es.close()
    return nc


def host_prep_D(inputs):
    W = inputs["L1_ffn_W"]; b = inputs["L1_ffn_b"]
    A = inputs["L1_ffn_A"]; Bm = inputs["L1_ffn_Bm"]; Wr = inputs["L1_ffn_Wr"]
    wT = np.ascontiguousarray(W[PERM_D][:, PERM_D].T)
    wT = pack_tiles([wT[ic * 128:(ic + 1) * 128, oc * 128:(oc + 1) * 128]
                     for oc in range(32) for ic in range(32)], 128, 8)
    bcol = np.ascontiguousarray(b[PERM_D].reshape(32, 128).T)
    wrT = Wr[:, PERM_D].T
    wrT = np.ascontiguousarray(wrT.reshape(32, 128, 4).transpose(1, 0, 2).reshape(128, 128))
    aT = A.reshape(32, 4096)[:, PERM_D].T
    aT = np.ascontiguousarray(aT.reshape(32, 128, 32).transpose(1, 0, 2).reshape(128, 1024))
    bmT = np.ascontiguousarray(Bm.transpose(0, 2, 1).reshape(32, 4096)[:, PERM_D])
    repmat = np.zeros((4, 32), dtype=np.float32)
    for e_ in range(4):
        repmat[e_, e_ * 8:(e_ + 1) * 8] = 1.0
    return dict(wT=wT, bcol=bcol, wrT=wrT, aT=aT, bmT=bmT, repmat=repmat,
                ones41=np.ones((4, 1), dtype=np.float32),
                ones132=np.ones((1, 32), dtype=np.float32),
                ones_col=np.ones((128, 1), dtype=BF))


CC_GROUPS = [[0, 1], [2, 3], [4, 5], [6, 7]]


def build_full():
    nc = bass.Bass("TRN2", target_bir_lowering=False)
    es = contextlib.ExitStack()

    def ein(name, shape, dt=F32):
        return nc.dram_tensor(name, shape, dt, kind="ExternalInput").ap()

    def intl(name, shape, dt=F32):
        return nc.dram_tensor(name, shape, dt, kind="Internal").ap()

    x0_fm = ein("x0_fm", [D, TOK])
    wqT = ein("wqT", [2 * 128 * 128, 1024]); wkT = ein("wkT", [D, 1024]); wvT = ein("wvT", [D, 1024])
    bq = ein("bq", [D, 1]); bk = ein("bk", [1, 1024]); bv = ein("bv", [1, 1024])
    bo = ein("bo", [D, 1])
    ones_bf = ein("ones_bf", [1, 128], BF16)
    ones_col = ein("ones_col", [128, 1], BF16)
    qw_gu8 = ein("qw_gu8", [86 * 128, 4096], U8)
    sc_gu = ein("sc_gu", [32, N_GU])
    qz_gu8 = ein("qz_gu8", [32, N_IN], U8)
    qw_dn8 = ein("qw_dn8", [4 * 86 * 128, 512], U8)
    sc_dn = ein("sc_dn", [86, D])
    qz_dn8 = ein("qz_dn8", [86, 2048], U8)
    ind2 = ein("ind2", [2, 128], BF16)
    ind32T = ein("ind32T", [32, 16 * 128], BF16)
    wipT = ein("wipT", [296 * 128, 1024])
    convw = ein("convw", [128, 160]); convb = ein("convb", [128, 40])
    dtb = ein("dtb", [64, 1]); alog = ein("alog", [64, 1])
    dexp = ein("dexp", [128, 32]); nw = ein("nw", [128, 32])
    woutT = ein("woutT", [128 * 128, 1024])
    oh64 = ein("oh64", [64, 64 * 128], BF16)
    ohpair = ein("ohpair", [64, 32 * 128], BF16)
    ones64 = ein("ones64", [64, 128], BF16)
    mask0 = ein("mask0", [128, 256], BF16); mask1 = ein("mask1", [128, 128], BF16)
    idn = ein("idn", [128, 128], BF16)
    wT = ein("wT", [128 * 128, 1024]); bcol = ein("bcol", [128, 32])
    wrT = ein("wrT", [128, 128]); aT = ein("aT", [128, 1024])
    bmT = ein("bmT", [32, 4096]); repmat = ein("repmat", [4, 32])
    ones41 = ein("ones41", [4, 1]); ones132 = ein("ones132", [1, 32])
    o_out = nc.dram_tensor("o_out", [D, TOK], F32, kind="ExternalOutput").ap()
    kv_part = intl("kv_part", [128, 1032]); kv_red = intl("kv_red", [128, 1032])
    x1_d = intl("x1_d", [D, TOK]); x1pre_d = intl("x1pre_d", [D, TOK])
    lnrowA = intl("lnrowA", [2, 1024])
    h_d = intl("h_d", [N_IN, TOK], BF16)
    sdn_d = intl("sdn_d", [N_IN, D], BF16)
    zdnx_d = intl("zdnx_d", [N_IN, D], BF16)
    zdn_d = intl("zdn_d", [86, D], BF16)
    x2pre_d = intl("x2pre_d", [D, TOK]); lnrowB = intl("lnrowB", [2, 1024])
    x2own_d = intl("x2own_d", [D, TOK])
    x2g_d = intl("x2g_d", [2 * D, TOK])
    zx_d = intl("zx_d", [9344, 2048], BF16)
    ynw_d = intl("ynw_d", [4096, 2048], BF16)
    part2_d = intl("part2_d", [2 * D, TOK])
    ssq2_d = intl("ssq2_d", [2, TOK])
    partr_d = intl("partr_d", [D, TOK])
    ssqr_d = intl("ssqr_d", [1, TOK])
    drow_d = intl("drow_d", [1, 1024]); lnrowD = intl("lnrowD", [2, 1024])
    x3pre_d = intl("x3pre_d", [D, TOK]); x3_d = intl("x3_d", [D, TOK])
    x4pre_d = intl("x4pre_d", [D, TOK])

    k = KB(nc, es)
    k.init_echo(nc, es)
    blk = es.enter_context(nc.Block())

    k.set_scope("A1")
    esA1 = contextlib.ExitStack()
    emit_A1(k, nc, esA1, dict(x0_fm=x0_fm, wkT=wkT, wvT=wvT, bk=bk, bv=bv,
                              ones_bf=ones_bf, ones_col=ones_col, o_kv=kv_part))
    k.fence_all()
    k.reclaim_slots()
    esA1.close()
    k.set_scope("cc_kv")
    k.op("gpsimd", lambda e: e.collective_compute(
        "AllReduce", ALU.add, CC_GROUPS, ins=[kv_part], outs=[kv_red]), inc=1)
    k.fence_all()
    k.reclaim_slots()

    k.set_scope("A2")
    esA2 = contextlib.ExitStack()
    emit_A2(k, nc, esA2, dict(x0_fm=x0_fm, wqT=wqT, bq=bq, bo=bo,
                              kvA=kv_red, ones_bf=ones_bf, ones_col=ones_col,
                              o_x1=x1_d, x1pre_d=x1pre_d, lnrow_d=lnrowA))
    k.fence_all()
    k.reclaim_slots()
    esA2.close()

    k.set_scope("Bff")
    esB = contextlib.ExitStack()
    emit_B(k, nc, esB, dict(x1=x1_d, qw_gu8=qw_gu8, sc_gu=sc_gu, qz_gu8=qz_gu8,
                            qw_dn8=qw_dn8, sc_dn=sc_dn, qz_dn8=qz_dn8, ind2=ind2,
                            ind32T=ind32T, ones_col=ones_col, o_x2=x2own_d,
                            h_d=h_d, sdn_d=sdn_d, zdnx_d=zdnx_d, zdn_d=zdn_d,
                            x2pre_d=x2pre_d, lnrow_d=lnrowB))
    k.fence_all()
    k.reclaim_slots()
    esB.close()
    k.set_scope("cc_x2")
    k.op("gpsimd", lambda e: e.collective_compute(
        "AllGather", ALU.bypass, CC_GROUPS, ins=[x2own_d], outs=[x2g_d]), inc=1)
    k.fence_all()
    k.reclaim_slots()

    k.set_scope("Cmm")
    esC = contextlib.ExitStack()
    emit_C(k, nc, esC, dict(x2a=x2g_d[0:D, :], x2b=x2g_d[D:2 * D, :], wipT=wipT,
                            convw=convw, convb=convb, dtb=dtb, alog=alog,
                            dexp=dexp, nw=nw, woutT=woutT, oh64=oh64,
                            ohpair=ohpair, ones64=ones64, mask0=mask0,
                            mask1=mask1, idn=idn, ones_col=ones_col,
                            zx_d=zx_d, ynw_d=ynw_d, o_part=part2_d,
                            o_ssq=ssq2_d, _part2=True))
    k.fence_all()
    k.reclaim_slots()
    esC.close()
    k.set_scope("cc_part")
    k.op("gpsimd", lambda e: e.collective_compute(
        "ReduceScatter", ALU.add, CC_GROUPS, ins=[part2_d], outs=[partr_d]), inc=1)
    k.op("gpsimd", lambda e: e.collective_compute(
        "ReduceScatter", ALU.add, CC_GROUPS, ins=[ssq2_d], outs=[ssqr_d]), inc=1)
    k.fence_all()
    k.reclaim_slots()

    k.set_scope("Dff")
    esD = contextlib.ExitStack()
    emit_D(k, nc, esD, dict(partA=partr_d, ssqA=ssqr_d, x2f=x2own_d, wT=wT,
                            bcol=bcol, wrT=wrT, aT=aT, bmT=bmT, repmat=repmat,
                            ones41=ones41, ones132=ones132, ones_col=ones_col,
                            drow_d=drow_d, lnrow_d=lnrowD, x3pre_d=x3pre_d,
                            x3_d=x3_d, x4pre_d=x4pre_d, o_out=o_out))
    k.fence_all()
    k.reclaim_slots()
    esD.close()
    k.build(blk)
    es.close()
    return nc


def kernel_fused(**inputs):
    shared, x0_fms = host_prep_A(inputs)
    shb = host_prep_B(inputs)
    per_eta, shc = host_prep_C(inputs)
    shd = host_prep_D(inputs)
    nc = build_full()
    maps = []
    for c in range(NCORE):
        beta, eta = c // 2, c % 2
        m = dict(x0_fm=x0_fms[c], wqT=shared["wqT"], wkT=shared["wkT"],
                 wvT=shared["wvT"], bq=shared["bq"],
                 bk=shared["bk"], bv=shared["bv"], bo=shared["bo"],
                 ones_bf=shared["ones_bf"], ones_col=shared["ones_col"])
        m.update(shb); m.pop("ones_col", None); m["ones_col"] = shared["ones_col"]
        m.update(per_eta[eta])
        m.update({kk: vv for kk, vv in shc.items() if kk != "ones_col"})
        m.update({kk: vv for kk, vv in shd.items() if kk != "ones_col"})
        maps.append(m)
    res = run_bass_kernel_spmd(nc, maps, core_ids=list(range(NCORE)))
    out = np.empty((B, T, D), dtype=np.float32)
    for c in range(NCORE):
        beta, eta = c // 2, c % 2
        o = np.asarray(res.results[c]["o_out"], dtype=np.float32)
        out[beta, eta * TOK:(eta + 1) * TOK, :] = o[INV_PERM_D].T
    return out
